# version 02
# baseline (speedup 1.0000x reference)
import numpy as np
import ml_dtypes
import concourse.bass as bass
import concourse.mybir as mybir
from concourse.bass_utils import run_bass_kernel_spmd

F32 = mybir.dt.float32
BF16 = mybir.dt.bfloat16
AF = mybir.ActivationFunctionType
ALU = mybir.AluOpType
AX = mybir.AxisListType

NCORE = 8
TOK = 2048
NT = 16
D = 2048
KD = 16
MOE_D = 1408
NFC = 11
NE = 16
TOPK = 6
S = 1024
HKV = 4
NH = 16
FD = 768
NFD = 6
EPS_A = 1e-8
EPS_LN = 1e-5
BIG = 1.0e30


def _prep(i):
    f32 = np.float32

    def c(a):
        return np.ascontiguousarray(a, dtype=f32)

    wg = np.asarray(i["L0_ffn_w_gate"], f32)
    wu = np.asarray(i["L0_ffn_w_up"], f32)
    wd = np.asarray(i["L0_ffn_w_down"], f32)
    gw = np.asarray(i["L0_ffn_gate_w"], f32)
    wg_t = c(wg.reshape(NE, NFC, 128, KD, 128).transpose(0, 1, 4, 3, 2)
             .reshape(NE, NFC, 128, D))
    wu_t = c(wu.reshape(NE, NFC, 128, KD, 128).transpose(0, 1, 4, 3, 2)
             .reshape(NE, NFC, 128, D))
    wd_t = c(wd.reshape(NE, D, NFC, 128).transpose(0, 2, 3, 1))
    gw_t = c(gw.reshape(16, KD, 128).transpose(2, 1, 0).reshape(128, KD * 16))

    wk = np.asarray(i["L0_attn_Wk"], f32)
    wv = np.asarray(i["L0_attn_Wv"], f32)
    wmb = np.asarray(i["L0_attn_Wmb"], f32)
    wq = np.asarray(i["L0_attn_Wq"], f32)
    wgt = np.asarray(i["L0_attn_Wg"], f32)
    wk_t, wv_t = c(wk.T), c(wv.T)
    wmb_t, wq_t, wgt_t = c(wmb.T), c(wq.T), c(wgt.T)

    bk = np.asarray(i["L0_attn_bk"], f32)
    bv = np.asarray(i["L0_attn_bv"], f32)
    bkv = np.concatenate(
        [np.stack([bk.reshape(4, 128)[g], bv.reshape(4, 128)[g]]).reshape(256)
         for g in range(4)]).reshape(1, 1024)
    bkv = c(bkv)
    bmb = c(np.asarray(i["L0_attn_bmb"], f32).reshape(1, D))
    bq = c(np.asarray(i["L0_attn_bq"], f32).reshape(1, D))
    bg = c(np.asarray(i["L0_attn_bg"], f32).reshape(1, D))

    mem = np.asarray(i["L0_attn_memory"], f32)
    nrm = np.asarray(i["L0_attn_norm"], f32)
    mem4 = c(mem.reshape(4, 4, NFD, 128, 128).transpose(0, 2, 3, 1, 4)
             .reshape(4, NFD, 128, 512))
    norm4 = c(nrm.reshape(4, 4, NFD, 128).transpose(0, 2, 3, 1))
    maug = c(np.concatenate(
        [mem.reshape(NH, NFD, 128, 128), nrm.reshape(NH, NFD, 128, 1)], axis=3))

    x = np.asarray(i["L0_attn_hidden_states"], f32).reshape(NCORE * TOK, D)
    shared = dict(wg_t=wg_t, wu_t=wu_t, wd_t=wd_t, gw_t=gw_t,
                  wk_t=wk_t, wv_t=wv_t, wmb_t=wmb_t, wq_t=wq_t, wgt_t=wgt_t,
                  bkv=bkv, bmb=bmb, bq=bq, bg=bg,
                  mem4=mem4, norm4=norm4, maug=maug)
    maps = []
    for cix in range(NCORE):
        m = dict(shared)
        m["x_in"] = np.ascontiguousarray(x[cix * TOK:(cix + 1) * TOK])
        maps.append(m)
    return maps


class Ctx:

    def __init__(self, nc, stack):
        self.nc = nc
        self.stack = stack
        self.sem = {}
        self.cnt = {}

    def s(self, name):
        if name not in self.sem:
            self.sem[name] = self.stack.enter_context(self.nc.semaphore(name))
            self.cnt[name] = 0
        return self.sem[name]

    def inc(self, ins, name, k=1):
        ins.then_inc(self.s(name), k)
        self.cnt[name] += k
        return self.cnt[name]

    def snap(self, *names):
        return {n: self.cnt.get(n, 0) for n in names}

    def vmark(self):
        return self.cnt.get("vo", 0)


_ORDERED_OPS = ("activation", "tensor_tensor", "tensor_scalar",
                "scalar_tensor_tensor", "tensor_reduce", "tensor_copy",
                "reciprocal", "memset", "tensor_add", "tensor_sub",
                "tensor_mul", "select")


class _OrderedEng:

    def __init__(self, eng, C, nm):
        self._e = eng
        self._C = C
        self._nm = nm

    def __getattr__(self, a):
        f = getattr(self._e, a)
        if a not in _ORDERED_OPS:
            return f

        def wrap(*args, **kw):
            n = self._C.cnt.get(self._nm, 0)
            if n > 0:
                self._e.wait_ge(self._C.s(self._nm), n)
            ins = f(*args, **kw)
            ins.then_inc(self._C.s(self._nm), 1)
            self._C.cnt[self._nm] = n + 1
            return ins
        return wrap


def build_program(phases=("ffn1", "attnA", "aug", "attnB", "ffn2"),
                  sim_safe=False):
    import contextlib
    nc = bass.Bass("TRN2", target_bir_lowering=False, debug=False,
                   detect_race_conditions=not sim_safe)

    x_in = nc.dram_tensor("x_in", [TOK, D], F32, kind="ExternalInput").ap()
    y_out = nc.dram_tensor("y_out", [TOK, D], F32, kind="ExternalOutput").ap()
    wg_t = nc.dram_tensor("wg_t", [NE, NFC, 128, D], F32, kind="ExternalInput").ap()
    wu_t = nc.dram_tensor("wu_t", [NE, NFC, 128, D], F32, kind="ExternalInput").ap()
    wd_t = nc.dram_tensor("wd_t", [NE, NFC, 128, D], F32, kind="ExternalInput").ap()
    gw_t = nc.dram_tensor("gw_t", [128, KD * 16], F32, kind="ExternalInput").ap()
    wk_t = nc.dram_tensor("wk_t", [D, 512], F32, kind="ExternalInput").ap()
    wv_t = nc.dram_tensor("wv_t", [D, 512], F32, kind="ExternalInput").ap()
    wmb_t = nc.dram_tensor("wmb_t", [D, D], F32, kind="ExternalInput").ap()
    wq_t = nc.dram_tensor("wq_t", [D, D], F32, kind="ExternalInput").ap()
    wgt_t = nc.dram_tensor("wgt_t", [D, D], F32, kind="ExternalInput").ap()
    bkv = nc.dram_tensor("bkv", [1, 1024], F32, kind="ExternalInput").ap()
    bmb = nc.dram_tensor("bmb", [1, D], F32, kind="ExternalInput").ap()
    bq = nc.dram_tensor("bq", [1, D], F32, kind="ExternalInput").ap()
    bg = nc.dram_tensor("bg", [1, D], F32, kind="ExternalInput").ap()
    mem4 = nc.dram_tensor("mem4", [4, NFD, 128, 512], F32, kind="ExternalInput").ap()
    norm4 = nc.dram_tensor("norm4", [4, NFD, 128, 4], F32, kind="ExternalInput").ap()
    maug = nc.dram_tensor("maug", [NH, NFD, 128, 129], F32, kind="ExternalInput").ap()

    dev = "dev_x" in phases
    x_dram = nc.dram_tensor("x_dram", [TOK, D], F32,
                            kind="ExternalOutput" if dev else "Internal").ap()
    h2_dram = nc.dram_tensor(
        "h2_dram", [KD, 128, TOK], BF16,
        kind="ExternalOutput" if "dev_h2" in phases else "Internal").ap()
    cc_src = [nc.dram_tensor("cc_src%d" % g, [128, 3096], F32,
                             kind="Internal").ap() for g in range(4)]
    cc_dst = [nc.dram_tensor("cc_dst%d" % g, [128, 3096], F32,
                             addr_space="Shared", kind="Internal").ap()
              for g in range(4)]
    aug_dram = nc.dram_tensor("aug_dram", [NH, 128, NFD * 129], BF16,
                              kind="Internal").ap()

    ident_h = nc.inline_tensor(np.eye(128, dtype=ml_dtypes.bfloat16), name="identc")
    identf_h = nc.inline_tensor(np.eye(16, dtype=np.float32), name="identfc")
    ones_h = nc.inline_tensor(np.ones((1, 128), dtype=ml_dtypes.bfloat16), name="onesc")

    st = contextlib.ExitStack()
    with st:
        C = Ctx(nc, st)
        sb = lambda n, sh, dt: st.enter_context(nc.sbuf_tensor(n, sh, dt))
        ps = lambda n, sh, dt: st.enter_context(nc.psum_tensor(n, sh, dt))

        Hf = sb("Hf", [128, KD * S], BF16)
        ident = sb("ident", [128, 128], BF16)
        identf = sb("identf", [16, 16], F32)
        ones = sb("ones", [1, 128], BF16)
        bkv_s = sb("bkv_s", [1, 1024], BF16)
        bmb_s = sb("bmb_s", [1, D], BF16)
        bq_s = sb("bq_s", [1, D], BF16)
        bg_s = sb("bg_s", [1, D], BF16)
        gw_s = sb("gw_s", [128, KD * 16], BF16)
        ln2_mr = sb("ln2_mr", [128, 2 * NT], F32)
        cols = sb("cols", [128, 64], F32)
        epsA = sb("epsA", [128, 1], F32)

        Xh = sb("Xh", [128, 8 * D], F32)
        xts = sb("xts", [128, D], BF16)
        gsilu = sb("gsilu", [128, S], F32)
        inter = sb("inter", [128, NFC * S], BF16)
        wgb = sb("wgb", [128, 2 * D], BF16)
        wub = sb("wub", [128, 2 * D], BF16)
        wdb = sb("wdb", [128, NFC * D], BF16)
        cw = sb("cw", [128, 8 * 16], F32)
        sc_sb = sb("sc_sb", [128, 8 * 16], F32)
        esc = sb("esc", [128, 8 * 16], F32)
        wk_sc = sb("wk_sc", [128, 8 * 16], F32)
        escsn = sb("escsn", [128, 8 * 16], F32)
        scsn = sb("scsn", [128, 8 * 16], F32)
        colsn = sb("colsn", [128, 32], F32)

        tpp = ps("tpp", [128, 2048], BF16)
        P0 = ps("P0", [128, 1024], F32)
        P1 = ps("P1", [128, 1024], F32)
        P2 = ps("P2", [128, 1024], F32)

        with nc.Block() as bk_:
            BARS = [0]

            def barrier():
                BARS[0] += 5
                tgt = BARS[0]
                for en in ("tensor", "vector", "scalar", "sync", "gpsimd"):
                    def f(eng, en=en, tgt=tgt):
                        if en == "vector" and C.cnt.get("vo", 0) > 0:
                            eng.wait_ge(C.s("vo"), C.cnt["vo"])
                        eng.sem_inc(C.s("BAR"), 1)
                        eng.wait_ge(C.s("BAR"), tgt)
                    getattr(bk_, en)(f)

            def p0_sync(eng):
                eng.dma_start(out=ident[:], in_=ident_h.ap()).then_inc(C.s("cld"), 16)
                eng.dma_start(out=identf[:], in_=identf_h.ap()).then_inc(C.s("cld"), 16)
                eng.dma_start(out=ones[:], in_=ones_h.ap()).then_inc(C.s("cld"), 16)
                C.cnt["cld"] = C.cnt.get("cld", 0) + 48
                eng.wait_ge(C.s("cld"), C.cnt["cld"])

            def p0_gp(eng):
                for dst, src in ((bkv_s, bkv), (bmb_s, bmb), (bq_s, bq),
                                 (bg_s, bg), (gw_s, gw_t)):
                    eng.dma_start(out=dst[:], in_=src).then_inc(C.s("cldg"), 16)
                    C.cnt["cldg"] = C.cnt.get("cldg", 0) + 16
                eng.wait_ge(C.s("cldg"), C.cnt["cldg"])

            def p0_vec(ueng):
                eng = _OrderedEng(ueng, C, "vo")
                eng.memset(epsA[:], EPS_A)

            bk_.sync(p0_sync)
            bk_.gpsimd(p0_gp)
            bk_.vector(p0_vec)
            barrier()

            def emit_ffn_sweep(sw, x_src, x_dst, ln2_out, half):
                t0g = sw * 8
                nE = globals().get("NE_RUN", NE)
                sweep_ne = globals().get("NE_RUN_SW", None)
                if sweep_ne is not None:
                    nE = sweep_ne[globals().setdefault("_SW_IDX", 0) % len(sweep_ne)]
                    globals()["_SW_IDX"] = globals()["_SW_IDX"] + 1

                def prev_slot(e, fc):
                    if fc >= 2:
                        return e * NFC + fc - 2
                    if e == 0:
                        return None
                    return (e - 1) * NFC + (NFC - 1 if fc == 0 else NFC - 2)

                B = C.snap("xld", "sq", "st2", "tp", "hE", "rt1", "scE", "smE",
                           "g", "sil", "u", "wg0", "wg1", "wu0", "wu1", "wdld",
                           "d", "xst", "h2E", "h2d", "sq2", "st22", "tp2")
                M = {}

                def f_gp(eng):
                    for e in range(nE):
                        for fc in range(NFC):
                            sg = "wg0" if fc % 2 == 0 else "wg1"
                            su = "wu0" if fc % 2 == 0 else "wu1"
                            pv = prev_slot(e, fc)
                            if pv is not None:
                                eng.wait_ge(C.s("g"), B["g"] + pv + 1)
                            eng.dma_start(out=wgb[:, (fc % 2) * D:(fc % 2 + 1) * D],
                                          in_=wg_t[e, fc]).then_inc(C.s(sg), 16)
                            C.cnt[sg] += 16
                            if pv is not None:
                                eng.wait_ge(C.s("u"), B["u"] + pv + 1)
                            eng.dma_start(out=wub[:, (fc % 2) * D:(fc % 2 + 1) * D],
                                          in_=wu_t[e, fc]).then_inc(C.s(su), 16)
                            C.cnt[su] += 16
                        if e >= 1:
                            eng.wait_ge(C.s("d"), B["d"] + e * 16)
                        for fc in range(NFC):
                            eng.dma_start(out=wdb[:, fc * D:(fc + 1) * D],
                                          in_=wd_t[e, fc]).then_inc(C.s("wdld"), 16)
                            C.cnt["wdld"] += 16

                bk_.gpsimd(f_gp)

                def f_vec(ueng):
                    eng = _OrderedEng(ueng, C, "vo")
                    ueng.wait_ge(C.s("xld"), B["xld"] + 128)
                    for t in range(8):
                        eng.tensor_reduce(cols[:, 56 + t:57 + t],
                                          Xh[:, t * D:(t + 1) * D], AX.X, ALU.add)
                    ueng.wait_ge(C.s("sq"), B["sq"] + 8)
                    eng.tensor_scalar(cols[:, 32:40], cols[:, 56:64],
                                      1.0 / D, None, ALU.mult)
                    eng.tensor_scalar(cols[:, 0:8], cols[:, 16:24],
                                      1.0 / D, None, ALU.mult)
                    eng.tensor_tensor(cols[:, 48:56], cols[:, 32:40],
                                      cols[:, 32:40], ALU.mult)
                    eng.tensor_tensor(cols[:, 0:8], cols[:, 0:8],
                                      cols[:, 48:56], ALU.subtract)
                    eng.tensor_scalar(cols[:, 0:8], cols[:, 0:8],
                                      EPS_LN, None, ALU.add)
                    M["st1"] = C.vmark()
                    ueng.wait_ge(C.s("st2"), B["st2"] + 1)
                    eng.reciprocal(cols[:, 40:48], cols[:, 8:16])
                    for t in range(8):
                        if t >= 1:
                            ueng.wait_ge(C.s("tp"), B["tp"] + t * 4)
                        eng.tensor_scalar(xts[:], Xh[:, t * D:(t + 1) * D],
                                          cols[:, 32 + t:33 + t],
                                          cols[:, 40 + t:41 + t],
                                          ALU.subtract, ALU.mult)
                        M["ap%d" % t] = C.vmark()
                    for t in range(8):
                        ueng.wait_ge(C.s("scE"), B["scE"] + t + 1)
                        eng.tensor_reduce(cols[:, 24 + t:25 + t],
                                          sc_sb[:, t * 16:(t + 1) * 16],
                                          AX.X, ALU.max)
                        eng.tensor_scalar(cols[:, 24 + t:25 + t],
                                          cols[:, 24 + t:25 + t], -1.0, None,
                                          ALU.mult)
                        M["sm1%d" % t] = C.vmark()
                    for t in range(8):
                        sl = slice(t * 16, (t + 1) * 16)
                        ueng.wait_ge(C.s("smE"), B["smE"] + t + 1)
                        eng.tensor_reduce(cols[:, 24 + t:25 + t], esc[:, sl],
                                          AX.X, ALU.add)
                        eng.reciprocal(cols[:, 48 + t:49 + t],
                                       cols[:, 24 + t:25 + t])
                        eng.tensor_scalar(sc_sb[:, sl], esc[:, sl],
                                          cols[:, 48 + t:49 + t], None, ALU.mult)
                        eng.tensor_copy(wk_sc[:, sl], sc_sb[:, sl])
                        for r2 in range(TOPK):
                            eng.tensor_reduce(cols[:, 24 + t:25 + t],
                                              wk_sc[:, sl], AX.X, ALU.max)
                            if r2 < TOPK - 1:
                                eng.tensor_scalar(esc[:, sl], wk_sc[:, sl],
                                                  cols[:, 24 + t:25 + t], None,
                                                  ALU.is_ge)
                                eng.scalar_tensor_tensor(wk_sc[:, sl], esc[:, sl],
                                                         -BIG, wk_sc[:, sl],
                                                         ALU.mult, ALU.add)
                        eng.tensor_scalar(esc[:, sl], sc_sb[:, sl],
                                          cols[:, 24 + t:25 + t], None, ALU.is_ge)
                        eng.scalar_tensor_tensor(cw[:, sl], sc_sb[:, sl], half,
                                                 esc[:, sl], ALU.mult, ALU.mult)
                    M["cwd"] = C.vmark()
                    for e in range(nE):
                        for fc in range(NFC):
                            idx = e * NFC + fc
                            ueng.wait_ge(C.s("u"), B["u"] + idx + 1)
                            ueng.wait_ge(C.s("sil"), B["sil"] + idx + 1)
                            if fc == 0 and e >= 1:
                                ueng.wait_ge(C.s("d"), B["d"] + e * 16)
                            eng.tensor_tensor(inter[:, fc * S:(fc + 1) * S],
                                              gsilu[:], P2[:], ALU.mult)
                            M["itr%d" % idx] = C.vmark()
                        for t in range(8):
                            for dh in range(2):
                                gidx = t * 2 + dh
                                ueng.wait_ge(C.s("d"), B["d"] + e * 16 + gidx + 1)
                                pr = (P0, P1, P2)[gidx % 3]
                                eng.scalar_tensor_tensor(
                                    Xh[:, t * D + dh * 1024:t * D + (dh + 1) * 1024],
                                    pr[:], cw[:, t * 16 + e:t * 16 + e + 1],
                                    Xh[:, t * D + dh * 1024:t * D + (dh + 1) * 1024],
                                    ALU.mult, ALU.add)
                                M["xa%d_%d" % (e, gidx)] = C.vmark()
                    M["xa_all"] = C.vmark()
                    if ln2_out:
                        for t in range(8):
                            eng.tensor_reduce(cols[:, 56 + t:57 + t],
                                              Xh[:, t * D:(t + 1) * D], AX.X,
                                              ALU.add)
                        ueng.wait_ge(C.s("sq2"), B["sq2"] + 8)
                        eng.tensor_scalar(cols[:, 32:40], cols[:, 56:64],
                                          1.0 / D, None, ALU.mult)
                        eng.tensor_scalar(cols[:, 0:8], cols[:, 16:24],
                                          1.0 / D, None, ALU.mult)
                        eng.tensor_tensor(cols[:, 48:56], cols[:, 32:40],
                                          cols[:, 32:40], ALU.mult)
                        eng.tensor_tensor(cols[:, 0:8], cols[:, 0:8],
                                          cols[:, 48:56], ALU.subtract)
                        eng.tensor_scalar(cols[:, 0:8], cols[:, 0:8],
                                          EPS_LN, None, ALU.add)
                        M["st12"] = C.vmark()
                        ueng.wait_ge(C.s("st22"), B["st22"] + 1)
                        eng.reciprocal(cols[:, 40:48], cols[:, 8:16])
                        for t in range(8):
                            g2 = t0g + t
                            eng.tensor_copy(ln2_mr[:, 2 * g2:2 * g2 + 1],
                                            cols[:, 32 + t:33 + t])
                            eng.tensor_copy(ln2_mr[:, 2 * g2 + 1:2 * g2 + 2],
                                            cols[:, 40 + t:41 + t])
                            if t >= 1:
                                ueng.wait_ge(C.s("tp2"), B["tp2"] + t * 4)
                            eng.tensor_scalar(xts[:], Xh[:, t * D:(t + 1) * D],
                                              cols[:, 32 + t:33 + t],
                                              cols[:, 40 + t:41 + t],
                                              ALU.subtract, ALU.mult)
                            M["ap2%d" % t] = C.vmark()

                bk_.vector(f_vec)

                def f_scal(eng):
                    eng.wait_ge(C.s("xld"), B["xld"] + 128)
                    for t in range(8):
                        eng.activation(inter[:, 0:D], Xh[:, t * D:(t + 1) * D],
                                       AF.Square,
                                       accum_out=cols[:, 16 + t:17 + t]
                                       ).then_inc(C.s("sq"), 1)
                    C.cnt["sq"] += 8
                    eng.wait_ge(C.s("vo"), M["st1"])
                    eng.activation(cols[:, 8:16], cols[:, 0:8], AF.Sqrt
                                   ).then_inc(C.s("st2"), 1)
                    C.cnt["st2"] += 1
                    for t in range(8):
                        for gb in range(4):
                            eng.wait_ge(C.s("tp"), B["tp"] + t * 4 + gb + 1)
                            dst = Hf.ap().rearrange("p (k j) -> p k j", k=KD)[
                                :, gb * 4:(gb + 1) * 4, t * 128:(t + 1) * 128]
                            src2 = tpp[:, ((t * 4 + gb) % 2) * 1024:
                                       ((t * 4 + gb) % 2) * 1024 + 512].rearrange(
                                "p (k j) -> p k j", k=4)
                            eng.activation(dst, src2, AF.Copy).then_inc(C.s("hE"), 1)
                    C.cnt["hE"] += 32
                    for t in range(8):
                        eng.wait_ge(C.s("rt1"), B["rt1"] + t + 1)
                        ps_r = P0 if t % 2 == 0 else P1
                        eng.activation(sc_sb[:, t * 16:(t + 1) * 16],
                                       ps_r[:, 0:16], AF.Copy
                                       ).then_inc(C.s("scE"), 1)
                    C.cnt["scE"] += 8
                    for t in range(8):
                        eng.wait_ge(C.s("vo"), M["sm1%d" % t])
                        eng.activation(esc[:, t * 16:(t + 1) * 16],
                                       sc_sb[:, t * 16:(t + 1) * 16], AF.Exp,
                                       bias=cols[:, 24 + t:25 + t]
                                       ).then_inc(C.s("smE"), 1)
                    C.cnt["smE"] += 8
                    for e in range(nE):
                        for fc in range(NFC):
                            idx = e * NFC + fc
                            eng.wait_ge(C.s("g"), B["g"] + idx + 1)
                            if idx >= 1:
                                eng.wait_ge(C.s("vo"), M["itr%d" % (idx - 1)])
                            gp = P0 if fc % 2 == 0 else P1
                            eng.activation(gsilu[:], gp[:], AF.Silu
                                           ).then_inc(C.s("sil"), 1)
                            C.cnt["sil"] += 1
                    if ln2_out:
                        eng.wait_ge(C.s("vo"), M["xa_all"])
                        for t in range(8):
                            eng.activation(inter[:, 0:D], Xh[:, t * D:(t + 1) * D],
                                           AF.Square,
                                           accum_out=cols[:, 16 + t:17 + t]
                                           ).then_inc(C.s("sq2"), 1)
                        C.cnt["sq2"] += 8
                        eng.wait_ge(C.s("vo"), M["st12"])
                        eng.activation(cols[:, 8:16], cols[:, 0:8], AF.Sqrt
                                       ).then_inc(C.s("st22"), 1)
                        C.cnt["st22"] += 1
                        for t in range(8):
                            for gb in range(4):
                                eng.wait_ge(C.s("tp2"), B["tp2"] + t * 4 + gb + 1)
                                dst = Hf.ap().rearrange("p (k j) -> p k j", k=KD)[
                                    :, gb * 4:(gb + 1) * 4, t * 128:(t + 1) * 128]
                                src2 = tpp[:, ((t * 4 + gb) % 2) * 1024:
                                           ((t * 4 + gb) % 2) * 1024 + 512].rearrange(
                                    "p (k j) -> p k j", k=4)
                                eng.activation(dst, src2, AF.Copy
                                               ).then_inc(C.s("h2E"), 1)
                        C.cnt["h2E"] += 32

                bk_.scalar(f_scal)

                def f_tens(eng):
                    for t in range(8):
                        eng.wait_ge(C.s("vo"), M["ap%d" % t])
                        for gb in range(4):
                            bi = t * 4 + gb
                            if bi >= 2:
                                eng.wait_ge(C.s("hE"), B["hE"] + bi - 1)
                            for j in range(4):
                                k = gb * 4 + j
                                ins = eng.transpose(
                                    tpp[:, (bi % 2) * 1024 + j * 128:
                                        (bi % 2) * 1024 + (j + 1) * 128],
                                    xts[:, k * 128:(k + 1) * 128],
                                    ident[:])
                            ins.then_inc(C.s("tp"), 1)
                    C.cnt["tp"] += 32
                    eng.wait_ge(C.s("hE"), B["hE"] + 32)
                    for t in range(8):
                        ps_r = P0 if t % 2 == 0 else P1
                        if t >= 2:
                            eng.wait_ge(C.s("scE"), B["scE"] + t - 1)
                        for k in range(KD):
                            ins = eng.matmul(
                                ps_r[:, 0:16],
                                Hf[:, k * S + t * 128:k * S + (t + 1) * 128],
                                gw_s[:, k * 16:(k + 1) * 16],
                                start=(k == 0), stop=(k == KD - 1))
                        ins.then_inc(C.s("rt1"), 1)
                        C.cnt["rt1"] += 1
                    for e in range(nE):
                        if e >= 1:
                            eng.wait_ge(C.s("vo"), M["xa%d_15" % (e - 1)])
                        for fc in range(NFC):
                            idx = e * NFC + fc
                            gp = P0 if fc % 2 == 0 else P1
                            if e == 0 and fc <= 1:
                                eng.wait_ge(C.s("scE"), B["scE"] + 8)
                            pv = prev_slot(e, fc)
                            if pv is not None:
                                eng.wait_ge(C.s("sil"), B["sil"] + pv + 1)
                            sg = "wg0" if fc % 2 == 0 else "wg1"
                            ng = e * 6 + fc // 2 + 1 if fc % 2 == 0 \
                                else e * 5 + fc // 2 + 1
                            eng.wait_ge(C.s(sg), B[sg] + 16 * ng)
                            for k in range(KD):
                                for nh2 in range(2):
                                    ins = eng.matmul(
                                        gp[:, nh2 * 512:(nh2 + 1) * 512],
                                        wgb[:, (fc % 2) * D + k * 128:
                                            (fc % 2) * D + (k + 1) * 128],
                                        Hf[:, k * S + nh2 * 512:k * S + (nh2 + 1) * 512],
                                        start=(k == 0), stop=(k == KD - 1))
                            ins.then_inc(C.s("g"), 1)
                            C.cnt["g"] += 1
                            if idx >= 1:
                                eng.wait_ge(C.s("vo"), M["itr%d" % (idx - 1)])
                            su = "wu0" if fc % 2 == 0 else "wu1"
                            eng.wait_ge(C.s(su), B[su] + 16 * ng)
                            for k in range(KD):
                                for nh2 in range(2):
                                    ins = eng.matmul(
                                        P2[:, nh2 * 512:(nh2 + 1) * 512],
                                        wub[:, (fc % 2) * D + k * 128:
                                            (fc % 2) * D + (k + 1) * 128],
                                        Hf[:, k * S + nh2 * 512:k * S + (nh2 + 1) * 512],
                                        start=(k == 0), stop=(k == KD - 1))
                            ins.then_inc(C.s("u"), 1)
                            C.cnt["u"] += 1
                        eng.wait_ge(C.s("sil"), B["sil"] + e * NFC + NFC)
                        eng.wait_ge(C.s("vo"), M["itr%d" % (e * NFC + NFC - 1)])
                        eng.wait_ge(C.s("wdld"), B["wdld"] + 16 * (e + 1) * NFC)
                        for t in range(8):
                            for dh in range(2):
                                gidx = t * 2 + dh
                                pr = (P0, P1, P2)[gidx % 3]
                                if gidx >= 3:
                                    eng.wait_ge(C.s("vo"),
                                                M["xa%d_%d" % (e, gidx - 3)])
                                for fc in range(NFC):
                                    for nh2 in range(2):
                                        ins = eng.matmul(
                                            pr[:, nh2 * 512:(nh2 + 1) * 512],
                                            inter[:, fc * S + t * 128:
                                                  fc * S + (t + 1) * 128],
                                            wdb[:, fc * D + dh * 1024 + nh2 * 512:
                                                fc * D + dh * 1024 + (nh2 + 1) * 512],
                                            start=(fc == 0), stop=(fc == NFC - 1))
                                ins.then_inc(C.s("d"), 1)
                                C.cnt["d"] += 1
                    if ln2_out:
                        for t in range(8):
                            eng.wait_ge(C.s("vo"), M["ap2%d" % t])
                            for gb in range(4):
                                bi = t * 4 + gb
                                if bi >= 2:
                                    eng.wait_ge(C.s("h2E"), B["h2E"] + bi - 1)
                                for j in range(4):
                                    k = gb * 4 + j
                                    ins = eng.transpose(
                                        tpp[:, (bi % 2) * 1024 + j * 128:
                                            (bi % 2) * 1024 + (j + 1) * 128],
                                        xts[:, k * 128:(k + 1) * 128],
                                        ident[:])
                                ins.then_inc(C.s("tp2"), 1)
                        C.cnt["tp2"] += 32

                bk_.tensor(f_tens)

                def f_sync(eng):
                    for t in range(8):
                        eng.dma_start(
                            out=Xh[:, t * D:(t + 1) * D],
                            in_=x_src[(t0g + t) * 128:(t0g + t + 1) * 128, :],
                        ).then_inc(C.s("xld"), 16)
                    C.cnt["xld"] += 128
                    eng.wait_ge(C.s("vo"), M["xa_all"])
                    for t in range(8):
                        eng.dma_start(
                            out=x_dst[(t0g + t) * 128:(t0g + t + 1) * 128, :],
                            in_=Xh[:, t * D:(t + 1) * D],
                        ).then_inc(C.s("xst"), 16)
                    C.cnt["xst"] += 128
                    if ln2_out:
                        eng.wait_ge(C.s("h2E"), B["h2E"] + 32)
                        eng.dma_start(
                            out=h2_dram[:, :, t0g * 128:t0g * 128 + S]
                                .rearrange("k p j -> p k j"),
                            in_=Hf.ap().rearrange("p (k j) -> p k j", k=KD),
                        ).then_inc(C.s("h2d"), 16)
                        C.cnt["h2d"] += 16
                        eng.wait_ge(C.s("h2d"), C.cnt["h2d"])
                    eng.wait_ge(C.s("xst"), C.cnt["xst"])

                bk_.sync(f_sync)

            DCO = (0, 129, 258, 512, 641, 770)
            AWL_KG = [0] * 4

            def emit_attnA_unit(kg, blk):
                B = C.snap("hld", "awl", "kv1", "mb1", "nd1", "tpA", "dl1",
                           "kvE", "sqA", "sqt", "mbE", "dnE", "tpE", "dout",
                           "ccs")
                M = {}
                blkoff = blk * 512

                def f_gp(eng):
                    if blk == 0:
                        wkv = wdb[:, 0:4096].rearrange("p (k c) -> p k c", k=KD)
                        eng.dma_start(
                            out=wkv[:, :, 0:128],
                            in_=wk_t[:, kg * 128:(kg + 1) * 128]
                                .rearrange("(k p) j -> p k j", p=128),
                        ).then_inc(C.s("awl"), 16)
                        eng.dma_start(
                            out=wkv[:, :, 128:256],
                            in_=wv_t[:, kg * 128:(kg + 1) * 128]
                                .rearrange("(k p) j -> p k j", p=128),
                        ).then_inc(C.s("awl"), 16)
                        eng.dma_start(
                            out=wdb[:, 4096:12288].rearrange("p (k c) -> p k c",
                                                             k=KD),
                            in_=wmb_t[:, kg * 512:(kg + 1) * 512]
                                .rearrange("(k p) j -> p k j", p=128),
                        ).then_inc(C.s("awl"), 16)
                        eng.dma_start(
                            out=wdb[:, 12288:15360].rearrange("p (c j) -> p c j",
                                                              c=NFD),
                            in_=mem4[kg].rearrange("c p j -> p c j"),
                        ).then_inc(C.s("awl"), 16)
                        eng.dma_start(
                            out=wdb[:, 15360:15384].rearrange("p (c j) -> p c j",
                                                              c=NFD),
                            in_=norm4[kg].rearrange("c p j -> p c j"),
                        ).then_inc(C.s("awl"), 16)
                        C.cnt["awl"] += 80
                        AWL_KG[kg] = C.cnt["awl"]
                    if blk == 3:
                        eng.wait_ge(C.s("vo"), M_DL[0])
                        eng.dma_start(out=cc_src[kg], in_=Xh[:, 8192:11288]
                                      ).then_inc(C.s("dout"), 16)
                        C.cnt["dout"] += 16
                        eng.wait_ge(C.s("dout"), C.cnt["dout"])
                        if True:
                            eng.collective_compute(
                                "AllReduce", ALU.add,
                                replica_groups=[list(range(NCORE))],
                                ins=[cc_src[kg].opt()], outs=[cc_dst[kg].opt()],
                            ).then_inc(C.s("ccs"), 1)
                        C.cnt["ccs"] += 1

                M_DL = [0]

                def f_vec(ueng):
                    eng = _OrderedEng(ueng, C, "vo")
                    for t in range(4):
                        par = (t % 2)
                        xc = 1024 + par * 256
                        mks = 4864 + par * 768
                        ueng.wait_ge(C.s("kvE"), B["kvE"] + t + 1)
                        for c in range(NFD):
                            jr = c // 2 + 1
                            hh2 = c % 2
                            base = xc + hh2 * 128
                            eng.tensor_tensor(
                                Xh[:, mks + c * 128 + jr:mks + (c + 1) * 128],
                                Xh[:, base + jr:base + 128],
                                Xh[:, base:base + 128 - jr], ALU.mult)
                            wsrc = xc + (256 if hh2 == 0 else 128) - jr
                            eng.tensor_tensor(
                                Xh[:, mks + c * 128:mks + c * 128 + jr],
                                Xh[:, base:base + jr],
                                Xh[:, wsrc:wsrc + jr], ALU.mult)
                        M["dp%d" % t] = C.vmark()
                        ueng.wait_ge(C.s("sqA"), B["sqA"] + t + 1)
                        eng.tensor_copy(cols[:, 0 + t:1 + t], cols[:, 16 + t:17 + t])
                        M["bn%d" % t] = C.vmark()
                        ueng.wait_ge(C.s("sqt"), B["sqt"] + t + 1)
                        eng.tensor_scalar(cols[:, 24 + t:25 + t],
                                          cols[:, 8 + t:9 + t], 1e-12, None,
                                          ALU.max)
                        eng.reciprocal(cols[:, 32 + t:33 + t],
                                       cols[:, 24 + t:25 + t])
                        eng.tensor_scalar(inter[:, t * 768:(t + 1) * 768],
                                          Xh[:, mks:mks + 768],
                                          cols[:, 32 + t:33 + t], None, ALU.mult)
                        eng.tensor_tensor(cols[:, 40 + t:41 + t],
                                          cols[:, 8 + t:9 + t],
                                          cols[:, 32 + t:33 + t], ALU.mult)
                        eng.tensor_tensor(cols[:, 48 + t:49 + t],
                                          cols[:, 40 + t:41 + t],
                                          cols[:, 40 + t:41 + t], ALU.mult)
                        eng.reciprocal(cols[:, 56 + t:57 + t],
                                       cols[:, 48 + t:49 + t])
                        M["nrm%d" % t] = C.vmark()
                        ueng.wait_ge(C.s("dnE"), B["dnE"] + t + 1)
                        dq = 1536 + t * 4
                        rq = 1552 + t * 4
                        cq = 1568 + t * 4
                        eng.reciprocal(Xh[:, rq:rq + 4], Xh[:, dq:dq + 4])
                        eng.tensor_scalar(Xh[:, cq:cq + 4], Xh[:, dq:dq + 4],
                                          cols[:, 56 + t:57 + t], None, ALU.mult)
                        eng.tensor_scalar(Xh[:, cq:cq + 4], Xh[:, cq:cq + 4],
                                          -1.0, 1.0, ALU.mult, ALU.add)
                        eng.tensor_scalar(Xh[:, cq:cq + 4], Xh[:, cq:cq + 4],
                                          0.0, 1.0, ALU.max, ALU.min)
                        ueng.wait_ge(C.s("nd1"), B["nd1"] + t + 1)
                        ueng.wait_ge(C.s("mbE"), B["mbE"] + t + 1)
                        for hh in range(4):
                            eng.tensor_scalar(Xh[:, 6400:6528],
                                              P0[:, 512 + hh * 128:
                                                 512 + (hh + 1) * 128],
                                              Xh[:, rq + hh:rq + hh + 1], None,
                                              ALU.mult)
                            eng.tensor_tensor(Xh[:, 6528:6656],
                                              Xh[:, t * 128:(t + 1) * 128],
                                              Xh[:, 6400:6528], ALU.subtract)
                            ws = 6144 + (t * 4 + hh) * 129
                            eng.tensor_tensor(
                                inter[:, ws:ws + 128], Xh[:, 6528:6656],
                                Xh[:, 2048 + t * 512 + hh * 128:
                                   2048 + t * 512 + (hh + 1) * 128], ALU.mult)
                            eng.tensor_copy(inter[:, ws + 128:ws + 129],
                                            Xh[:, cq + hh:cq + hh + 1])
                        M["pv%d" % t] = C.vmark()
                    M["wmv"] = C.vmark()
                    for hh in range(4):
                        ueng.wait_ge(C.s("dl1"), B["dl1"] + hh + 1)
                        ds = 8192 + hh * 774
                        if blk == 0:
                            eng.tensor_copy(Xh[:, ds:ds + 387], P2[:, 0:387])
                            eng.tensor_copy(Xh[:, ds + 387:ds + 774],
                                            P2[:, 512:899])
                        else:
                            eng.tensor_tensor(Xh[:, ds:ds + 387],
                                              Xh[:, ds:ds + 387],
                                              P2[:, 0:387], ALU.add)
                            eng.tensor_tensor(Xh[:, ds + 387:ds + 774],
                                              Xh[:, ds + 387:ds + 774],
                                              P2[:, 512:899], ALU.add)
                        M["dlE%d" % hh] = C.vmark()
                    M_DL[0] = C.vmark()

                bk_.vector(f_vec)

                def f_scal(eng):
                    for t in range(4):
                        par = (t % 2)
                        xc = 1024 + par * 256
                        mks = 4864 + par * 768
                        eng.wait_ge(C.s("kv1"), B["kv1"] + t + 1)
                        eng.activation(Xh[:, xc:xc + 128], P0[:, 0:128], AF.Relu)
                        eng.activation(Xh[:, xc + 128:xc + 256], P0[:, 0:128],
                                       AF.Relu, scale=-1.0)
                        eng.activation(Xh[:, t * 128:(t + 1) * 128],
                                       P0[:, 128:256], AF.Copy
                                       ).then_inc(C.s("kvE"), 1)
                        C.cnt["kvE"] += 1
                        eng.wait_ge(C.s("vo"), M["dp%d" % t])
                        eng.activation(gsilu[:, 0:768], Xh[:, mks:mks + 768],
                                       AF.Square,
                                       accum_out=cols[:, 16 + t:17 + t]
                                       ).then_inc(C.s("sqA"), 1)
                        C.cnt["sqA"] += 1
                        eng.wait_ge(C.s("vo"), M["bn%d" % t])
                        eng.activation(cols[:, 8 + t:9 + t], cols[:, 0 + t:1 + t],
                                       AF.Sqrt).then_inc(C.s("sqt"), 1)
                        C.cnt["sqt"] += 1
                        eng.wait_ge(C.s("mb1"), B["mb1"] + t + 1)
                        eng.activation(Xh[:, 2048 + t * 512:2048 + (t + 1) * 512],
                                       P0[:, 0:512], AF.Sigmoid
                                       ).then_inc(C.s("mbE"), 1)
                        C.cnt["mbE"] += 1
                        eng.wait_ge(C.s("tpA"), B["tpA"] + t + 1)
                        dstT = inter[:, 3072:6144].rearrange(
                            "p (c j) -> p c j", c=NFD)[:, :, t * 128:(t + 1) * 128]
                        srcT = tpp[:, par * 1024:par * 1024 + 768].rearrange(
                            "p (c j) -> p c j", c=NFD)
                        eng.activation(dstT, srcT, AF.Copy).then_inc(C.s("tpE"), 1)
                        C.cnt["tpE"] += 1
                        eng.wait_ge(C.s("nd1"), B["nd1"] + t + 1)
                        eng.activation(Xh[:, 1536 + t * 4:1536 + (t + 1) * 4],
                                       P1[:, 0:4], AF.Identity, bias=epsA[:]
                                       ).then_inc(C.s("dnE"), 1)
                        C.cnt["dnE"] += 1

                bk_.scalar(f_scal)

                def f_tens(eng):
                    eng.wait_ge(C.s("hld"), B["hld"] + 16)
                    eng.wait_ge(C.s("awl"), AWL_KG[kg])
                    for t in range(4):
                        par = (t % 2)
                        if t >= 1:
                            eng.wait_ge(C.s("mbE"), B["mbE"] + t)
                        for k in range(KD):
                            eng.matmul(P0[:, 0:256],
                                       Hf[:, k * S + t * 128:k * S + (t + 1) * 128],
                                       wdb[:, k * 256:(k + 1) * 256],
                                       start=(k == 0), stop=False)
                        ins = eng.matmul(P0[:, 0:256], ones[0:1, 0:128],
                                         bkv_s[0:1, kg * 256:(kg + 1) * 256],
                                         start=False, stop=True)
                        ins.then_inc(C.s("kv1"), 1)
                        C.cnt["kv1"] += 1
                        eng.wait_ge(C.s("kvE"), B["kvE"] + t + 1)
                        for k in range(KD):
                            eng.matmul(P0[:, 0:512],
                                       Hf[:, k * S + t * 128:k * S + (t + 1) * 128],
                                       wdb[:, 4096 + k * 512:4096 + (k + 1) * 512],
                                       start=(k == 0), stop=False)
                        ins = eng.matmul(P0[:, 0:512], ones[0:1, 0:128],
                                         bmb_s[0:1, kg * 512:(kg + 1) * 512],
                                         start=False, stop=True)
                        ins.then_inc(C.s("mb1"), 1)
                        C.cnt["mb1"] += 1
                        eng.wait_ge(C.s("vo"), M["nrm%d" % t])
                        if t >= 2:
                            eng.wait_ge(C.s("tpE"), B["tpE"] + t - 1)
                        for c in range(NFD):
                            ins = eng.transpose(
                                tpp[:, par * 1024 + c * 128:
                                    par * 1024 + (c + 1) * 128],
                                inter[:, t * 768 + c * 128:t * 768 + (c + 1) * 128],
                                ident[:])
                        ins.then_inc(C.s("tpA"), 1)
                        C.cnt["tpA"] += 1
                        eng.wait_ge(C.s("tpE"), B["tpE"] + t + 1)
                        if t >= 1:
                            eng.wait_ge(C.s("dnE"), B["dnE"] + t)
                            eng.wait_ge(C.s("vo"), M["pv%d" % (t - 1)])
                        for c in range(NFD):
                            lh = inter[:, 3072 + c * 512 + t * 128:
                                       3072 + c * 512 + (t + 1) * 128]
                            eng.matmul(P0[:, 512:1024], lh,
                                       wdb[:, 12288 + c * 512:12288 + (c + 1) * 512],
                                       start=(c == 0), stop=(c == NFD - 1))
                            ins = eng.matmul(P1[:, 0:4], lh,
                                             wdb[:, 15360 + c * 4:15360 + (c + 1) * 4],
                                             start=(c == 0), stop=(c == NFD - 1))
                        ins.then_inc(C.s("nd1"), 1)
                        C.cnt["nd1"] += 1
                    eng.wait_ge(C.s("vo"), M["wmv"])
                    for hh in range(4):
                        if hh >= 1:
                            eng.wait_ge(C.s("vo"), M["dlE%d" % (hh - 1)])
                        elif blk >= 1:
                            eng.wait_ge(C.s("vo"), B_DL_PREV[0])
                        for c in range(NFD):
                            for t in range(4):
                                ws = 6144 + (t * 4 + hh) * 129
                                ins = eng.matmul(
                                    P2[:, DCO[c]:DCO[c] + 129],
                                    inter[:, t * 768 + c * 128:
                                          t * 768 + (c + 1) * 128],
                                    inter[:, ws:ws + 129],
                                    start=(t == 0), stop=(t == 3))
                        ins.then_inc(C.s("dl1"), 1)
                        C.cnt["dl1"] += 1

                bk_.tensor(f_tens)

                def f_sync(eng):
                    eng.dma_start(
                        out=Hf.ap().rearrange("p (k j) -> p k j", k=KD)[:, :, 0:512],
                        in_=h2_dram[:, :, blkoff:blkoff + 512]
                            .rearrange("k p j -> p k j"),
                    ).then_inc(C.s("hld"), 16)
                    C.cnt["hld"] += 16

                bk_.sync(f_sync)
                bk_.gpsimd(f_gp)
                B_DL_PREV[0] = M_DL[0]

            B_DL_PREV = [0]
            AUG_LAST = [0]

            def emit_aug_kg(kg):
                B = C.snap("agl", "ags", "ccs")
                M = {}

                def f_gp(eng):
                    eng.wait_ge(C.s("ccs"), kg + 1)
                    eng.dma_start(
                        out=Xh[:, 0:3096].rearrange("p (h c j) -> p h c j",
                                                    h=4, c=NFD),
                        in_=maug[4 * kg:4 * kg + 4]
                            .rearrange("h c p j -> p h c j"),
                    ).then_inc(C.s("agl"), 16)
                    eng.dma_start(out=Xh[:, 4096:7192], in_=cc_dst[kg]
                                  ).then_inc(C.s("agl"), 16)
                    C.cnt["agl"] += 32
                    eng.wait_ge(C.s("vo"), M["aug"])
                    eng.dma_start(
                        out=aug_dram[4 * kg:4 * kg + 4]
                            .rearrange("h p j -> p h j"),
                        in_=wdb[:, 16384:19480].rearrange("p (h j) -> p h j", h=4),
                    ).then_inc(C.s("ags"), 16)
                    C.cnt["ags"] += 16
                    eng.wait_ge(C.s("ags"), C.cnt["ags"])

                def f_vec(ueng):
                    eng = _OrderedEng(ueng, C, "vo")
                    ueng.wait_ge(C.s("agl"), B["agl"] + 32)
                    eng.tensor_tensor(wdb[:, 16384:19480], Xh[:, 0:3096],
                                      Xh[:, 4096:7192], ALU.add)
                    M["aug"] = C.vmark()
                    AUG_LAST[0] = M["aug"]

                bk_.vector(f_vec)
                bk_.gpsimd(f_gp)

            BWL_KG = [0]

            def emit_attnB_unit(blk, kg):
                B = C.snap("hldB", "xldB", "bwl", "q1", "g1", "tpAB", "na1",
                           "qE", "gE", "tpEB", "sqB", "sqtB", "naE", "xstB")
                M = {}
                blkoff = blk * 512

                def f_gp(eng):
                    eng.dma_start(
                        out=wdb[:, 0:8192].rearrange("p (k c) -> p k c", k=KD),
                        in_=wq_t[:, kg * 512:(kg + 1) * 512]
                            .rearrange("(k p) j -> p k j", p=128),
                    ).then_inc(C.s("bwl"), 16)
                    eng.dma_start(
                        out=wdb[:, 8192:16384].rearrange("p (k c) -> p k c", k=KD),
                        in_=wgt_t[:, kg * 512:(kg + 1) * 512]
                            .rearrange("(k p) j -> p k j", p=128),
                    ).then_inc(C.s("bwl"), 16)
                    eng.wait_ge(C.s("ags"), 16 * (kg + 1))
                    eng.dma_start(
                        out=inter[:, 3840:6936].rearrange("p (h j) -> p h j", h=4),
                        in_=aug_dram[4 * kg:4 * kg + 4]
                            .rearrange("h p j -> p h j"),
                    ).then_inc(C.s("bwl"), 16)
                    C.cnt["bwl"] += 48

                def f_vec(ueng):
                    eng = _OrderedEng(ueng, C, "vo")
                    if kg == 0:
                        ueng.wait_ge(C.s("xldB"), B["xldB"] + 64)
                        for t in range(4):
                            g2 = blk * 4 + t
                            eng.tensor_scalar(cols[:, 0 + t:1 + t],
                                              ln2_mr[:, 2 * g2 + 1:2 * g2 + 2],
                                              1.0, None, ALU.add)
                            eng.tensor_tensor(cols[:, 8 + t:9 + t],
                                              ln2_mr[:, 2 * g2:2 * g2 + 1],
                                              ln2_mr[:, 2 * g2 + 1:2 * g2 + 2],
                                              ALU.mult)
                            eng.tensor_scalar(Xh[:, t * D:(t + 1) * D],
                                              Xh[:, t * D:(t + 1) * D],
                                              cols[:, 0 + t:1 + t],
                                              cols[:, 8 + t:9 + t],
                                              ALU.mult, ALU.subtract)
                        M["lnB"] = C.vmark()
                    for t in range(4):
                        qb = 8192 + t * 512
                        ueng.wait_ge(C.s("qE"), B["qE"] + t + 1)
                        if t >= 1:
                            ueng.wait_ge(C.s("tpAB"), B["tpAB"] + 4 * t)
                        xq = Xh[:, 12288:13312].rearrange("p (h c) -> p h c", h=4)
                        q4v = Xh[:, qb:qb + 512].rearrange("p (h c) -> p h c", h=4)
                        mq4v = Xh[:, 13312:16384].rearrange("p (h c) -> p h c", h=4)
                        eng.tensor_scalar(xq[:, :, 0:128], q4v, 0.0, None, ALU.max)
                        eng.tensor_scalar(xq[:, :, 128:256], q4v, -1.0, 0.0,
                                          ALU.mult, ALU.max)
                        for c in range(NFD):
                            jr = c // 2 + 1
                            hh2 = c % 2
                            bb = hh2 * 128
                            eng.tensor_tensor(
                                mq4v[:, :, c * 128 + jr:(c + 1) * 128],
                                xq[:, :, bb + jr:bb + 128],
                                xq[:, :, bb:bb + 128 - jr], ALU.mult)
                            wsrc = (256 if hh2 == 0 else 128) - jr
                            eng.tensor_tensor(
                                mq4v[:, :, c * 128:c * 128 + jr],
                                xq[:, :, bb:bb + jr],
                                xq[:, :, wsrc:wsrc + jr], ALU.mult)
                        M["mq%d" % t] = C.vmark()
                        ueng.wait_ge(C.s("sqB"), B["sqB"] + 4 * (t + 1))
                        eng.tensor_copy(cols[:, 0:4], cols[:, 16:20])
                        M["bnB%d" % t] = C.vmark()
                        ueng.wait_ge(C.s("sqtB"), B["sqtB"] + t + 1)
                        eng.tensor_scalar(cols[:, 40:44], cols[:, 8:12], 1e-12,
                                          None, ALU.max)
                        eng.reciprocal(cols[:, 48:52], cols[:, 40:44])
                        for hh in range(4):
                            eng.tensor_scalar(
                                inter[:, hh * 768:(hh + 1) * 768],
                                Xh[:, 13312 + hh * 768:13312 + (hh + 1) * 768],
                                cols[:, 48 + hh:49 + hh], None, ALU.mult)
                        M["mqn%d" % t] = C.vmark()
                        ueng.wait_ge(C.s("gE"), B["gE"] + t + 1)
                        for hh in range(4):
                            n = t * 4 + hh
                            ueng.wait_ge(C.s("naE"), B["naE"] + n + 1)
                            eng.reciprocal(cols[:, 32:33], cols[:, 24:25])
                            M["rc%d" % n] = C.vmark()
                            nb = (n % 2) * 512
                            eng.tensor_scalar(sc_sb[:, 0:128], P1[:, nb:nb + 128],
                                              cols[:, 32:33], None, ALU.mult)
                            M["mh%d" % n] = C.vmark()
                            eng.tensor_tensor(
                                esc[:, 0:128],
                                Xh[:, 10240 + t * 512 + hh * 128:
                                   10240 + t * 512 + (hh + 1) * 128],
                                sc_sb[:, 0:128], ALU.mult)
                            xsl = Xh[:, t * D + (4 * kg + hh) * 128:
                                     t * D + (4 * kg + hh + 1) * 128]
                            eng.tensor_tensor(xsl, xsl, esc[:, 0:128], ALU.add)
                    M["updF"] = C.vmark()

                bk_.vector(f_vec)

                def f_scal(eng):
                    for t in range(4):
                        pq = P0 if t % 2 == 0 else P2
                        eng.wait_ge(C.s("q1"), B["q1"] + t + 1)
                        eng.activation(Xh[:, 8192 + t * 512:8192 + (t + 1) * 512],
                                       pq[:, 0:512], AF.Copy
                                       ).then_inc(C.s("qE"), 1)
                        C.cnt["qE"] += 1
                        eng.wait_ge(C.s("g1"), B["g1"] + t + 1)
                        eng.activation(Xh[:, 10240 + t * 512:10240 + (t + 1) * 512],
                                       pq[:, 512:1024], AF.Sigmoid
                                       ).then_inc(C.s("gE"), 1)
                        C.cnt["gE"] += 1
                        eng.wait_ge(C.s("vo"), M["mq%d" % t])
                        for hh in range(4):
                            eng.activation(gsilu[:, 0:768],
                                           Xh[:, 13312 + hh * 768:
                                              13312 + (hh + 1) * 768],
                                           AF.Square,
                                           accum_out=cols[:, 16 + hh:17 + hh]
                                           ).then_inc(C.s("sqB"), 1)
                            C.cnt["sqB"] += 1
                        eng.wait_ge(C.s("vo"), M["bnB%d" % t])
                        eng.activation(cols[:, 8:12], cols[:, 0:4], AF.Sqrt
                                       ).then_inc(C.s("sqtB"), 1)
                        C.cnt["sqtB"] += 1
                        for hh in range(4):
                            n = t * 4 + hh
                            eng.wait_ge(C.s("tpAB"), B["tpAB"] + n + 1)
                            eng.activation(inter[:, 3072:3840],
                                           tpp[:, (n % 2) * 1024:
                                               (n % 2) * 1024 + 768], AF.Copy
                                           ).then_inc(C.s("tpEB"), 1)
                            C.cnt["tpEB"] += 1
                            eng.wait_ge(C.s("na1"), B["na1"] + n + 1)
                            if n >= 1:
                                eng.wait_ge(C.s("vo"), M["rc%d" % (n - 1)])
                            nb = (n % 2) * 512
                            eng.activation(cols[:, 24:25], P1[:, nb + 128:nb + 129],
                                           AF.Identity, bias=epsA[:]
                                           ).then_inc(C.s("naE"), 1)
                            C.cnt["naE"] += 1

                bk_.scalar(f_scal)

                def f_tens(eng):
                    eng.wait_ge(C.s("hldB"), B["hldB"] + (16 if kg == 0 else 0))
                    eng.wait_ge(C.s("bwl"), B["bwl"] + 48)
                    if kg == 0:
                        eng.wait_ge(C.s("vo"), M["lnB"])
                    for t in range(4):
                        pq = P0 if t % 2 == 0 else P2
                        if t >= 2:
                            eng.wait_ge(C.s("qE"), B["qE"] + t - 1)
                        for k in range(KD):
                            eng.matmul(pq[:, 0:512],
                                       Hf[:, k * S + t * 128:k * S + (t + 1) * 128],
                                       wdb[:, k * 512:(k + 1) * 512],
                                       start=(k == 0), stop=False)
                        ins = eng.matmul(pq[:, 0:512], ones[0:1, 0:128],
                                         bq_s[0:1, kg * 512:(kg + 1) * 512],
                                         start=False, stop=True)
                        ins.then_inc(C.s("q1"), 1)
                        C.cnt["q1"] += 1
                        if t >= 2:
                            eng.wait_ge(C.s("gE"), B["gE"] + t - 1)
                        for k in range(KD):
                            eng.matmul(pq[:, 512:1024],
                                       Hf[:, k * S + t * 128:k * S + (t + 1) * 128],
                                       wdb[:, 8192 + k * 512:8192 + (k + 1) * 512],
                                       start=(k == 0), stop=False)
                        ins = eng.matmul(pq[:, 512:1024], ones[0:1, 0:128],
                                         bg_s[0:1, kg * 512:(kg + 1) * 512],
                                         start=False, stop=True)
                        ins.then_inc(C.s("g1"), 1)
                        C.cnt["g1"] += 1
                        eng.wait_ge(C.s("vo"), M["mqn%d" % t])
                        for hh in range(4):
                            n = t * 4 + hh
                            if n >= 2:
                                eng.wait_ge(C.s("tpEB"), B["tpEB"] + n - 1)
                            for c in range(NFD):
                                ins = eng.transpose(
                                    tpp[:, (n % 2) * 1024 + c * 128:
                                        (n % 2) * 1024 + (c + 1) * 128],
                                    inter[:, hh * 768 + c * 128:
                                          hh * 768 + (c + 1) * 128],
                                    ident[:])
                            ins.then_inc(C.s("tpAB"), 1)
                            C.cnt["tpAB"] += 1
                            eng.wait_ge(C.s("tpEB"), B["tpEB"] + n + 1)
                            if n >= 2:
                                eng.wait_ge(C.s("vo"), M["mh%d" % (n - 2)])
                            nb = (n % 2) * 512
                            for c in range(NFD):
                                ins = eng.matmul(
                                    P1[:, nb:nb + 129],
                                    inter[:, 3072 + c * 128:3072 + (c + 1) * 128],
                                    inter[:, 3840 + hh * 774 + c * 129:
                                          3840 + hh * 774 + (c + 1) * 129],
                                    start=(c == 0), stop=(c == NFD - 1))
                            ins.then_inc(C.s("na1"), 1)
                            C.cnt["na1"] += 1

                bk_.tensor(f_tens)

                def f_sync(eng):
                    if kg == 0:
                        if blk == 0:
                            eng.wait_ge(C.s("vo"), AUG_LAST[0])
                        for t in range(4):
                            eng.dma_start(
                                out=Xh[:, t * D:(t + 1) * D],
                                in_=x_dram[(blk * 4 + t) * 128:
                                           (blk * 4 + t + 1) * 128, :],
                            ).then_inc(C.s("xldB"), 16)
                        C.cnt["xldB"] += 64
                        eng.dma_start(
                            out=Hf.ap().rearrange("p (k j) -> p k j", k=KD)[
                                :, :, 0:512],
                            in_=h2_dram[:, :, blkoff:blkoff + 512]
                                .rearrange("k p j -> p k j"),
                        ).then_inc(C.s("hldB"), 16)
                        C.cnt["hldB"] += 16
                    if kg == 3:
                        eng.wait_ge(C.s("vo"), M["updF"])
                        for t in range(4):
                            eng.dma_start(
                                out=x_dram[(blk * 4 + t) * 128:
                                           (blk * 4 + t + 1) * 128, :],
                                in_=Xh[:, t * D:(t + 1) * D],
                            ).then_inc(C.s("xstB"), 16)
                        C.cnt["xstB"] += 64
                        eng.wait_ge(C.s("xstB"), C.cnt["xstB"])

                bk_.sync(f_sync)
                bk_.gpsimd(f_gp)

            ln2_on = not globals().get("LN2_OFF", False)
            if "ffn1" in phases:
                emit_ffn_sweep(0, x_in, x_dram, ln2_on, 0.5)
                barrier()
                emit_ffn_sweep(1, x_in, x_dram, ln2_on, 0.5)
                barrier()
            if "attnA" in phases:
                for kg_ in range(4):
                    for blk_ in range(4):
                        emit_attnA_unit(kg_, blk_)
                        barrier()
            if "aug" in phases:
                for kg_ in range(4):
                    emit_aug_kg(kg_)
            if "attnB" in phases:
                for blk_ in range(4):
                    for kg_ in range(4):
                        emit_attnB_unit(blk_, kg_)
                        barrier()
            if "dev_i" in phases:
                inter_d = nc.dram_tensor("inter_d", [128, NFC * S], BF16,
                                         kind="ExternalOutput").ap()
                cw_d = nc.dram_tensor("cw_d", [128, 128], F32,
                                      kind="ExternalOutput").ap()
                gsilu_d = nc.dram_tensor("gsilu_d", [128, S], F32,
                                         kind="ExternalOutput").ap()
                cols_d = nc.dram_tensor("cols_d", [128, 64], F32,
                                        kind="ExternalOutput").ap()
                hf_d = nc.dram_tensor("hf_d", [128, KD * S], BF16,
                                      kind="ExternalOutput").ap()

                sc_d = nc.dram_tensor("sc_d", [128, 128], F32,
                                      kind="ExternalOutput").ap()
                esc_d = nc.dram_tensor("esc_d", [128, 128], F32,
                                       kind="ExternalOutput").ap()

                gw_d = nc.dram_tensor("gw_d", [128, KD * 16], BF16,
                                      kind="ExternalOutput").ap()

                escsn_d = nc.dram_tensor("escsn_d", [128, 128], F32,
                                         kind="ExternalOutput").ap()
                scsn_d = nc.dram_tensor("scsn_d", [128, 128], F32,
                                        kind="ExternalOutput").ap()
                colsn_d = nc.dram_tensor("colsn_d", [128, 32], F32,
                                         kind="ExternalOutput").ap()

                def f_dump(eng):
                    for dst, srcb in ((inter_d, inter), (cw_d, cw),
                                      (gsilu_d, gsilu), (cols_d, cols),
                                      (hf_d, Hf), (sc_d, sc_sb), (esc_d, esc),
                                      (gw_d, gw_s), (escsn_d, escsn),
                                      (scsn_d, scsn), (colsn_d, colsn)):
                        eng.dma_start(out=dst, in_=srcb[:]).then_inc(C.s("dvd"), 16)
                        C.cnt["dvd"] = C.cnt.get("dvd", 0) + 16
                    eng.wait_ge(C.s("dvd"), C.cnt["dvd"])
                bk_.sync(f_dump)
                barrier()
            if "ffn2" in phases:
                src = x_dram if ("attnB" in phases or "ffn1" in phases) else x_in
                emit_ffn_sweep(0, src, y_out, False, 0.5)
                barrier()
                emit_ffn_sweep(1, src, y_out, False, 0.5)
                barrier()

    return nc


def kernel(**inputs):
    maps = _prep(inputs)
    nc = build_program()
    res = run_bass_kernel_spmd(nc, maps, core_ids=list(range(NCORE)))
    out = np.concatenate([res.results[c]["y_out"] for c in range(NCORE)], axis=0)
    return out.reshape(4, 4096, 2048)


# version 03
# speedup vs baseline: 1.0272x; 1.0272x over previous
import numpy as np
import ml_dtypes
import concourse.bass as bass
import concourse.mybir as mybir
from concourse.bass_utils import run_bass_kernel_spmd

F32 = mybir.dt.float32
BF16 = mybir.dt.bfloat16
AF = mybir.ActivationFunctionType
ALU = mybir.AluOpType
AX = mybir.AxisListType

NCORE = 8
TOK = 2048
NT = 16
D = 2048
KD = 16
MOE_D = 1408
NFC = 11
NE = 16
TOPK = 6
S = 1024
HKV = 4
NH = 16
FD = 768
NFD = 6
EPS_A = 1e-8
EPS_LN = 1e-5
BIG = 1.0e30


def _prep(i):
    f32 = np.float32

    def c(a):
        return np.ascontiguousarray(a, dtype=f32)

    wg = np.asarray(i["L0_ffn_w_gate"], f32)
    wu = np.asarray(i["L0_ffn_w_up"], f32)
    wd = np.asarray(i["L0_ffn_w_down"], f32)
    gw = np.asarray(i["L0_ffn_gate_w"], f32)
    wg_t = c(wg.reshape(NE, NFC, 128, KD, 128).transpose(0, 1, 4, 3, 2)
             .reshape(NE, NFC, 128, D))
    wu_t = c(wu.reshape(NE, NFC, 128, KD, 128).transpose(0, 1, 4, 3, 2)
             .reshape(NE, NFC, 128, D))
    wd_t = c(wd.reshape(NE, D, NFC, 128).transpose(0, 2, 3, 1))
    gw_t = c(gw.reshape(16, KD, 128).transpose(2, 1, 0).reshape(128, KD * 16))

    wk = np.asarray(i["L0_attn_Wk"], f32)
    wv = np.asarray(i["L0_attn_Wv"], f32)
    wmb = np.asarray(i["L0_attn_Wmb"], f32)
    wq = np.asarray(i["L0_attn_Wq"], f32)
    wgt = np.asarray(i["L0_attn_Wg"], f32)
    wk_t, wv_t = c(wk.T), c(wv.T)
    wmb_t, wq_t, wgt_t = c(wmb.T), c(wq.T), c(wgt.T)

    bk = np.asarray(i["L0_attn_bk"], f32)
    bv = np.asarray(i["L0_attn_bv"], f32)
    bkv = np.concatenate(
        [np.stack([bk.reshape(4, 128)[g], bv.reshape(4, 128)[g]]).reshape(256)
         for g in range(4)]).reshape(1, 1024)
    bkv = c(bkv)
    bmb = c(np.asarray(i["L0_attn_bmb"], f32).reshape(1, D))
    bq = c(np.asarray(i["L0_attn_bq"], f32).reshape(1, D))
    bg = c(np.asarray(i["L0_attn_bg"], f32).reshape(1, D))

    mem = np.asarray(i["L0_attn_memory"], f32)
    nrm = np.asarray(i["L0_attn_norm"], f32)
    mem4 = c(mem.reshape(4, 4, NFD, 128, 128).transpose(0, 2, 3, 1, 4)
             .reshape(4, NFD, 128, 512))
    norm4 = c(nrm.reshape(4, 4, NFD, 128).transpose(0, 2, 3, 1))
    maug = c(np.concatenate(
        [mem.reshape(NH, NFD, 128, 128), nrm.reshape(NH, NFD, 128, 1)], axis=3))

    x = np.asarray(i["L0_attn_hidden_states"], f32).reshape(NCORE * TOK, D)
    shared = dict(wg_t=wg_t, wu_t=wu_t, wd_t=wd_t, gw_t=gw_t,
                  wk_t=wk_t, wv_t=wv_t, wmb_t=wmb_t, wq_t=wq_t, wgt_t=wgt_t,
                  bkv=bkv, bmb=bmb, bq=bq, bg=bg,
                  mem4=mem4, norm4=norm4, maug=maug)
    maps = []
    for cix in range(NCORE):
        m = dict(shared)
        m["x_in"] = np.ascontiguousarray(x[cix * TOK:(cix + 1) * TOK])
        maps.append(m)
    return maps


class Ctx:

    def __init__(self, nc, stack):
        self.nc = nc
        self.stack = stack
        self.sem = {}
        self.cnt = {}

    def s(self, name):
        if name not in self.sem:
            self.sem[name] = self.stack.enter_context(self.nc.semaphore(name))
            self.cnt[name] = 0
        return self.sem[name]

    def inc(self, ins, name, k=1):
        ins.then_inc(self.s(name), k)
        self.cnt[name] += k
        return self.cnt[name]

    def snap(self, *names):
        return {n: self.cnt.get(n, 0) for n in names}

    def vmark(self):
        return self.cnt.get("vo", 0)


_ORDERED_OPS = ("activation", "tensor_tensor", "tensor_scalar",
                "scalar_tensor_tensor", "tensor_reduce", "tensor_copy",
                "reciprocal", "memset", "tensor_add", "tensor_sub",
                "tensor_mul", "select")


class _OrderedEng:

    def __init__(self, eng, C, nm):
        self._e = eng
        self._C = C
        self._nm = nm

    def __getattr__(self, a):
        f = getattr(self._e, a)
        if a not in _ORDERED_OPS:
            return f

        def wrap(*args, **kw):
            n = self._C.cnt.get(self._nm, 0)
            if n > 0:
                self._e.wait_ge(self._C.s(self._nm), n)
            ins = f(*args, **kw)
            ins.then_inc(self._C.s(self._nm), 1)
            self._C.cnt[self._nm] = n + 1
            return ins
        return wrap


def build_program(phases=("ffn1", "attnA", "aug", "attnB", "ffn2"),
                  sim_safe=False):
    import contextlib
    nc = bass.Bass("TRN2", target_bir_lowering=False, debug=False,
                   detect_race_conditions=not sim_safe)

    x_in = nc.dram_tensor("x_in", [TOK, D], F32, kind="ExternalInput").ap()
    y_out = nc.dram_tensor("y_out", [TOK, D], F32, kind="ExternalOutput").ap()
    wg_t = nc.dram_tensor("wg_t", [NE, NFC, 128, D], F32, kind="ExternalInput").ap()
    wu_t = nc.dram_tensor("wu_t", [NE, NFC, 128, D], F32, kind="ExternalInput").ap()
    wd_t = nc.dram_tensor("wd_t", [NE, NFC, 128, D], F32, kind="ExternalInput").ap()
    gw_t = nc.dram_tensor("gw_t", [128, KD * 16], F32, kind="ExternalInput").ap()
    wk_t = nc.dram_tensor("wk_t", [D, 512], F32, kind="ExternalInput").ap()
    wv_t = nc.dram_tensor("wv_t", [D, 512], F32, kind="ExternalInput").ap()
    wmb_t = nc.dram_tensor("wmb_t", [D, D], F32, kind="ExternalInput").ap()
    wq_t = nc.dram_tensor("wq_t", [D, D], F32, kind="ExternalInput").ap()
    wgt_t = nc.dram_tensor("wgt_t", [D, D], F32, kind="ExternalInput").ap()
    bkv = nc.dram_tensor("bkv", [1, 1024], F32, kind="ExternalInput").ap()
    bmb = nc.dram_tensor("bmb", [1, D], F32, kind="ExternalInput").ap()
    bq = nc.dram_tensor("bq", [1, D], F32, kind="ExternalInput").ap()
    bg = nc.dram_tensor("bg", [1, D], F32, kind="ExternalInput").ap()
    mem4 = nc.dram_tensor("mem4", [4, NFD, 128, 512], F32, kind="ExternalInput").ap()
    norm4 = nc.dram_tensor("norm4", [4, NFD, 128, 4], F32, kind="ExternalInput").ap()
    maug = nc.dram_tensor("maug", [NH, NFD, 128, 129], F32, kind="ExternalInput").ap()

    dev = "dev_x" in phases
    x_dram = nc.dram_tensor("x_dram", [TOK, D], F32,
                            kind="ExternalOutput" if dev else "Internal").ap()
    h2_dram = nc.dram_tensor(
        "h2_dram", [KD, 128, TOK], BF16,
        kind="ExternalOutput" if "dev_h2" in phases else "Internal").ap()
    cc_src = [nc.dram_tensor("cc_src%d" % g, [128, 3096], F32,
                             kind="Internal").ap() for g in range(4)]
    cc_dst = [nc.dram_tensor("cc_dst%d" % g, [128, 3096], F32,
                             addr_space="Shared", kind="Internal").ap()
              for g in range(4)]
    aug_dram = nc.dram_tensor("aug_dram", [NH, 128, NFD * 129], BF16,
                              kind="Internal").ap()

    ident_h = nc.inline_tensor(np.eye(128, dtype=ml_dtypes.bfloat16), name="identc")
    identf_h = nc.inline_tensor(np.eye(16, dtype=np.float32), name="identfc")
    ones_h = nc.inline_tensor(np.ones((1, 128), dtype=ml_dtypes.bfloat16), name="onesc")

    st = contextlib.ExitStack()
    with st:
        C = Ctx(nc, st)
        sb = lambda n, sh, dt: st.enter_context(nc.sbuf_tensor(n, sh, dt))
        ps = lambda n, sh, dt: st.enter_context(nc.psum_tensor(n, sh, dt))

        Hf = sb("Hf", [128, KD * S], BF16)
        ident = sb("ident", [128, 128], BF16)
        identf = sb("identf", [16, 16], F32)
        ones = sb("ones", [1, 128], BF16)
        bkv_s = sb("bkv_s", [1, 1024], BF16)
        bmb_s = sb("bmb_s", [1, D], BF16)
        bq_s = sb("bq_s", [1, D], BF16)
        bg_s = sb("bg_s", [1, D], BF16)
        gw_s = sb("gw_s", [128, KD * 16], BF16)
        ln2_mr = sb("ln2_mr", [128, 2 * NT], F32)
        cols = sb("cols", [128, 64], F32)
        epsA = sb("epsA", [128, 1], F32)

        Xh = sb("Xh", [128, 8 * D], F32)
        xts = sb("xts", [128, D], BF16)
        gsilu = sb("gsilu", [128, S], F32)
        inter = sb("inter", [128, NFC * S], BF16)
        wgb = sb("wgb", [128, 2 * D], BF16)
        wub = sb("wub", [128, 2 * D], BF16)
        wdb = sb("wdb", [128, NFC * D], BF16)
        cw = sb("cw", [128, 8 * 16], F32)
        sc_sb = sb("sc_sb", [128, 8 * 16], F32)
        esc = sb("esc", [128, 8 * 16], F32)
        wk_sc = sb("wk_sc", [128, 8 * 16], F32)
        escsn = sb("escsn", [128, 8 * 16], F32)
        scsn = sb("scsn", [128, 8 * 16], F32)
        colsn = sb("colsn", [128, 32], F32)

        tpp = ps("tpp", [128, 2048], BF16)
        P0 = ps("P0", [128, 1024], F32)
        P1 = ps("P1", [128, 1024], F32)
        P2 = ps("P2", [128, 1024], F32)

        with nc.Block() as bk_:
            BARS = [0]

            def barrier():
                BARS[0] += 5
                tgt = BARS[0]
                for en in ("tensor", "vector", "scalar", "sync", "gpsimd"):
                    def f(eng, en=en, tgt=tgt):
                        if en == "vector" and C.cnt.get("vo", 0) > 0:
                            eng.wait_ge(C.s("vo"), C.cnt["vo"])
                        eng.sem_inc(C.s("BAR"), 1)
                        eng.wait_ge(C.s("BAR"), tgt)
                    getattr(bk_, en)(f)

            def p0_sync(eng):
                eng.dma_start(out=ident[:], in_=ident_h.ap()).then_inc(C.s("cld"), 16)
                eng.dma_start(out=identf[:], in_=identf_h.ap()).then_inc(C.s("cld"), 16)
                eng.dma_start(out=ones[:], in_=ones_h.ap()).then_inc(C.s("cld"), 16)
                C.cnt["cld"] = C.cnt.get("cld", 0) + 48
                eng.wait_ge(C.s("cld"), C.cnt["cld"])

            def p0_gp(eng):
                for dst, src in ((bkv_s, bkv), (bmb_s, bmb), (bq_s, bq),
                                 (bg_s, bg), (gw_s, gw_t)):
                    eng.dma_start(out=dst[:], in_=src).then_inc(C.s("cldg"), 16)
                    C.cnt["cldg"] = C.cnt.get("cldg", 0) + 16
                eng.wait_ge(C.s("cldg"), C.cnt["cldg"])

            def p0_vec(ueng):
                eng = _OrderedEng(ueng, C, "vo")
                eng.memset(epsA[:], EPS_A)

            bk_.sync(p0_sync)
            bk_.gpsimd(p0_gp)
            bk_.vector(p0_vec)
            barrier()

            def emit_ffn_sweep(sw, x_src, x_dst, ln2_out, half):
                t0g = sw * 8
                nE = globals().get("NE_RUN", NE)
                sweep_ne = globals().get("NE_RUN_SW", None)
                if sweep_ne is not None:
                    nE = sweep_ne[globals().setdefault("_SW_IDX", 0) % len(sweep_ne)]
                    globals()["_SW_IDX"] = globals()["_SW_IDX"] + 1

                def prev_slot(e, fc):
                    if fc >= 2:
                        return e * NFC + fc - 2
                    if e == 0:
                        return None
                    return (e - 1) * NFC + (NFC - 1 if fc == 0 else NFC - 2)

                B = C.snap("xld", "sq", "st2", "tp", "hE", "rt1", "scE", "smE",
                           "g", "sil", "u", "wg0", "wg1", "wu0", "wu1", "wdld",
                           "d", "xst", "h2E", "h2d", "sq2", "st22", "tp2")
                M = {}

                def f_gp(eng):
                    for e in range(nE):
                        for fc in range(NFC):
                            sg = "wg0" if fc % 2 == 0 else "wg1"
                            su = "wu0" if fc % 2 == 0 else "wu1"
                            pv = prev_slot(e, fc)
                            if pv is not None:
                                eng.wait_ge(C.s("g"), B["g"] + pv + 1)
                            eng.dma_start(out=wgb[:, (fc % 2) * D:(fc % 2 + 1) * D],
                                          in_=wg_t[e, fc]).then_inc(C.s(sg), 16)
                            C.cnt[sg] += 16
                            if pv is not None:
                                eng.wait_ge(C.s("u"), B["u"] + pv + 1)
                            eng.dma_start(out=wub[:, (fc % 2) * D:(fc % 2 + 1) * D],
                                          in_=wu_t[e, fc]).then_inc(C.s(su), 16)
                            C.cnt[su] += 16
                        if e >= 1:
                            eng.wait_ge(C.s("d"), B["d"] + e * 16)
                        for fc in range(NFC):
                            eng.dma_start(out=wdb[:, fc * D:(fc + 1) * D],
                                          in_=wd_t[e, fc]).then_inc(C.s("wdld"), 16)
                            C.cnt["wdld"] += 16

                bk_.gpsimd(f_gp)

                def f_vec(ueng):
                    eng = _OrderedEng(ueng, C, "vo")
                    ueng.wait_ge(C.s("xld"), B["xld"] + 128)
                    for t in range(8):
                        eng.tensor_reduce(cols[:, 56 + t:57 + t],
                                          Xh[:, t * D:(t + 1) * D], AX.X, ALU.add)
                    ueng.wait_ge(C.s("sq"), B["sq"] + 8)
                    eng.tensor_scalar(cols[:, 32:40], cols[:, 56:64],
                                      1.0 / D, None, ALU.mult)
                    eng.tensor_scalar(cols[:, 0:8], cols[:, 16:24],
                                      1.0 / D, None, ALU.mult)
                    eng.tensor_tensor(cols[:, 48:56], cols[:, 32:40],
                                      cols[:, 32:40], ALU.mult)
                    eng.tensor_tensor(cols[:, 0:8], cols[:, 0:8],
                                      cols[:, 48:56], ALU.subtract)
                    eng.tensor_scalar(cols[:, 0:8], cols[:, 0:8],
                                      EPS_LN, None, ALU.add)
                    M["st1"] = C.vmark()
                    ueng.wait_ge(C.s("st2"), B["st2"] + 1)
                    eng.reciprocal(cols[:, 40:48], cols[:, 8:16])
                    for t in range(8):
                        if t >= 1:
                            ueng.wait_ge(C.s("tp"), B["tp"] + t * 4)
                        eng.tensor_scalar(xts[:], Xh[:, t * D:(t + 1) * D],
                                          cols[:, 32 + t:33 + t],
                                          cols[:, 40 + t:41 + t],
                                          ALU.subtract, ALU.mult)
                        M["ap%d" % t] = C.vmark()
                    for t in range(8):
                        ueng.wait_ge(C.s("scE"), B["scE"] + t + 1)
                        eng.tensor_reduce(cols[:, 24 + t:25 + t],
                                          sc_sb[:, t * 16:(t + 1) * 16],
                                          AX.X, ALU.max)
                        eng.tensor_scalar(cols[:, 24 + t:25 + t],
                                          cols[:, 24 + t:25 + t], -1.0, None,
                                          ALU.mult)
                        M["sm1%d" % t] = C.vmark()
                    for t in range(8):
                        sl = slice(t * 16, (t + 1) * 16)
                        ueng.wait_ge(C.s("smE"), B["smE"] + t + 1)
                        eng.tensor_reduce(cols[:, 24 + t:25 + t], esc[:, sl],
                                          AX.X, ALU.add)
                        eng.reciprocal(cols[:, 48 + t:49 + t],
                                       cols[:, 24 + t:25 + t])
                        eng.tensor_scalar(sc_sb[:, sl], esc[:, sl],
                                          cols[:, 48 + t:49 + t], None, ALU.mult)
                        eng.tensor_copy(wk_sc[:, sl], sc_sb[:, sl])
                        for r2 in range(TOPK):
                            eng.tensor_reduce(cols[:, 24 + t:25 + t],
                                              wk_sc[:, sl], AX.X, ALU.max)
                            if r2 < TOPK - 1:
                                eng.tensor_scalar(esc[:, sl], wk_sc[:, sl],
                                                  cols[:, 24 + t:25 + t], None,
                                                  ALU.is_ge)
                                eng.scalar_tensor_tensor(wk_sc[:, sl], esc[:, sl],
                                                         -BIG, wk_sc[:, sl],
                                                         ALU.mult, ALU.add)
                        eng.tensor_scalar(esc[:, sl], sc_sb[:, sl],
                                          cols[:, 24 + t:25 + t], None, ALU.is_ge)
                        eng.scalar_tensor_tensor(cw[:, sl], sc_sb[:, sl], half,
                                                 esc[:, sl], ALU.mult, ALU.mult)
                    M["cwd"] = C.vmark()
                    for e in range(nE):
                        for fc in range(NFC):
                            idx = e * NFC + fc
                            ueng.wait_ge(C.s("u"), B["u"] + idx + 1)
                            ueng.wait_ge(C.s("sil"), B["sil"] + idx + 1)
                            if fc == 0 and e >= 1:
                                ueng.wait_ge(C.s("d"), B["d"] + e * 16)
                            eng.tensor_tensor(inter[:, fc * S:(fc + 1) * S],
                                              gsilu[:], P2[:], ALU.mult)
                            M["itr%d" % idx] = C.vmark()
                        for t in range(8):
                            for dh in range(2):
                                gidx = t * 2 + dh
                                ueng.wait_ge(C.s("d"), B["d"] + e * 16 + gidx + 1)
                                pr = (P0, P1, P2)[gidx % 3]
                                eng.scalar_tensor_tensor(
                                    Xh[:, t * D + dh * 1024:t * D + (dh + 1) * 1024],
                                    pr[:], cw[:, t * 16 + e:t * 16 + e + 1],
                                    Xh[:, t * D + dh * 1024:t * D + (dh + 1) * 1024],
                                    ALU.mult, ALU.add)
                                M["xa%d_%d" % (e, gidx)] = C.vmark()
                    M["xa_all"] = C.vmark()
                    if ln2_out:
                        for t in range(8):
                            eng.tensor_reduce(cols[:, 56 + t:57 + t],
                                              Xh[:, t * D:(t + 1) * D], AX.X,
                                              ALU.add)
                        ueng.wait_ge(C.s("sq2"), B["sq2"] + 8)
                        eng.tensor_scalar(cols[:, 32:40], cols[:, 56:64],
                                          1.0 / D, None, ALU.mult)
                        eng.tensor_scalar(cols[:, 0:8], cols[:, 16:24],
                                          1.0 / D, None, ALU.mult)
                        eng.tensor_tensor(cols[:, 48:56], cols[:, 32:40],
                                          cols[:, 32:40], ALU.mult)
                        eng.tensor_tensor(cols[:, 0:8], cols[:, 0:8],
                                          cols[:, 48:56], ALU.subtract)
                        eng.tensor_scalar(cols[:, 0:8], cols[:, 0:8],
                                          EPS_LN, None, ALU.add)
                        M["st12"] = C.vmark()
                        ueng.wait_ge(C.s("st22"), B["st22"] + 1)
                        eng.reciprocal(cols[:, 40:48], cols[:, 8:16])
                        for t in range(8):
                            g2 = t0g + t
                            eng.tensor_copy(ln2_mr[:, 2 * g2:2 * g2 + 1],
                                            cols[:, 32 + t:33 + t])
                            eng.tensor_copy(ln2_mr[:, 2 * g2 + 1:2 * g2 + 2],
                                            cols[:, 40 + t:41 + t])
                            if t >= 1:
                                ueng.wait_ge(C.s("tp2"), B["tp2"] + t * 4)
                            eng.tensor_scalar(xts[:], Xh[:, t * D:(t + 1) * D],
                                              cols[:, 32 + t:33 + t],
                                              cols[:, 40 + t:41 + t],
                                              ALU.subtract, ALU.mult)
                            M["ap2%d" % t] = C.vmark()

                bk_.vector(f_vec)

                def f_scal(eng):
                    eng.wait_ge(C.s("xld"), B["xld"] + 128)
                    for t in range(8):
                        eng.activation(inter[:, 0:D], Xh[:, t * D:(t + 1) * D],
                                       AF.Square,
                                       accum_out=cols[:, 16 + t:17 + t]
                                       ).then_inc(C.s("sq"), 1)
                    C.cnt["sq"] += 8
                    eng.wait_ge(C.s("vo"), M["st1"])
                    eng.activation(cols[:, 8:16], cols[:, 0:8], AF.Sqrt
                                   ).then_inc(C.s("st2"), 1)
                    C.cnt["st2"] += 1
                    for t in range(8):
                        for gb in range(4):
                            eng.wait_ge(C.s("tp"), B["tp"] + t * 4 + gb + 1)
                            dst = Hf.ap().rearrange("p (k j) -> p k j", k=KD)[
                                :, gb * 4:(gb + 1) * 4, t * 128:(t + 1) * 128]
                            src2 = tpp[:, ((t * 4 + gb) % 2) * 1024:
                                       ((t * 4 + gb) % 2) * 1024 + 512].rearrange(
                                "p (k j) -> p k j", k=4)
                            eng.activation(dst, src2, AF.Copy).then_inc(C.s("hE"), 1)
                    C.cnt["hE"] += 32
                    for t in range(8):
                        eng.wait_ge(C.s("rt1"), B["rt1"] + t + 1)
                        ps_r = P0 if t % 2 == 0 else P1
                        eng.activation(sc_sb[:, t * 16:(t + 1) * 16],
                                       ps_r[:, 0:16], AF.Copy
                                       ).then_inc(C.s("scE"), 1)
                    C.cnt["scE"] += 8
                    for t in range(8):
                        eng.wait_ge(C.s("vo"), M["sm1%d" % t])
                        eng.activation(esc[:, t * 16:(t + 1) * 16],
                                       sc_sb[:, t * 16:(t + 1) * 16], AF.Exp,
                                       bias=cols[:, 24 + t:25 + t]
                                       ).then_inc(C.s("smE"), 1)
                    C.cnt["smE"] += 8
                    for e in range(nE):
                        for fc in range(NFC):
                            idx = e * NFC + fc
                            eng.wait_ge(C.s("g"), B["g"] + idx + 1)
                            if idx >= 1:
                                eng.wait_ge(C.s("vo"), M["itr%d" % (idx - 1)])
                            gp = P0 if fc % 2 == 0 else P1
                            eng.activation(gsilu[:], gp[:], AF.Silu
                                           ).then_inc(C.s("sil"), 1)
                            C.cnt["sil"] += 1
                    if ln2_out:
                        eng.wait_ge(C.s("vo"), M["xa_all"])
                        for t in range(8):
                            eng.activation(inter[:, 0:D], Xh[:, t * D:(t + 1) * D],
                                           AF.Square,
                                           accum_out=cols[:, 16 + t:17 + t]
                                           ).then_inc(C.s("sq2"), 1)
                        C.cnt["sq2"] += 8
                        eng.wait_ge(C.s("vo"), M["st12"])
                        eng.activation(cols[:, 8:16], cols[:, 0:8], AF.Sqrt
                                       ).then_inc(C.s("st22"), 1)
                        C.cnt["st22"] += 1
                        for t in range(8):
                            for gb in range(4):
                                eng.wait_ge(C.s("tp2"), B["tp2"] + t * 4 + gb + 1)
                                dst = Hf.ap().rearrange("p (k j) -> p k j", k=KD)[
                                    :, gb * 4:(gb + 1) * 4, t * 128:(t + 1) * 128]
                                src2 = tpp[:, ((t * 4 + gb) % 2) * 1024:
                                           ((t * 4 + gb) % 2) * 1024 + 512].rearrange(
                                    "p (k j) -> p k j", k=4)
                                eng.activation(dst, src2, AF.Copy
                                               ).then_inc(C.s("h2E"), 1)
                        C.cnt["h2E"] += 32

                bk_.scalar(f_scal)

                def f_tens(eng):
                    for t in range(8):
                        eng.wait_ge(C.s("vo"), M["ap%d" % t])
                        for gb in range(4):
                            bi = t * 4 + gb
                            if bi >= 2:
                                eng.wait_ge(C.s("hE"), B["hE"] + bi - 1)
                            for j in range(4):
                                k = gb * 4 + j
                                ins = eng.transpose(
                                    tpp[:, (bi % 2) * 1024 + j * 128:
                                        (bi % 2) * 1024 + (j + 1) * 128],
                                    xts[:, k * 128:(k + 1) * 128],
                                    ident[:])
                            ins.then_inc(C.s("tp"), 1)
                    C.cnt["tp"] += 32
                    eng.wait_ge(C.s("hE"), B["hE"] + 32)
                    for t in range(8):
                        ps_r = P0 if t % 2 == 0 else P1
                        if t >= 2:
                            eng.wait_ge(C.s("scE"), B["scE"] + t - 1)
                        for k in range(KD):
                            ins = eng.matmul(
                                ps_r[:, 0:16],
                                Hf[:, k * S + t * 128:k * S + (t + 1) * 128],
                                gw_s[:, k * 16:(k + 1) * 16],
                                start=(k == 0), stop=(k == KD - 1))
                        ins.then_inc(C.s("rt1"), 1)
                        C.cnt["rt1"] += 1
                    for e in range(nE):
                        if e >= 1:
                            eng.wait_ge(C.s("vo"), M["xa%d_15" % (e - 1)])
                        for fc in range(NFC):
                            idx = e * NFC + fc
                            gp = P0 if fc % 2 == 0 else P1
                            if e == 0 and fc <= 1:
                                eng.wait_ge(C.s("scE"), B["scE"] + 8)
                            pv = prev_slot(e, fc)
                            if pv is not None:
                                eng.wait_ge(C.s("sil"), B["sil"] + pv + 1)
                            sg = "wg0" if fc % 2 == 0 else "wg1"
                            ng = e * 6 + fc // 2 + 1 if fc % 2 == 0 \
                                else e * 5 + fc // 2 + 1
                            eng.wait_ge(C.s(sg), B[sg] + 16 * ng)
                            for nh2 in range(2):
                                for k in range(KD):
                                    ins = eng.matmul(
                                        gp[:, nh2 * 512:(nh2 + 1) * 512],
                                        wgb[:, (fc % 2) * D + k * 128:
                                            (fc % 2) * D + (k + 1) * 128],
                                        Hf[:, k * S + nh2 * 512:k * S + (nh2 + 1) * 512],
                                        start=(k == 0), stop=(k == KD - 1))
                            ins.then_inc(C.s("g"), 1)
                            C.cnt["g"] += 1
                            if idx >= 1:
                                eng.wait_ge(C.s("vo"), M["itr%d" % (idx - 1)])
                            su = "wu0" if fc % 2 == 0 else "wu1"
                            eng.wait_ge(C.s(su), B[su] + 16 * ng)
                            for nh2 in range(2):
                                for k in range(KD):
                                    ins = eng.matmul(
                                        P2[:, nh2 * 512:(nh2 + 1) * 512],
                                        wub[:, (fc % 2) * D + k * 128:
                                            (fc % 2) * D + (k + 1) * 128],
                                        Hf[:, k * S + nh2 * 512:k * S + (nh2 + 1) * 512],
                                        start=(k == 0), stop=(k == KD - 1))
                            ins.then_inc(C.s("u"), 1)
                            C.cnt["u"] += 1
                        eng.wait_ge(C.s("sil"), B["sil"] + e * NFC + NFC)
                        eng.wait_ge(C.s("vo"), M["itr%d" % (e * NFC + NFC - 1)])
                        eng.wait_ge(C.s("wdld"), B["wdld"] + 16 * (e + 1) * NFC)
                        for t in range(8):
                            for dh in range(2):
                                gidx = t * 2 + dh
                                pr = (P0, P1, P2)[gidx % 3]
                                if gidx >= 3:
                                    eng.wait_ge(C.s("vo"),
                                                M["xa%d_%d" % (e, gidx - 3)])
                                for nh2 in range(2):
                                    for fc in range(NFC):
                                        ins = eng.matmul(
                                            pr[:, nh2 * 512:(nh2 + 1) * 512],
                                            inter[:, fc * S + t * 128:
                                                  fc * S + (t + 1) * 128],
                                            wdb[:, fc * D + dh * 1024 + nh2 * 512:
                                                fc * D + dh * 1024 + (nh2 + 1) * 512],
                                            start=(fc == 0), stop=(fc == NFC - 1))
                                ins.then_inc(C.s("d"), 1)
                                C.cnt["d"] += 1
                    if ln2_out:
                        for t in range(8):
                            eng.wait_ge(C.s("vo"), M["ap2%d" % t])
                            for gb in range(4):
                                bi = t * 4 + gb
                                if bi >= 2:
                                    eng.wait_ge(C.s("h2E"), B["h2E"] + bi - 1)
                                for j in range(4):
                                    k = gb * 4 + j
                                    ins = eng.transpose(
                                        tpp[:, (bi % 2) * 1024 + j * 128:
                                            (bi % 2) * 1024 + (j + 1) * 128],
                                        xts[:, k * 128:(k + 1) * 128],
                                        ident[:])
                                ins.then_inc(C.s("tp2"), 1)
                        C.cnt["tp2"] += 32

                bk_.tensor(f_tens)

                def f_sync(eng):
                    for t in range(8):
                        eng.dma_start(
                            out=Xh[:, t * D:(t + 1) * D],
                            in_=x_src[(t0g + t) * 128:(t0g + t + 1) * 128, :],
                        ).then_inc(C.s("xld"), 16)
                    C.cnt["xld"] += 128
                    eng.wait_ge(C.s("vo"), M["xa_all"])
                    for t in range(8):
                        eng.dma_start(
                            out=x_dst[(t0g + t) * 128:(t0g + t + 1) * 128, :],
                            in_=Xh[:, t * D:(t + 1) * D],
                        ).then_inc(C.s("xst"), 16)
                    C.cnt["xst"] += 128
                    if ln2_out:
                        eng.wait_ge(C.s("h2E"), B["h2E"] + 32)
                        eng.dma_start(
                            out=h2_dram[:, :, t0g * 128:t0g * 128 + S]
                                .rearrange("k p j -> p k j"),
                            in_=Hf.ap().rearrange("p (k j) -> p k j", k=KD),
                        ).then_inc(C.s("h2d"), 16)
                        C.cnt["h2d"] += 16
                        eng.wait_ge(C.s("h2d"), C.cnt["h2d"])
                    eng.wait_ge(C.s("xst"), C.cnt["xst"])

                bk_.sync(f_sync)

            DCO = (0, 129, 258, 512, 641, 770)
            AWL_KG = [0] * 4

            def emit_attnA_unit(kg, blk):
                B = C.snap("hld", "awl", "kv1", "mb1", "nd1", "tpA", "dl1",
                           "kvE", "sqA", "sqt", "mbE", "dnE", "tpE", "dout",
                           "ccs")
                M = {}
                blkoff = blk * 512

                def f_gp(eng):
                    if blk == 0:
                        wkv = wdb[:, 0:4096].rearrange("p (k c) -> p k c", k=KD)
                        eng.dma_start(
                            out=wkv[:, :, 0:128],
                            in_=wk_t[:, kg * 128:(kg + 1) * 128]
                                .rearrange("(k p) j -> p k j", p=128),
                        ).then_inc(C.s("awl"), 16)
                        eng.dma_start(
                            out=wkv[:, :, 128:256],
                            in_=wv_t[:, kg * 128:(kg + 1) * 128]
                                .rearrange("(k p) j -> p k j", p=128),
                        ).then_inc(C.s("awl"), 16)
                        eng.dma_start(
                            out=wdb[:, 4096:12288].rearrange("p (k c) -> p k c",
                                                             k=KD),
                            in_=wmb_t[:, kg * 512:(kg + 1) * 512]
                                .rearrange("(k p) j -> p k j", p=128),
                        ).then_inc(C.s("awl"), 16)
                        eng.dma_start(
                            out=wdb[:, 12288:15360].rearrange("p (c j) -> p c j",
                                                              c=NFD),
                            in_=mem4[kg].rearrange("c p j -> p c j"),
                        ).then_inc(C.s("awl"), 16)
                        eng.dma_start(
                            out=wdb[:, 15360:15384].rearrange("p (c j) -> p c j",
                                                              c=NFD),
                            in_=norm4[kg].rearrange("c p j -> p c j"),
                        ).then_inc(C.s("awl"), 16)
                        C.cnt["awl"] += 80
                        AWL_KG[kg] = C.cnt["awl"]
                    if blk == 3:
                        eng.wait_ge(C.s("vo"), M_DL[0])
                        eng.dma_start(out=cc_src[kg], in_=Xh[:, 8192:11288]
                                      ).then_inc(C.s("dout"), 16)
                        C.cnt["dout"] += 16
                        eng.wait_ge(C.s("dout"), C.cnt["dout"])
                        if True:
                            eng.collective_compute(
                                "AllReduce", ALU.add,
                                replica_groups=[list(range(NCORE))],
                                ins=[cc_src[kg].opt()], outs=[cc_dst[kg].opt()],
                            ).then_inc(C.s("ccs"), 1)
                        C.cnt["ccs"] += 1

                M_DL = [0]

                def f_vec(ueng):
                    eng = _OrderedEng(ueng, C, "vo")
                    for t in range(4):
                        par = (t % 2)
                        xc = 1024 + par * 256
                        mks = 4864 + par * 768
                        ueng.wait_ge(C.s("kvE"), B["kvE"] + t + 1)
                        for c in range(NFD):
                            jr = c // 2 + 1
                            hh2 = c % 2
                            base = xc + hh2 * 128
                            eng.tensor_tensor(
                                Xh[:, mks + c * 128 + jr:mks + (c + 1) * 128],
                                Xh[:, base + jr:base + 128],
                                Xh[:, base:base + 128 - jr], ALU.mult)
                            wsrc = xc + (256 if hh2 == 0 else 128) - jr
                            eng.tensor_tensor(
                                Xh[:, mks + c * 128:mks + c * 128 + jr],
                                Xh[:, base:base + jr],
                                Xh[:, wsrc:wsrc + jr], ALU.mult)
                        M["dp%d" % t] = C.vmark()
                        ueng.wait_ge(C.s("sqA"), B["sqA"] + t + 1)
                        eng.tensor_copy(cols[:, 0 + t:1 + t], cols[:, 16 + t:17 + t])
                        M["bn%d" % t] = C.vmark()
                        ueng.wait_ge(C.s("sqt"), B["sqt"] + t + 1)
                        eng.tensor_scalar(cols[:, 24 + t:25 + t],
                                          cols[:, 8 + t:9 + t], 1e-12, None,
                                          ALU.max)
                        eng.reciprocal(cols[:, 32 + t:33 + t],
                                       cols[:, 24 + t:25 + t])
                        eng.tensor_scalar(inter[:, t * 768:(t + 1) * 768],
                                          Xh[:, mks:mks + 768],
                                          cols[:, 32 + t:33 + t], None, ALU.mult)
                        eng.tensor_tensor(cols[:, 40 + t:41 + t],
                                          cols[:, 8 + t:9 + t],
                                          cols[:, 32 + t:33 + t], ALU.mult)
                        eng.tensor_tensor(cols[:, 48 + t:49 + t],
                                          cols[:, 40 + t:41 + t],
                                          cols[:, 40 + t:41 + t], ALU.mult)
                        eng.reciprocal(cols[:, 56 + t:57 + t],
                                       cols[:, 48 + t:49 + t])
                        M["nrm%d" % t] = C.vmark()
                        ueng.wait_ge(C.s("dnE"), B["dnE"] + t + 1)
                        dq = 1536 + t * 4
                        rq = 1552 + t * 4
                        cq = 1568 + t * 4
                        eng.reciprocal(Xh[:, rq:rq + 4], Xh[:, dq:dq + 4])
                        eng.tensor_scalar(Xh[:, cq:cq + 4], Xh[:, dq:dq + 4],
                                          cols[:, 56 + t:57 + t], None, ALU.mult)
                        eng.tensor_scalar(Xh[:, cq:cq + 4], Xh[:, cq:cq + 4],
                                          -1.0, 1.0, ALU.mult, ALU.add)
                        eng.tensor_scalar(Xh[:, cq:cq + 4], Xh[:, cq:cq + 4],
                                          0.0, 1.0, ALU.max, ALU.min)
                        ueng.wait_ge(C.s("nd1"), B["nd1"] + t + 1)
                        ueng.wait_ge(C.s("mbE"), B["mbE"] + t + 1)
                        for hh in range(4):
                            eng.tensor_scalar(Xh[:, 6400:6528],
                                              P0[:, 512 + hh * 128:
                                                 512 + (hh + 1) * 128],
                                              Xh[:, rq + hh:rq + hh + 1], None,
                                              ALU.mult)
                            eng.tensor_tensor(Xh[:, 6528:6656],
                                              Xh[:, t * 128:(t + 1) * 128],
                                              Xh[:, 6400:6528], ALU.subtract)
                            ws = 6144 + (t * 4 + hh) * 129
                            eng.tensor_tensor(
                                inter[:, ws:ws + 128], Xh[:, 6528:6656],
                                Xh[:, 2048 + t * 512 + hh * 128:
                                   2048 + t * 512 + (hh + 1) * 128], ALU.mult)
                            eng.tensor_copy(inter[:, ws + 128:ws + 129],
                                            Xh[:, cq + hh:cq + hh + 1])
                        M["pv%d" % t] = C.vmark()
                    M["wmv"] = C.vmark()
                    for hh in range(4):
                        ueng.wait_ge(C.s("dl1"), B["dl1"] + hh + 1)
                        ds = 8192 + hh * 774
                        if blk == 0:
                            eng.tensor_copy(Xh[:, ds:ds + 387], P2[:, 0:387])
                            eng.tensor_copy(Xh[:, ds + 387:ds + 774],
                                            P2[:, 512:899])
                        else:
                            eng.tensor_tensor(Xh[:, ds:ds + 387],
                                              Xh[:, ds:ds + 387],
                                              P2[:, 0:387], ALU.add)
                            eng.tensor_tensor(Xh[:, ds + 387:ds + 774],
                                              Xh[:, ds + 387:ds + 774],
                                              P2[:, 512:899], ALU.add)
                        M["dlE%d" % hh] = C.vmark()
                    M_DL[0] = C.vmark()

                bk_.vector(f_vec)

                def f_scal(eng):
                    for t in range(4):
                        par = (t % 2)
                        xc = 1024 + par * 256
                        mks = 4864 + par * 768
                        eng.wait_ge(C.s("kv1"), B["kv1"] + t + 1)
                        eng.activation(Xh[:, xc:xc + 128], P0[:, 0:128], AF.Relu)
                        eng.activation(Xh[:, xc + 128:xc + 256], P0[:, 0:128],
                                       AF.Relu, scale=-1.0)
                        eng.activation(Xh[:, t * 128:(t + 1) * 128],
                                       P0[:, 128:256], AF.Copy
                                       ).then_inc(C.s("kvE"), 1)
                        C.cnt["kvE"] += 1
                        eng.wait_ge(C.s("vo"), M["dp%d" % t])
                        eng.activation(gsilu[:, 0:768], Xh[:, mks:mks + 768],
                                       AF.Square,
                                       accum_out=cols[:, 16 + t:17 + t]
                                       ).then_inc(C.s("sqA"), 1)
                        C.cnt["sqA"] += 1
                        eng.wait_ge(C.s("vo"), M["bn%d" % t])
                        eng.activation(cols[:, 8 + t:9 + t], cols[:, 0 + t:1 + t],
                                       AF.Sqrt).then_inc(C.s("sqt"), 1)
                        C.cnt["sqt"] += 1
                        eng.wait_ge(C.s("mb1"), B["mb1"] + t + 1)
                        eng.activation(Xh[:, 2048 + t * 512:2048 + (t + 1) * 512],
                                       P0[:, 0:512], AF.Sigmoid
                                       ).then_inc(C.s("mbE"), 1)
                        C.cnt["mbE"] += 1
                        eng.wait_ge(C.s("tpA"), B["tpA"] + t + 1)
                        dstT = inter[:, 3072:6144].rearrange(
                            "p (c j) -> p c j", c=NFD)[:, :, t * 128:(t + 1) * 128]
                        srcT = tpp[:, par * 1024:par * 1024 + 768].rearrange(
                            "p (c j) -> p c j", c=NFD)
                        eng.activation(dstT, srcT, AF.Copy).then_inc(C.s("tpE"), 1)
                        C.cnt["tpE"] += 1
                        eng.wait_ge(C.s("nd1"), B["nd1"] + t + 1)
                        eng.activation(Xh[:, 1536 + t * 4:1536 + (t + 1) * 4],
                                       P1[:, 0:4], AF.Identity, bias=epsA[:]
                                       ).then_inc(C.s("dnE"), 1)
                        C.cnt["dnE"] += 1

                bk_.scalar(f_scal)

                def f_tens(eng):
                    eng.wait_ge(C.s("hld"), B["hld"] + 16)
                    eng.wait_ge(C.s("awl"), AWL_KG[kg])
                    for t in range(4):
                        par = (t % 2)
                        if t >= 1:
                            eng.wait_ge(C.s("mbE"), B["mbE"] + t)
                        for k in range(KD):
                            eng.matmul(P0[:, 0:256],
                                       Hf[:, k * S + t * 128:k * S + (t + 1) * 128],
                                       wdb[:, k * 256:(k + 1) * 256],
                                       start=(k == 0), stop=False)
                        ins = eng.matmul(P0[:, 0:256], ones[0:1, 0:128],
                                         bkv_s[0:1, kg * 256:(kg + 1) * 256],
                                         start=False, stop=True)
                        ins.then_inc(C.s("kv1"), 1)
                        C.cnt["kv1"] += 1
                        eng.wait_ge(C.s("kvE"), B["kvE"] + t + 1)
                        for k in range(KD):
                            eng.matmul(P0[:, 0:512],
                                       Hf[:, k * S + t * 128:k * S + (t + 1) * 128],
                                       wdb[:, 4096 + k * 512:4096 + (k + 1) * 512],
                                       start=(k == 0), stop=False)
                        ins = eng.matmul(P0[:, 0:512], ones[0:1, 0:128],
                                         bmb_s[0:1, kg * 512:(kg + 1) * 512],
                                         start=False, stop=True)
                        ins.then_inc(C.s("mb1"), 1)
                        C.cnt["mb1"] += 1
                        eng.wait_ge(C.s("vo"), M["nrm%d" % t])
                        if t >= 2:
                            eng.wait_ge(C.s("tpE"), B["tpE"] + t - 1)
                        for c in range(NFD):
                            ins = eng.transpose(
                                tpp[:, par * 1024 + c * 128:
                                    par * 1024 + (c + 1) * 128],
                                inter[:, t * 768 + c * 128:t * 768 + (c + 1) * 128],
                                ident[:])
                        ins.then_inc(C.s("tpA"), 1)
                        C.cnt["tpA"] += 1
                        eng.wait_ge(C.s("tpE"), B["tpE"] + t + 1)
                        if t >= 1:
                            eng.wait_ge(C.s("dnE"), B["dnE"] + t)
                            eng.wait_ge(C.s("vo"), M["pv%d" % (t - 1)])
                        for c in range(NFD):
                            lh = inter[:, 3072 + c * 512 + t * 128:
                                       3072 + c * 512 + (t + 1) * 128]
                            eng.matmul(P0[:, 512:1024], lh,
                                       wdb[:, 12288 + c * 512:12288 + (c + 1) * 512],
                                       start=(c == 0), stop=(c == NFD - 1))
                            ins = eng.matmul(P1[:, 0:4], lh,
                                             wdb[:, 15360 + c * 4:15360 + (c + 1) * 4],
                                             start=(c == 0), stop=(c == NFD - 1))
                        ins.then_inc(C.s("nd1"), 1)
                        C.cnt["nd1"] += 1
                    eng.wait_ge(C.s("vo"), M["wmv"])
                    for hh in range(4):
                        if hh >= 1:
                            eng.wait_ge(C.s("vo"), M["dlE%d" % (hh - 1)])
                        elif blk >= 1:
                            eng.wait_ge(C.s("vo"), B_DL_PREV[0])
                        for c in range(NFD):
                            for t in range(4):
                                ws = 6144 + (t * 4 + hh) * 129
                                ins = eng.matmul(
                                    P2[:, DCO[c]:DCO[c] + 129],
                                    inter[:, t * 768 + c * 128:
                                          t * 768 + (c + 1) * 128],
                                    inter[:, ws:ws + 129],
                                    start=(t == 0), stop=(t == 3))
                        ins.then_inc(C.s("dl1"), 1)
                        C.cnt["dl1"] += 1

                bk_.tensor(f_tens)

                def f_sync(eng):
                    eng.dma_start(
                        out=Hf.ap().rearrange("p (k j) -> p k j", k=KD)[:, :, 0:512],
                        in_=h2_dram[:, :, blkoff:blkoff + 512]
                            .rearrange("k p j -> p k j"),
                    ).then_inc(C.s("hld"), 16)
                    C.cnt["hld"] += 16

                bk_.sync(f_sync)
                bk_.gpsimd(f_gp)
                B_DL_PREV[0] = M_DL[0]

            B_DL_PREV = [0]
            AUG_LAST = [0]

            def emit_aug_kg(kg):
                B = C.snap("agl", "ags", "ccs")
                M = {}

                def f_gp(eng):
                    eng.wait_ge(C.s("ccs"), kg + 1)
                    eng.dma_start(
                        out=Xh[:, 0:3096].rearrange("p (h c j) -> p h c j",
                                                    h=4, c=NFD),
                        in_=maug[4 * kg:4 * kg + 4]
                            .rearrange("h c p j -> p h c j"),
                    ).then_inc(C.s("agl"), 16)
                    eng.dma_start(out=Xh[:, 4096:7192], in_=cc_dst[kg]
                                  ).then_inc(C.s("agl"), 16)
                    C.cnt["agl"] += 32
                    eng.wait_ge(C.s("vo"), M["aug"])
                    eng.dma_start(
                        out=aug_dram[4 * kg:4 * kg + 4]
                            .rearrange("h p j -> p h j"),
                        in_=wdb[:, 16384:19480].rearrange("p (h j) -> p h j", h=4),
                    ).then_inc(C.s("ags"), 16)
                    C.cnt["ags"] += 16
                    eng.wait_ge(C.s("ags"), C.cnt["ags"])

                def f_vec(ueng):
                    eng = _OrderedEng(ueng, C, "vo")
                    ueng.wait_ge(C.s("agl"), B["agl"] + 32)
                    eng.tensor_tensor(wdb[:, 16384:19480], Xh[:, 0:3096],
                                      Xh[:, 4096:7192], ALU.add)
                    M["aug"] = C.vmark()
                    AUG_LAST[0] = M["aug"]

                bk_.vector(f_vec)
                bk_.gpsimd(f_gp)

            BWL_KG = [0]

            def emit_attnB_unit(blk, kg):
                B = C.snap("hldB", "xldB", "bwl", "q1", "g1", "tpAB", "na1",
                           "qE", "gE", "tpEB", "sqB", "sqtB", "naE", "xstB")
                M = {}
                blkoff = blk * 512

                def f_gp(eng):
                    eng.dma_start(
                        out=wdb[:, 0:8192].rearrange("p (k c) -> p k c", k=KD),
                        in_=wq_t[:, kg * 512:(kg + 1) * 512]
                            .rearrange("(k p) j -> p k j", p=128),
                    ).then_inc(C.s("bwl"), 16)
                    eng.dma_start(
                        out=wdb[:, 8192:16384].rearrange("p (k c) -> p k c", k=KD),
                        in_=wgt_t[:, kg * 512:(kg + 1) * 512]
                            .rearrange("(k p) j -> p k j", p=128),
                    ).then_inc(C.s("bwl"), 16)
                    eng.wait_ge(C.s("ags"), 16 * (kg + 1))
                    eng.dma_start(
                        out=inter[:, 3840:6936].rearrange("p (h j) -> p h j", h=4),
                        in_=aug_dram[4 * kg:4 * kg + 4]
                            .rearrange("h p j -> p h j"),
                    ).then_inc(C.s("bwl"), 16)
                    C.cnt["bwl"] += 48

                def f_vec(ueng):
                    eng = _OrderedEng(ueng, C, "vo")
                    if kg == 0:
                        ueng.wait_ge(C.s("xldB"), B["xldB"] + 64)
                        for t in range(4):
                            g2 = blk * 4 + t
                            eng.tensor_scalar(cols[:, 0 + t:1 + t],
                                              ln2_mr[:, 2 * g2 + 1:2 * g2 + 2],
                                              1.0, None, ALU.add)
                            eng.tensor_tensor(cols[:, 8 + t:9 + t],
                                              ln2_mr[:, 2 * g2:2 * g2 + 1],
                                              ln2_mr[:, 2 * g2 + 1:2 * g2 + 2],
                                              ALU.mult)
                            eng.tensor_scalar(Xh[:, t * D:(t + 1) * D],
                                              Xh[:, t * D:(t + 1) * D],
                                              cols[:, 0 + t:1 + t],
                                              cols[:, 8 + t:9 + t],
                                              ALU.mult, ALU.subtract)
                        M["lnB"] = C.vmark()
                    for t in range(4):
                        qb = 8192 + t * 512
                        ueng.wait_ge(C.s("qE"), B["qE"] + t + 1)
                        if t >= 1:
                            ueng.wait_ge(C.s("tpAB"), B["tpAB"] + 4 * t)
                        xq = Xh[:, 12288:13312].rearrange("p (h c) -> p h c", h=4)
                        q4v = Xh[:, qb:qb + 512].rearrange("p (h c) -> p h c", h=4)
                        mq4v = Xh[:, 13312:16384].rearrange("p (h c) -> p h c", h=4)
                        eng.tensor_scalar(xq[:, :, 0:128], q4v, 0.0, None, ALU.max)
                        eng.tensor_scalar(xq[:, :, 128:256], q4v, -1.0, 0.0,
                                          ALU.mult, ALU.max)
                        for c in range(NFD):
                            jr = c // 2 + 1
                            hh2 = c % 2
                            bb = hh2 * 128
                            eng.tensor_tensor(
                                mq4v[:, :, c * 128 + jr:(c + 1) * 128],
                                xq[:, :, bb + jr:bb + 128],
                                xq[:, :, bb:bb + 128 - jr], ALU.mult)
                            wsrc = (256 if hh2 == 0 else 128) - jr
                            eng.tensor_tensor(
                                mq4v[:, :, c * 128:c * 128 + jr],
                                xq[:, :, bb:bb + jr],
                                xq[:, :, wsrc:wsrc + jr], ALU.mult)
                        M["mq%d" % t] = C.vmark()
                        ueng.wait_ge(C.s("sqB"), B["sqB"] + 4 * (t + 1))
                        eng.tensor_copy(cols[:, 0:4], cols[:, 16:20])
                        M["bnB%d" % t] = C.vmark()
                        ueng.wait_ge(C.s("sqtB"), B["sqtB"] + t + 1)
                        eng.tensor_scalar(cols[:, 40:44], cols[:, 8:12], 1e-12,
                                          None, ALU.max)
                        eng.reciprocal(cols[:, 48:52], cols[:, 40:44])
                        for hh in range(4):
                            eng.tensor_scalar(
                                inter[:, hh * 768:(hh + 1) * 768],
                                Xh[:, 13312 + hh * 768:13312 + (hh + 1) * 768],
                                cols[:, 48 + hh:49 + hh], None, ALU.mult)
                        M["mqn%d" % t] = C.vmark()
                        ueng.wait_ge(C.s("gE"), B["gE"] + t + 1)
                        for hh in range(4):
                            n = t * 4 + hh
                            ueng.wait_ge(C.s("naE"), B["naE"] + n + 1)
                            eng.reciprocal(cols[:, 32:33], cols[:, 24:25])
                            M["rc%d" % n] = C.vmark()
                            nb = (n % 2) * 512
                            eng.tensor_scalar(sc_sb[:, 0:128], P1[:, nb:nb + 128],
                                              cols[:, 32:33], None, ALU.mult)
                            M["mh%d" % n] = C.vmark()
                            eng.tensor_tensor(
                                esc[:, 0:128],
                                Xh[:, 10240 + t * 512 + hh * 128:
                                   10240 + t * 512 + (hh + 1) * 128],
                                sc_sb[:, 0:128], ALU.mult)
                            xsl = Xh[:, t * D + (4 * kg + hh) * 128:
                                     t * D + (4 * kg + hh + 1) * 128]
                            eng.tensor_tensor(xsl, xsl, esc[:, 0:128], ALU.add)
                    M["updF"] = C.vmark()

                bk_.vector(f_vec)

                def f_scal(eng):
                    for t in range(4):
                        pq = P0 if t % 2 == 0 else P2
                        eng.wait_ge(C.s("q1"), B["q1"] + t + 1)
                        eng.activation(Xh[:, 8192 + t * 512:8192 + (t + 1) * 512],
                                       pq[:, 0:512], AF.Copy
                                       ).then_inc(C.s("qE"), 1)
                        C.cnt["qE"] += 1
                        eng.wait_ge(C.s("g1"), B["g1"] + t + 1)
                        eng.activation(Xh[:, 10240 + t * 512:10240 + (t + 1) * 512],
                                       pq[:, 512:1024], AF.Sigmoid
                                       ).then_inc(C.s("gE"), 1)
                        C.cnt["gE"] += 1
                        eng.wait_ge(C.s("vo"), M["mq%d" % t])
                        for hh in range(4):
                            eng.activation(gsilu[:, 0:768],
                                           Xh[:, 13312 + hh * 768:
                                              13312 + (hh + 1) * 768],
                                           AF.Square,
                                           accum_out=cols[:, 16 + hh:17 + hh]
                                           ).then_inc(C.s("sqB"), 1)
                            C.cnt["sqB"] += 1
                        eng.wait_ge(C.s("vo"), M["bnB%d" % t])
                        eng.activation(cols[:, 8:12], cols[:, 0:4], AF.Sqrt
                                       ).then_inc(C.s("sqtB"), 1)
                        C.cnt["sqtB"] += 1
                        for hh in range(4):
                            n = t * 4 + hh
                            eng.wait_ge(C.s("tpAB"), B["tpAB"] + n + 1)
                            eng.activation(inter[:, 3072:3840],
                                           tpp[:, (n % 2) * 1024:
                                               (n % 2) * 1024 + 768], AF.Copy
                                           ).then_inc(C.s("tpEB"), 1)
                            C.cnt["tpEB"] += 1
                            eng.wait_ge(C.s("na1"), B["na1"] + n + 1)
                            if n >= 1:
                                eng.wait_ge(C.s("vo"), M["rc%d" % (n - 1)])
                            nb = (n % 2) * 512
                            eng.activation(cols[:, 24:25], P1[:, nb + 128:nb + 129],
                                           AF.Identity, bias=epsA[:]
                                           ).then_inc(C.s("naE"), 1)
                            C.cnt["naE"] += 1

                bk_.scalar(f_scal)

                def f_tens(eng):
                    eng.wait_ge(C.s("hldB"), B["hldB"] + (16 if kg == 0 else 0))
                    eng.wait_ge(C.s("bwl"), B["bwl"] + 48)
                    if kg == 0:
                        eng.wait_ge(C.s("vo"), M["lnB"])
                    for t in range(4):
                        pq = P0 if t % 2 == 0 else P2
                        if t >= 2:
                            eng.wait_ge(C.s("qE"), B["qE"] + t - 1)
                        for k in range(KD):
                            eng.matmul(pq[:, 0:512],
                                       Hf[:, k * S + t * 128:k * S + (t + 1) * 128],
                                       wdb[:, k * 512:(k + 1) * 512],
                                       start=(k == 0), stop=False)
                        ins = eng.matmul(pq[:, 0:512], ones[0:1, 0:128],
                                         bq_s[0:1, kg * 512:(kg + 1) * 512],
                                         start=False, stop=True)
                        ins.then_inc(C.s("q1"), 1)
                        C.cnt["q1"] += 1
                        if t >= 2:
                            eng.wait_ge(C.s("gE"), B["gE"] + t - 1)
                        for k in range(KD):
                            eng.matmul(pq[:, 512:1024],
                                       Hf[:, k * S + t * 128:k * S + (t + 1) * 128],
                                       wdb[:, 8192 + k * 512:8192 + (k + 1) * 512],
                                       start=(k == 0), stop=False)
                        ins = eng.matmul(pq[:, 512:1024], ones[0:1, 0:128],
                                         bg_s[0:1, kg * 512:(kg + 1) * 512],
                                         start=False, stop=True)
                        ins.then_inc(C.s("g1"), 1)
                        C.cnt["g1"] += 1
                        eng.wait_ge(C.s("vo"), M["mqn%d" % t])
                        for hh in range(4):
                            n = t * 4 + hh
                            if n >= 2:
                                eng.wait_ge(C.s("tpEB"), B["tpEB"] + n - 1)
                            for c in range(NFD):
                                ins = eng.transpose(
                                    tpp[:, (n % 2) * 1024 + c * 128:
                                        (n % 2) * 1024 + (c + 1) * 128],
                                    inter[:, hh * 768 + c * 128:
                                          hh * 768 + (c + 1) * 128],
                                    ident[:])
                            ins.then_inc(C.s("tpAB"), 1)
                            C.cnt["tpAB"] += 1
                            eng.wait_ge(C.s("tpEB"), B["tpEB"] + n + 1)
                            if n >= 2:
                                eng.wait_ge(C.s("vo"), M["mh%d" % (n - 2)])
                            nb = (n % 2) * 512
                            for c in range(NFD):
                                ins = eng.matmul(
                                    P1[:, nb:nb + 129],
                                    inter[:, 3072 + c * 128:3072 + (c + 1) * 128],
                                    inter[:, 3840 + hh * 774 + c * 129:
                                          3840 + hh * 774 + (c + 1) * 129],
                                    start=(c == 0), stop=(c == NFD - 1))
                            ins.then_inc(C.s("na1"), 1)
                            C.cnt["na1"] += 1

                bk_.tensor(f_tens)

                def f_sync(eng):
                    if kg == 0:
                        if blk == 0:
                            eng.wait_ge(C.s("vo"), AUG_LAST[0])
                        for t in range(4):
                            eng.dma_start(
                                out=Xh[:, t * D:(t + 1) * D],
                                in_=x_dram[(blk * 4 + t) * 128:
                                           (blk * 4 + t + 1) * 128, :],
                            ).then_inc(C.s("xldB"), 16)
                        C.cnt["xldB"] += 64
                        eng.dma_start(
                            out=Hf.ap().rearrange("p (k j) -> p k j", k=KD)[
                                :, :, 0:512],
                            in_=h2_dram[:, :, blkoff:blkoff + 512]
                                .rearrange("k p j -> p k j"),
                        ).then_inc(C.s("hldB"), 16)
                        C.cnt["hldB"] += 16
                    if kg == 3:
                        eng.wait_ge(C.s("vo"), M["updF"])
                        for t in range(4):
                            eng.dma_start(
                                out=x_dram[(blk * 4 + t) * 128:
                                           (blk * 4 + t + 1) * 128, :],
                                in_=Xh[:, t * D:(t + 1) * D],
                            ).then_inc(C.s("xstB"), 16)
                        C.cnt["xstB"] += 64
                        eng.wait_ge(C.s("xstB"), C.cnt["xstB"])

                bk_.sync(f_sync)
                bk_.gpsimd(f_gp)

            ln2_on = not globals().get("LN2_OFF", False)
            if "ffn1" in phases:
                emit_ffn_sweep(0, x_in, x_dram, ln2_on, 0.5)
                barrier()
                emit_ffn_sweep(1, x_in, x_dram, ln2_on, 0.5)
                barrier()
            if "attnA" in phases:
                for kg_ in range(4):
                    for blk_ in range(4):
                        emit_attnA_unit(kg_, blk_)
                        barrier()
            if "aug" in phases:
                for kg_ in range(4):
                    emit_aug_kg(kg_)
            if "attnB" in phases:
                for blk_ in range(4):
                    for kg_ in range(4):
                        emit_attnB_unit(blk_, kg_)
                        barrier()
            if "dev_i" in phases:
                inter_d = nc.dram_tensor("inter_d", [128, NFC * S], BF16,
                                         kind="ExternalOutput").ap()
                cw_d = nc.dram_tensor("cw_d", [128, 128], F32,
                                      kind="ExternalOutput").ap()
                gsilu_d = nc.dram_tensor("gsilu_d", [128, S], F32,
                                         kind="ExternalOutput").ap()
                cols_d = nc.dram_tensor("cols_d", [128, 64], F32,
                                        kind="ExternalOutput").ap()
                hf_d = nc.dram_tensor("hf_d", [128, KD * S], BF16,
                                      kind="ExternalOutput").ap()

                sc_d = nc.dram_tensor("sc_d", [128, 128], F32,
                                      kind="ExternalOutput").ap()
                esc_d = nc.dram_tensor("esc_d", [128, 128], F32,
                                       kind="ExternalOutput").ap()

                gw_d = nc.dram_tensor("gw_d", [128, KD * 16], BF16,
                                      kind="ExternalOutput").ap()

                escsn_d = nc.dram_tensor("escsn_d", [128, 128], F32,
                                         kind="ExternalOutput").ap()
                scsn_d = nc.dram_tensor("scsn_d", [128, 128], F32,
                                        kind="ExternalOutput").ap()
                colsn_d = nc.dram_tensor("colsn_d", [128, 32], F32,
                                         kind="ExternalOutput").ap()

                def f_dump(eng):
                    for dst, srcb in ((inter_d, inter), (cw_d, cw),
                                      (gsilu_d, gsilu), (cols_d, cols),
                                      (hf_d, Hf), (sc_d, sc_sb), (esc_d, esc),
                                      (gw_d, gw_s), (escsn_d, escsn),
                                      (scsn_d, scsn), (colsn_d, colsn)):
                        eng.dma_start(out=dst, in_=srcb[:]).then_inc(C.s("dvd"), 16)
                        C.cnt["dvd"] = C.cnt.get("dvd", 0) + 16
                    eng.wait_ge(C.s("dvd"), C.cnt["dvd"])
                bk_.sync(f_dump)
                barrier()
            if "ffn2" in phases:
                src = x_dram if ("attnB" in phases or "ffn1" in phases) else x_in
                emit_ffn_sweep(0, src, y_out, False, 0.5)
                barrier()
                emit_ffn_sweep(1, src, y_out, False, 0.5)
                barrier()

    return nc


def kernel(**inputs):
    maps = _prep(inputs)
    nc = build_program()
    res = run_bass_kernel_spmd(nc, maps, core_ids=list(range(NCORE)))
    out = np.concatenate([res.results[c]["y_out"] for c in range(NCORE)], axis=0)
    return out.reshape(4, 4096, 2048)


# version 04
# speedup vs baseline: 1.0546x; 1.0266x over previous
import numpy as np
import ml_dtypes
import concourse.bass as bass
import concourse.mybir as mybir
from concourse.bass_utils import run_bass_kernel_spmd

F32 = mybir.dt.float32
BF16 = mybir.dt.bfloat16
AF = mybir.ActivationFunctionType
ALU = mybir.AluOpType
AX = mybir.AxisListType

NCORE = 8
TOK = 2048
NT = 16
D = 2048
KD = 16
MOE_D = 1408
NFC = 11
NE = 16
TOPK = 6
S = 1024
HKV = 4
NH = 16
FD = 768
NFD = 6
EPS_A = 1e-8
EPS_LN = 1e-5
BIG = 1.0e30


def _prep(i):
    f32 = np.float32

    def c(a):
        return np.ascontiguousarray(a, dtype=f32)

    wg = np.asarray(i["L0_ffn_w_gate"], f32)
    wu = np.asarray(i["L0_ffn_w_up"], f32)
    wd = np.asarray(i["L0_ffn_w_down"], f32)
    gw = np.asarray(i["L0_ffn_gate_w"], f32)
    wg_t = c(wg.reshape(NE, NFC, 128, KD, 128).transpose(0, 1, 4, 3, 2)
             .reshape(NE, NFC, 128, D))
    wu_t = c(wu.reshape(NE, NFC, 128, KD, 128).transpose(0, 1, 4, 3, 2)
             .reshape(NE, NFC, 128, D))
    wd_t = c(wd.reshape(NE, D, NFC, 128).transpose(0, 2, 3, 1))
    gw_t = c(gw.reshape(16, KD, 128).transpose(2, 1, 0).reshape(128, KD * 16))

    wk = np.asarray(i["L0_attn_Wk"], f32)
    wv = np.asarray(i["L0_attn_Wv"], f32)
    wmb = np.asarray(i["L0_attn_Wmb"], f32)
    wq = np.asarray(i["L0_attn_Wq"], f32)
    wgt = np.asarray(i["L0_attn_Wg"], f32)
    wk_t, wv_t = c(wk.T), c(wv.T)
    wmb_t, wq_t, wgt_t = c(wmb.T), c(wq.T), c(wgt.T)

    bk = np.asarray(i["L0_attn_bk"], f32)
    bv = np.asarray(i["L0_attn_bv"], f32)
    bkv = np.concatenate(
        [np.stack([bk.reshape(4, 128)[g], bv.reshape(4, 128)[g]]).reshape(256)
         for g in range(4)]).reshape(1, 1024)
    bkv = c(bkv)
    bmb = c(np.asarray(i["L0_attn_bmb"], f32).reshape(1, D))
    bq = c(np.asarray(i["L0_attn_bq"], f32).reshape(1, D))
    bg = c(np.asarray(i["L0_attn_bg"], f32).reshape(1, D))

    mem = np.asarray(i["L0_attn_memory"], f32)
    nrm = np.asarray(i["L0_attn_norm"], f32)
    mem4 = c(mem.reshape(4, 4, NFD, 128, 128).transpose(0, 2, 3, 1, 4)
             .reshape(4, NFD, 128, 512))
    norm4 = c(nrm.reshape(4, 4, NFD, 128).transpose(0, 2, 3, 1))
    maug = c(np.concatenate(
        [mem.reshape(NH, NFD, 128, 128), nrm.reshape(NH, NFD, 128, 1)], axis=3))

    x = np.asarray(i["L0_attn_hidden_states"], f32).reshape(NCORE * TOK, D)
    shared = dict(wg_t=wg_t, wu_t=wu_t, wd_t=wd_t, gw_t=gw_t,
                  wk_t=wk_t, wv_t=wv_t, wmb_t=wmb_t, wq_t=wq_t, wgt_t=wgt_t,
                  bkv=bkv, bmb=bmb, bq=bq, bg=bg,
                  mem4=mem4, norm4=norm4, maug=maug)
    maps = []
    for cix in range(NCORE):
        m = dict(shared)
        m["x_in"] = np.ascontiguousarray(x[cix * TOK:(cix + 1) * TOK])
        maps.append(m)
    return maps


class Ctx:

    def __init__(self, nc, stack):
        self.nc = nc
        self.stack = stack
        self.sem = {}
        self.cnt = {}

    def s(self, name):
        if name not in self.sem:
            self.sem[name] = self.stack.enter_context(self.nc.semaphore(name))
            self.cnt[name] = 0
        return self.sem[name]

    def inc(self, ins, name, k=1):
        ins.then_inc(self.s(name), k)
        self.cnt[name] += k
        return self.cnt[name]

    def snap(self, *names):
        return {n: self.cnt.get(n, 0) for n in names}

    def vmark(self):
        return self.cnt.get("vo", 0)


_ORDERED_OPS = ("activation", "tensor_tensor", "tensor_scalar",
                "scalar_tensor_tensor", "tensor_reduce", "tensor_copy",
                "reciprocal", "memset", "tensor_add", "tensor_sub",
                "tensor_mul", "select")


class _OrderedEng:

    def __init__(self, eng, C, nm):
        self._e = eng
        self._C = C
        self._nm = nm

    def __getattr__(self, a):
        f = getattr(self._e, a)
        if a not in _ORDERED_OPS:
            return f

        def wrap(*args, **kw):
            n = self._C.cnt.get(self._nm, 0)
            if n > 0:
                self._e.wait_ge(self._C.s(self._nm), n)
            ins = f(*args, **kw)
            ins.then_inc(self._C.s(self._nm), 1)
            self._C.cnt[self._nm] = n + 1
            return ins
        return wrap


def build_program(phases=("ffn1", "attnA", "aug", "attnB", "ffn2"),
                  sim_safe=False):
    import contextlib
    nc = bass.Bass("TRN2", target_bir_lowering=False, debug=False,
                   detect_race_conditions=not sim_safe)

    x_in = nc.dram_tensor("x_in", [TOK, D], F32, kind="ExternalInput").ap()
    y_out = nc.dram_tensor("y_out", [TOK, D], F32, kind="ExternalOutput").ap()
    wg_t = nc.dram_tensor("wg_t", [NE, NFC, 128, D], F32, kind="ExternalInput").ap()
    wu_t = nc.dram_tensor("wu_t", [NE, NFC, 128, D], F32, kind="ExternalInput").ap()
    wd_t = nc.dram_tensor("wd_t", [NE, NFC, 128, D], F32, kind="ExternalInput").ap()
    gw_t = nc.dram_tensor("gw_t", [128, KD * 16], F32, kind="ExternalInput").ap()
    wk_t = nc.dram_tensor("wk_t", [D, 512], F32, kind="ExternalInput").ap()
    wv_t = nc.dram_tensor("wv_t", [D, 512], F32, kind="ExternalInput").ap()
    wmb_t = nc.dram_tensor("wmb_t", [D, D], F32, kind="ExternalInput").ap()
    wq_t = nc.dram_tensor("wq_t", [D, D], F32, kind="ExternalInput").ap()
    wgt_t = nc.dram_tensor("wgt_t", [D, D], F32, kind="ExternalInput").ap()
    bkv = nc.dram_tensor("bkv", [1, 1024], F32, kind="ExternalInput").ap()
    bmb = nc.dram_tensor("bmb", [1, D], F32, kind="ExternalInput").ap()
    bq = nc.dram_tensor("bq", [1, D], F32, kind="ExternalInput").ap()
    bg = nc.dram_tensor("bg", [1, D], F32, kind="ExternalInput").ap()
    mem4 = nc.dram_tensor("mem4", [4, NFD, 128, 512], F32, kind="ExternalInput").ap()
    norm4 = nc.dram_tensor("norm4", [4, NFD, 128, 4], F32, kind="ExternalInput").ap()
    maug = nc.dram_tensor("maug", [NH, NFD, 128, 129], F32, kind="ExternalInput").ap()

    dev = "dev_x" in phases
    x_dram = nc.dram_tensor("x_dram", [TOK, D], F32,
                            kind="ExternalOutput" if dev else "Internal").ap()
    h2_dram = nc.dram_tensor(
        "h2_dram", [KD, 128, TOK], BF16,
        kind="ExternalOutput" if "dev_h2" in phases else "Internal").ap()
    cc_src = [nc.dram_tensor("cc_src%d" % g, [128, 3096], F32,
                             kind="Internal").ap() for g in range(4)]
    cc_dst = [nc.dram_tensor("cc_dst%d" % g, [128, 3096], F32,
                             addr_space="Shared", kind="Internal").ap()
              for g in range(4)]
    aug_dram = nc.dram_tensor("aug_dram", [NH, 128, NFD * 129], BF16,
                              kind="Internal").ap()

    ident_h = nc.inline_tensor(np.eye(128, dtype=ml_dtypes.bfloat16), name="identc")
    identf_h = nc.inline_tensor(np.eye(16, dtype=np.float32), name="identfc")
    ones_h = nc.inline_tensor(np.ones((1, 128), dtype=ml_dtypes.bfloat16), name="onesc")

    st = contextlib.ExitStack()
    with st:
        C = Ctx(nc, st)
        sb = lambda n, sh, dt: st.enter_context(nc.sbuf_tensor(n, sh, dt))
        ps = lambda n, sh, dt: st.enter_context(nc.psum_tensor(n, sh, dt))

        Hf = sb("Hf", [128, KD * S], BF16)
        ident = sb("ident", [128, 128], BF16)
        identf = sb("identf", [16, 16], F32)
        ones = sb("ones", [1, 128], BF16)
        bkv_s = sb("bkv_s", [1, 1024], BF16)
        bmb_s = sb("bmb_s", [1, D], BF16)
        bq_s = sb("bq_s", [1, D], BF16)
        bg_s = sb("bg_s", [1, D], BF16)
        gw_s = sb("gw_s", [128, KD * 16], BF16)
        ln2_mr = sb("ln2_mr", [128, 2 * NT], F32)
        cols = sb("cols", [128, 64], F32)
        epsA = sb("epsA", [128, 1], F32)

        Xh = sb("Xh", [128, 8 * D], F32)
        xts = sb("xts", [128, D], BF16)
        gsilu = sb("gsilu", [128, S], F32)
        inter = sb("inter", [128, NFC * S], BF16)
        wgb = sb("wgb", [128, 2 * D], BF16)
        wub = sb("wub", [128, 2 * D], BF16)
        wdb = sb("wdb", [128, NFC * D], BF16)
        cw = sb("cw", [128, 8 * 16], F32)
        sc_sb = sb("sc_sb", [128, 8 * 16], F32)
        esc = sb("esc", [128, 8 * 16], F32)
        wk_sc = sb("wk_sc", [128, 8 * 16], F32)
        escsn = sb("escsn", [128, 8 * 16], F32)
        scsn = sb("scsn", [128, 8 * 16], F32)
        colsn = sb("colsn", [128, 32], F32)

        tpp = ps("tpp", [128, 2048], BF16)
        P0 = ps("P0", [128, 1024], F32)
        P1 = ps("P1", [128, 1024], F32)
        P2 = ps("P2", [128, 1024], F32)

        with nc.Block() as bk_:
            BARS = [0]

            def barrier():
                BARS[0] += 5
                tgt = BARS[0]
                for en in ("tensor", "vector", "scalar", "sync", "gpsimd"):
                    def f(eng, en=en, tgt=tgt):
                        if en == "vector" and C.cnt.get("vo", 0) > 0:
                            eng.wait_ge(C.s("vo"), C.cnt["vo"])
                        eng.sem_inc(C.s("BAR"), 1)
                        eng.wait_ge(C.s("BAR"), tgt)
                    getattr(bk_, en)(f)

            def p0_sync(eng):
                eng.dma_start(out=ident[:], in_=ident_h.ap()).then_inc(C.s("cld"), 16)
                eng.dma_start(out=identf[:], in_=identf_h.ap()).then_inc(C.s("cld"), 16)
                eng.dma_start(out=ones[:], in_=ones_h.ap()).then_inc(C.s("cld"), 16)
                C.cnt["cld"] = C.cnt.get("cld", 0) + 48
                eng.wait_ge(C.s("cld"), C.cnt["cld"])

            def p0_gp(eng):
                for dst, src in ((bkv_s, bkv), (bmb_s, bmb), (bq_s, bq),
                                 (bg_s, bg), (gw_s, gw_t)):
                    eng.dma_start(out=dst[:], in_=src).then_inc(C.s("cldg"), 16)
                    C.cnt["cldg"] = C.cnt.get("cldg", 0) + 16
                eng.wait_ge(C.s("cldg"), C.cnt["cldg"])

            def p0_vec(ueng):
                eng = _OrderedEng(ueng, C, "vo")
                eng.memset(epsA[:], EPS_A)

            bk_.sync(p0_sync)
            bk_.gpsimd(p0_gp)
            bk_.vector(p0_vec)
            barrier()

            def emit_ffn_sweep(sw, x_src, x_dst, ln2_out, half):
                t0g = sw * 8
                nE = globals().get("NE_RUN", NE)
                sweep_ne = globals().get("NE_RUN_SW", None)
                if sweep_ne is not None:
                    nE = sweep_ne[globals().setdefault("_SW_IDX", 0) % len(sweep_ne)]
                    globals()["_SW_IDX"] = globals()["_SW_IDX"] + 1

                def prev_slot(e, fc):
                    if fc >= 2:
                        return e * NFC + fc - 2
                    if e == 0:
                        return None
                    return (e - 1) * NFC + (NFC - 1 if fc == 0 else NFC - 2)

                B = C.snap("xld", "sq", "st2", "tp", "hE", "rt1", "scE", "smE",
                           "g", "sil", "u", "wg0", "wg1", "wu0", "wu1", "wdld",
                           "d", "xst", "h2E", "h2d", "sq2", "st22", "tp2")
                M = {}

                def f_gp(eng):
                    for e in range(nE):
                        for fc in range(NFC):
                            sg = "wg0" if fc % 2 == 0 else "wg1"
                            su = "wu0" if fc % 2 == 0 else "wu1"
                            pv = prev_slot(e, fc)
                            if pv is not None:
                                eng.wait_ge(C.s("g"), B["g"] + pv + 1)
                            eng.dma_start(out=wgb[:, (fc % 2) * D:(fc % 2 + 1) * D],
                                          in_=wg_t[e, fc]).then_inc(C.s(sg), 16)
                            C.cnt[sg] += 16
                            if pv is not None:
                                eng.wait_ge(C.s("u"), B["u"] + pv + 1)
                            eng.dma_start(out=wub[:, (fc % 2) * D:(fc % 2 + 1) * D],
                                          in_=wu_t[e, fc]).then_inc(C.s(su), 16)
                            C.cnt[su] += 16
                        if e >= 1:
                            eng.wait_ge(C.s("d"), B["d"] + e * 16)
                        for fc in range(NFC):
                            eng.dma_start(out=wdb[:, fc * D:(fc + 1) * D],
                                          in_=wd_t[e, fc]).then_inc(C.s("wdld"), 16)
                            C.cnt["wdld"] += 16

                bk_.gpsimd(f_gp)

                def f_vec(ueng):
                    eng = _OrderedEng(ueng, C, "vo")
                    ueng.wait_ge(C.s("xld"), B["xld"] + 128)
                    for t in range(8):
                        eng.tensor_reduce(cols[:, 56 + t:57 + t],
                                          Xh[:, t * D:(t + 1) * D], AX.X, ALU.add)
                    ueng.wait_ge(C.s("sq"), B["sq"] + 8)
                    eng.tensor_scalar(cols[:, 32:40], cols[:, 56:64],
                                      1.0 / D, None, ALU.mult)
                    eng.tensor_scalar(cols[:, 0:8], cols[:, 16:24],
                                      1.0 / D, None, ALU.mult)
                    eng.tensor_tensor(cols[:, 48:56], cols[:, 32:40],
                                      cols[:, 32:40], ALU.mult)
                    eng.tensor_tensor(cols[:, 0:8], cols[:, 0:8],
                                      cols[:, 48:56], ALU.subtract)
                    eng.tensor_scalar(cols[:, 0:8], cols[:, 0:8],
                                      EPS_LN, None, ALU.add)
                    M["st1"] = C.vmark()
                    ueng.wait_ge(C.s("st2"), B["st2"] + 1)
                    eng.reciprocal(cols[:, 40:48], cols[:, 8:16])
                    for t in range(8):
                        if t >= 1:
                            ueng.wait_ge(C.s("tp"), B["tp"] + t * 4)
                        eng.tensor_scalar(xts[:], Xh[:, t * D:(t + 1) * D],
                                          cols[:, 32 + t:33 + t],
                                          cols[:, 40 + t:41 + t],
                                          ALU.subtract, ALU.mult)
                        M["ap%d" % t] = C.vmark()
                    for t in range(8):
                        ueng.wait_ge(C.s("scE"), B["scE"] + t + 1)
                        eng.tensor_reduce(cols[:, 24 + t:25 + t],
                                          sc_sb[:, t * 16:(t + 1) * 16],
                                          AX.X, ALU.max)
                        eng.tensor_scalar(cols[:, 24 + t:25 + t],
                                          cols[:, 24 + t:25 + t], -1.0, None,
                                          ALU.mult)
                        M["sm1%d" % t] = C.vmark()
                    for t in range(8):
                        sl = slice(t * 16, (t + 1) * 16)
                        ueng.wait_ge(C.s("smE"), B["smE"] + t + 1)
                        eng.tensor_reduce(cols[:, 24 + t:25 + t], esc[:, sl],
                                          AX.X, ALU.add)
                        eng.reciprocal(cols[:, 48 + t:49 + t],
                                       cols[:, 24 + t:25 + t])
                        eng.tensor_scalar(sc_sb[:, sl], esc[:, sl],
                                          cols[:, 48 + t:49 + t], None, ALU.mult)
                        eng.tensor_copy(wk_sc[:, sl], sc_sb[:, sl])
                        for r2 in range(TOPK):
                            eng.tensor_reduce(cols[:, 24 + t:25 + t],
                                              wk_sc[:, sl], AX.X, ALU.max)
                            if r2 < TOPK - 1:
                                eng.tensor_scalar(esc[:, sl], wk_sc[:, sl],
                                                  cols[:, 24 + t:25 + t], None,
                                                  ALU.is_ge)
                                eng.scalar_tensor_tensor(wk_sc[:, sl], esc[:, sl],
                                                         -BIG, wk_sc[:, sl],
                                                         ALU.mult, ALU.add)
                        eng.tensor_scalar(esc[:, sl], sc_sb[:, sl],
                                          cols[:, 24 + t:25 + t], None, ALU.is_ge)
                        eng.scalar_tensor_tensor(cw[:, sl], sc_sb[:, sl], half,
                                                 esc[:, sl], ALU.mult, ALU.mult)
                    M["cwd"] = C.vmark()
                    for e in range(nE):
                        for fc in range(NFC):
                            idx = e * NFC + fc
                            ueng.wait_ge(C.s("u"), B["u"] + idx + 1)
                            ueng.wait_ge(C.s("sil"), B["sil"] + idx + 1)
                            if fc == 0 and e >= 1:
                                ueng.wait_ge(C.s("d"), B["d"] + e * 16)
                            eng.tensor_tensor(inter[:, fc * S:(fc + 1) * S],
                                              gsilu[:], P2[:], ALU.mult)
                            M["itr%d" % idx] = C.vmark()
                        for t in range(8):
                            for dh in range(2):
                                gidx = t * 2 + dh
                                ueng.wait_ge(C.s("d"), B["d"] + e * 16 + gidx + 1)
                                pr = (P0, P1, P2)[gidx % 3]
                                eng.scalar_tensor_tensor(
                                    Xh[:, t * D + dh * 1024:t * D + (dh + 1) * 1024],
                                    pr[:], cw[:, t * 16 + e:t * 16 + e + 1],
                                    Xh[:, t * D + dh * 1024:t * D + (dh + 1) * 1024],
                                    ALU.mult, ALU.add)
                                M["xa%d_%d" % (e, gidx)] = C.vmark()
                    M["xa_all"] = C.vmark()
                    if ln2_out:
                        for t in range(8):
                            eng.tensor_reduce(cols[:, 56 + t:57 + t],
                                              Xh[:, t * D:(t + 1) * D], AX.X,
                                              ALU.add)
                        ueng.wait_ge(C.s("sq2"), B["sq2"] + 8)
                        eng.tensor_scalar(cols[:, 32:40], cols[:, 56:64],
                                          1.0 / D, None, ALU.mult)
                        eng.tensor_scalar(cols[:, 0:8], cols[:, 16:24],
                                          1.0 / D, None, ALU.mult)
                        eng.tensor_tensor(cols[:, 48:56], cols[:, 32:40],
                                          cols[:, 32:40], ALU.mult)
                        eng.tensor_tensor(cols[:, 0:8], cols[:, 0:8],
                                          cols[:, 48:56], ALU.subtract)
                        eng.tensor_scalar(cols[:, 0:8], cols[:, 0:8],
                                          EPS_LN, None, ALU.add)
                        M["st12"] = C.vmark()
                        ueng.wait_ge(C.s("st22"), B["st22"] + 1)
                        eng.reciprocal(cols[:, 40:48], cols[:, 8:16])
                        for t in range(8):
                            g2 = t0g + t
                            eng.tensor_copy(ln2_mr[:, 2 * g2:2 * g2 + 1],
                                            cols[:, 32 + t:33 + t])
                            eng.tensor_copy(ln2_mr[:, 2 * g2 + 1:2 * g2 + 2],
                                            cols[:, 40 + t:41 + t])
                            if t >= 1:
                                ueng.wait_ge(C.s("tp2"), B["tp2"] + t * 4)
                            eng.tensor_scalar(xts[:], Xh[:, t * D:(t + 1) * D],
                                              cols[:, 32 + t:33 + t],
                                              cols[:, 40 + t:41 + t],
                                              ALU.subtract, ALU.mult)
                            M["ap2%d" % t] = C.vmark()

                bk_.vector(f_vec)

                def f_scal(eng):
                    eng.wait_ge(C.s("xld"), B["xld"] + 128)
                    for t in range(8):
                        eng.activation(inter[:, 0:D], Xh[:, t * D:(t + 1) * D],
                                       AF.Square,
                                       accum_out=cols[:, 16 + t:17 + t]
                                       ).then_inc(C.s("sq"), 1)
                    C.cnt["sq"] += 8
                    eng.wait_ge(C.s("vo"), M["st1"])
                    eng.activation(cols[:, 8:16], cols[:, 0:8], AF.Sqrt
                                   ).then_inc(C.s("st2"), 1)
                    C.cnt["st2"] += 1
                    for t in range(8):
                        for gb in range(4):
                            eng.wait_ge(C.s("tp"), B["tp"] + t * 4 + gb + 1)
                            dst = Hf.ap().rearrange("p (k j) -> p k j", k=KD)[
                                :, gb * 4:(gb + 1) * 4, t * 128:(t + 1) * 128]
                            src2 = tpp[:, ((t * 4 + gb) % 2) * 1024:
                                       ((t * 4 + gb) % 2) * 1024 + 512].rearrange(
                                "p (k j) -> p k j", k=4)
                            eng.activation(dst, src2, AF.Copy).then_inc(C.s("hE"), 1)
                    C.cnt["hE"] += 32
                    for t in range(8):
                        eng.wait_ge(C.s("rt1"), B["rt1"] + t + 1)
                        ps_r = P0 if t % 2 == 0 else P1
                        eng.activation(sc_sb[:, t * 16:(t + 1) * 16],
                                       ps_r[:, 0:16], AF.Copy
                                       ).then_inc(C.s("scE"), 1)
                    C.cnt["scE"] += 8
                    for t in range(8):
                        eng.wait_ge(C.s("vo"), M["sm1%d" % t])
                        eng.activation(esc[:, t * 16:(t + 1) * 16],
                                       sc_sb[:, t * 16:(t + 1) * 16], AF.Exp,
                                       bias=cols[:, 24 + t:25 + t]
                                       ).then_inc(C.s("smE"), 1)
                    C.cnt["smE"] += 8
                    for e in range(nE):
                        for fc in range(NFC):
                            idx = e * NFC + fc
                            eng.wait_ge(C.s("g"), B["g"] + idx + 1)
                            if idx >= 1:
                                eng.wait_ge(C.s("vo"), M["itr%d" % (idx - 1)])
                            gp = P0 if fc % 2 == 0 else P1
                            eng.activation(gsilu[:], gp[:], AF.Silu
                                           ).then_inc(C.s("sil"), 1)
                            C.cnt["sil"] += 1
                    if ln2_out:
                        eng.wait_ge(C.s("vo"), M["xa_all"])
                        for t in range(8):
                            eng.activation(inter[:, 0:D], Xh[:, t * D:(t + 1) * D],
                                           AF.Square,
                                           accum_out=cols[:, 16 + t:17 + t]
                                           ).then_inc(C.s("sq2"), 1)
                        C.cnt["sq2"] += 8
                        eng.wait_ge(C.s("vo"), M["st12"])
                        eng.activation(cols[:, 8:16], cols[:, 0:8], AF.Sqrt
                                       ).then_inc(C.s("st22"), 1)
                        C.cnt["st22"] += 1
                        for t in range(8):
                            for gb in range(4):
                                eng.wait_ge(C.s("tp2"), B["tp2"] + t * 4 + gb + 1)
                                dst = Hf.ap().rearrange("p (k j) -> p k j", k=KD)[
                                    :, gb * 4:(gb + 1) * 4, t * 128:(t + 1) * 128]
                                src2 = tpp[:, ((t * 4 + gb) % 2) * 1024:
                                           ((t * 4 + gb) % 2) * 1024 + 512].rearrange(
                                    "p (k j) -> p k j", k=4)
                                eng.activation(dst, src2, AF.Copy
                                               ).then_inc(C.s("h2E"), 1)
                        C.cnt["h2E"] += 32

                bk_.scalar(f_scal)

                def f_tens(eng):
                    for t in range(8):
                        eng.wait_ge(C.s("vo"), M["ap%d" % t])
                        for gb in range(4):
                            bi = t * 4 + gb
                            if bi >= 2:
                                eng.wait_ge(C.s("hE"), B["hE"] + bi - 1)
                            for j in range(4):
                                k = gb * 4 + j
                                ins = eng.transpose(
                                    tpp[:, (bi % 2) * 1024 + j * 128:
                                        (bi % 2) * 1024 + (j + 1) * 128],
                                    xts[:, k * 128:(k + 1) * 128],
                                    ident[:])
                            ins.then_inc(C.s("tp"), 1)
                    C.cnt["tp"] += 32
                    eng.wait_ge(C.s("hE"), B["hE"] + 32)
                    for t in range(8):
                        ps_r = P0 if t % 2 == 0 else P1
                        if t >= 2:
                            eng.wait_ge(C.s("scE"), B["scE"] + t - 1)
                        for k in range(KD):
                            ins = eng.matmul(
                                ps_r[:, 0:16],
                                Hf[:, k * S + t * 128:k * S + (t + 1) * 128],
                                gw_s[:, k * 16:(k + 1) * 16],
                                start=(k == 0), stop=(k == KD - 1))
                        ins.then_inc(C.s("rt1"), 1)
                        C.cnt["rt1"] += 1
                    for e in range(nE):
                        if e >= 1:
                            eng.wait_ge(C.s("vo"), M["xa%d_15" % (e - 1)])
                        for fc in range(NFC):
                            idx = e * NFC + fc
                            gp = P0 if fc % 2 == 0 else P1
                            if e == 0 and fc <= 1:
                                eng.wait_ge(C.s("scE"), B["scE"] + 8)
                            pv = prev_slot(e, fc)
                            if pv is not None:
                                eng.wait_ge(C.s("sil"), B["sil"] + pv + 1)
                            sg = "wg0" if fc % 2 == 0 else "wg1"
                            ng = e * 6 + fc // 2 + 1 if fc % 2 == 0 \
                                else e * 5 + fc // 2 + 1
                            eng.wait_ge(C.s(sg), B[sg] + 16 * ng)
                            for nh2 in range(2):
                                for k in range(KD):
                                    ins = eng.matmul(
                                        gp[:, nh2 * 512:(nh2 + 1) * 512],
                                        wgb[:, (fc % 2) * D + k * 128:
                                            (fc % 2) * D + (k + 1) * 128],
                                        Hf[:, k * S + nh2 * 512:k * S + (nh2 + 1) * 512],
                                        start=(k == 0), stop=(k == KD - 1))
                            ins.then_inc(C.s("g"), 1)
                            C.cnt["g"] += 1
                            if idx >= 1:
                                eng.wait_ge(C.s("vo"), M["itr%d" % (idx - 1)])
                            su = "wu0" if fc % 2 == 0 else "wu1"
                            eng.wait_ge(C.s(su), B[su] + 16 * ng)
                            for nh2 in range(2):
                                for k in range(KD):
                                    ins = eng.matmul(
                                        P2[:, nh2 * 512:(nh2 + 1) * 512],
                                        wub[:, (fc % 2) * D + k * 128:
                                            (fc % 2) * D + (k + 1) * 128],
                                        Hf[:, k * S + nh2 * 512:k * S + (nh2 + 1) * 512],
                                        start=(k == 0), stop=(k == KD - 1))
                            ins.then_inc(C.s("u"), 1)
                            C.cnt["u"] += 1
                        eng.wait_ge(C.s("sil"), B["sil"] + e * NFC + NFC)
                        eng.wait_ge(C.s("vo"), M["itr%d" % (e * NFC + NFC - 1)])
                        eng.wait_ge(C.s("wdld"), B["wdld"] + 16 * (e + 1) * NFC)
                        for t in range(8):
                            for dh in range(2):
                                gidx = t * 2 + dh
                                pr = (P0, P1, P2)[gidx % 3]
                                if gidx >= 3:
                                    eng.wait_ge(C.s("vo"),
                                                M["xa%d_%d" % (e, gidx - 3)])
                                for nh2 in range(2):
                                    for fc in range(NFC):
                                        ins = eng.matmul(
                                            pr[:, nh2 * 512:(nh2 + 1) * 512],
                                            inter[:, fc * S + t * 128:
                                                  fc * S + (t + 1) * 128],
                                            wdb[:, fc * D + dh * 1024 + nh2 * 512:
                                                fc * D + dh * 1024 + (nh2 + 1) * 512],
                                            start=(fc == 0), stop=(fc == NFC - 1))
                                ins.then_inc(C.s("d"), 1)
                                C.cnt["d"] += 1
                    if ln2_out:
                        for t in range(8):
                            eng.wait_ge(C.s("vo"), M["ap2%d" % t])
                            for gb in range(4):
                                bi = t * 4 + gb
                                if bi >= 2:
                                    eng.wait_ge(C.s("h2E"), B["h2E"] + bi - 1)
                                for j in range(4):
                                    k = gb * 4 + j
                                    ins = eng.transpose(
                                        tpp[:, (bi % 2) * 1024 + j * 128:
                                            (bi % 2) * 1024 + (j + 1) * 128],
                                        xts[:, k * 128:(k + 1) * 128],
                                        ident[:])
                                ins.then_inc(C.s("tp2"), 1)
                        C.cnt["tp2"] += 32

                bk_.tensor(f_tens)

                def f_sync(eng):
                    for t in range(8):
                        eng.dma_start(
                            out=Xh[:, t * D:(t + 1) * D],
                            in_=x_src[(t0g + t) * 128:(t0g + t + 1) * 128, :],
                        ).then_inc(C.s("xld"), 16)
                    C.cnt["xld"] += 128
                    eng.wait_ge(C.s("vo"), M["xa_all"])
                    for t in range(8):
                        eng.dma_start(
                            out=x_dst[(t0g + t) * 128:(t0g + t + 1) * 128, :],
                            in_=Xh[:, t * D:(t + 1) * D],
                        ).then_inc(C.s("xst"), 16)
                    C.cnt["xst"] += 128
                    if ln2_out:
                        eng.wait_ge(C.s("h2E"), B["h2E"] + 32)
                        eng.dma_start(
                            out=h2_dram[:, :, t0g * 128:t0g * 128 + S]
                                .rearrange("k p j -> p k j"),
                            in_=Hf.ap().rearrange("p (k j) -> p k j", k=KD),
                        ).then_inc(C.s("h2d"), 16)
                        C.cnt["h2d"] += 16
                        eng.wait_ge(C.s("h2d"), C.cnt["h2d"])
                    eng.wait_ge(C.s("xst"), C.cnt["xst"])

                bk_.sync(f_sync)

            DCO = (0, 129, 258, 512, 641, 770)
            AWL_KG = [0] * 4

            def emit_attnA_unit(kg, blk):
                B = C.snap("hld", "awl", "kv1", "mb1", "nd1", "tpA", "dl1",
                           "kvE", "sqA", "sqt", "mbE", "dnE", "tpE", "dout",
                           "ccs")
                M = {}
                blkoff = blk * 512
                from concourse.bass import broadcast_tensor_aps as _bc

                def f_gp(eng):
                    if blk == 0:
                        wkv = wdb[:, 0:4096].rearrange("p (k c) -> p k c", k=KD)
                        eng.dma_start(
                            out=wkv[:, :, 0:128],
                            in_=wk_t[:, kg * 128:(kg + 1) * 128]
                                .rearrange("(k p) j -> p k j", p=128),
                        ).then_inc(C.s("awl"), 16)
                        eng.dma_start(
                            out=wkv[:, :, 128:256],
                            in_=wv_t[:, kg * 128:(kg + 1) * 128]
                                .rearrange("(k p) j -> p k j", p=128),
                        ).then_inc(C.s("awl"), 16)
                        eng.dma_start(
                            out=wdb[:, 4096:12288].rearrange("p (k c) -> p k c",
                                                             k=KD),
                            in_=wmb_t[:, kg * 512:(kg + 1) * 512]
                                .rearrange("(k p) j -> p k j", p=128),
                        ).then_inc(C.s("awl"), 16)
                        eng.dma_start(
                            out=wdb[:, 12288:15360].rearrange("p (c j) -> p c j",
                                                              c=NFD),
                            in_=mem4[kg].rearrange("c p j -> p c j"),
                        ).then_inc(C.s("awl"), 16)
                        eng.dma_start(
                            out=wdb[:, 15360:15384].rearrange("p (c j) -> p c j",
                                                              c=NFD),
                            in_=norm4[kg].rearrange("c p j -> p c j"),
                        ).then_inc(C.s("awl"), 16)
                        C.cnt["awl"] += 80
                        AWL_KG[kg] = C.cnt["awl"]
                    if blk == 3:
                        eng.wait_ge(C.s("vo"), M_DL[0])
                        eng.dma_start(out=cc_src[kg], in_=Xh[:, 8192:11288]
                                      ).then_inc(C.s("dout"), 16)
                        C.cnt["dout"] += 16
                        eng.wait_ge(C.s("dout"), C.cnt["dout"])
                        if True:
                            eng.collective_compute(
                                "AllReduce", ALU.add,
                                replica_groups=[list(range(NCORE))],
                                ins=[cc_src[kg].opt()], outs=[cc_dst[kg].opt()],
                            ).then_inc(C.s("ccs"), 1)
                        C.cnt["ccs"] += 1

                M_DL = [0]

                def f_vec(ueng):
                    eng = _OrderedEng(ueng, C, "vo")
                    xc4 = Xh[:, 1024:2048].rearrange("p (t c) -> p t c", t=4)
                    mk4 = Xh[:, 4864:7936].rearrange("p (t c) -> p t c", t=4)
                    ueng.wait_ge(C.s("kvE"), B["kvE"] + 1)
                    for c in range(NFD):
                        jr = c // 2 + 1
                        hh2 = c % 2
                        bb = hh2 * 128
                        eng.tensor_tensor(
                            mk4[:, :, c * 128 + jr:(c + 1) * 128],
                            xc4[:, :, bb + jr:bb + 128],
                            xc4[:, :, bb:bb + 128 - jr], ALU.mult)
                        wsrc = (256 if hh2 == 0 else 128) - jr
                        eng.tensor_tensor(
                            mk4[:, :, c * 128:c * 128 + jr],
                            xc4[:, :, bb:bb + jr],
                            xc4[:, :, wsrc:wsrc + jr], ALU.mult)
                    M["dp"] = C.vmark()
                    ueng.wait_ge(C.s("sqA"), B["sqA"] + 1)
                    eng.tensor_copy(cols[:, 0:4], cols[:, 16:20])
                    M["bn"] = C.vmark()
                    ueng.wait_ge(C.s("sqt"), B["sqt"] + 1)
                    eng.tensor_scalar(cols[:, 24:28], cols[:, 8:12], 1e-12,
                                      None, ALU.max)
                    eng.reciprocal(cols[:, 32:36], cols[:, 24:28])
                    for t in range(4):
                        eng.tensor_scalar(inter[:, t * 768:(t + 1) * 768],
                                          Xh[:, 4864 + t * 768:4864 + (t + 1) * 768],
                                          cols[:, 32 + t:33 + t], None, ALU.mult)
                    M["mkn"] = C.vmark()
                    eng.tensor_tensor(cols[:, 40:44], cols[:, 8:12],
                                      cols[:, 32:36], ALU.mult)
                    eng.tensor_tensor(cols[:, 48:52], cols[:, 40:44],
                                      cols[:, 40:44], ALU.mult)
                    eng.reciprocal(cols[:, 56:60], cols[:, 48:52])
                    ueng.wait_ge(C.s("dnE"), B["dnE"] + 1)
                    eng.reciprocal(Xh[:, 528:544], Xh[:, 512:528])
                    d3 = Xh[:, 512:528].rearrange("p (t h) -> p t h", t=4)
                    r3 = cols[:, 56:60].rearrange("p (t h) -> p t h", t=4)
                    c3 = Xh[:, 544:560].rearrange("p (t h) -> p t h", t=4)
                    _, r3b = _bc(d3, r3)
                    eng.tensor_tensor(c3, d3, r3b, ALU.mult)
                    eng.tensor_scalar(Xh[:, 544:560], Xh[:, 544:560],
                                      -1.0, 1.0, ALU.mult, ALU.add)
                    eng.tensor_scalar(Xh[:, 544:560], Xh[:, 544:560],
                                      0.0, 1.0, ALU.max, ALU.min)
                    ueng.wait_ge(C.s("mbE"), B["mbE"] + 2)
                    for t in range(4):
                        ueng.wait_ge(C.s("nd1"), B["nd1"] + t + 1)
                        nps = (P0 if t < 2 else P1)
                        nsl = nps[:, (t % 2) * 512:(t % 2) * 512 + 512]
                        n3 = nsl.rearrange("p (h j) -> p h j", h=4)
                        rc3 = Xh[:, 528 + t * 4:528 + (t + 1) * 4].rearrange(
                            "p (h j) -> p h j", h=4)
                        _, rc3b = _bc(n3, rc3)
                        pv3 = Xh[:, 4352:4864].rearrange("p (h j) -> p h j", h=4)
                        eng.tensor_tensor(pv3, n3, rc3b, ALU.mult)
                        v3 = Xh[:, t * 128:(t + 1) * 128].rearrange(
                            "p (h j) -> p h j", h=1)
                        mv3 = Xh[:, 11288:11800].rearrange("p (h j) -> p h j", h=4)
                        _, v3b = _bc(pv3, v3)
                        eng.tensor_tensor(mv3, v3b, pv3, ALU.subtract)
                        ws = 6144 + t * 4 * 129
                        wm3 = inter.ap()[:, ws:ws + 4 * 129].rearrange(
                            "p (h j) -> p h j", j=129)[:, :, 0:128]
                        eng.tensor_tensor(wm3, mv3,
                                          Xh[:, 2048 + t * 512:2048 + (t + 1) * 512]
                                          .rearrange("p (h j) -> p h j", h=4),
                                          ALU.mult)
                        cf3 = inter.ap()[:, ws:ws + 4 * 129].rearrange(
                            "p (h j) -> p h j", j=129)[:, :, 128:129]
                        eng.tensor_copy(cf3,
                                        Xh[:, 544 + t * 4:544 + (t + 1) * 4]
                                        .rearrange("p (h j) -> p h j", h=4))
                        M["pv%d" % t] = C.vmark()
                    M["wmv"] = C.vmark()
                    for hh in range(4):
                        ueng.wait_ge(C.s("dl1"), B["dl1"] + hh + 1)
                        ds = 8192 + hh * 774
                        if blk == 0:
                            eng.tensor_copy(Xh[:, ds:ds + 387], P2[:, 0:387])
                            eng.tensor_copy(Xh[:, ds + 387:ds + 774],
                                            P2[:, 512:899])
                        else:
                            eng.tensor_tensor(Xh[:, ds:ds + 387],
                                              Xh[:, ds:ds + 387],
                                              P2[:, 0:387], ALU.add)
                            eng.tensor_tensor(Xh[:, ds + 387:ds + 774],
                                              Xh[:, ds + 387:ds + 774],
                                              P2[:, 512:899], ALU.add)
                        M["dlE%d" % hh] = C.vmark()
                    M_DL[0] = C.vmark()

                bk_.vector(f_vec)

                def f_scal(eng):
                    eng.wait_ge(C.s("kv1"), B["kv1"] + 1)
                    p03 = P0.ap().rearrange("p (t c) -> p t c", t=4)
                    xc4 = Xh[:, 1024:2048].rearrange("p (t c) -> p t c", t=4)
                    v4 = Xh[:, 0:512].rearrange("p (t c) -> p t c", t=4)
                    eng.activation(xc4[:, :, 0:128], p03[:, :, 0:128], AF.Relu)
                    eng.activation(xc4[:, :, 128:256], p03[:, :, 0:128],
                                   AF.Relu, scale=-1.0)
                    eng.activation(v4, p03[:, :, 128:256], AF.Copy
                                   ).then_inc(C.s("kvE"), 1)
                    C.cnt["kvE"] += 1
                    eng.wait_ge(C.s("vo"), M["dp"])
                    for t in range(4):
                        ins = eng.activation(gsilu[:, 0:768],
                                             Xh[:, 4864 + t * 768:4864 + (t + 1) * 768],
                                             AF.Square,
                                             accum_out=cols[:, 16 + t:17 + t])
                    ins.then_inc(C.s("sqA"), 1)
                    C.cnt["sqA"] += 1
                    eng.wait_ge(C.s("vo"), M["bn"])
                    eng.activation(cols[:, 8:12], cols[:, 0:4], AF.Sqrt
                                   ).then_inc(C.s("sqt"), 1)
                    C.cnt["sqt"] += 1
                    eng.wait_ge(C.s("mb1"), B["mb1"] + 1)
                    mb01 = Xh[:, 2048:3072].rearrange("p (t c) -> p t c", t=2)
                    mb23 = Xh[:, 3072:4096].rearrange("p (t c) -> p t c", t=2)
                    eng.activation(mb01, P1.ap().rearrange("p (t c) -> p t c", t=2),
                                   AF.Sigmoid).then_inc(C.s("mbE"), 1)
                    eng.activation(mb23, P2.ap().rearrange("p (t c) -> p t c", t=2),
                                   AF.Sigmoid).then_inc(C.s("mbE"), 1)
                    C.cnt["mbE"] += 2
                    for t in range(4):
                        par = t % 2
                        eng.wait_ge(C.s("tpA"), B["tpA"] + t + 1)
                        dstT = inter[:, 3072:6144].rearrange(
                            "p (c j) -> p c j", c=NFD)[:, :, t * 128:(t + 1) * 128]
                        srcT = tpp[:, par * 1024:par * 1024 + 768].rearrange(
                            "p (c j) -> p c j", c=NFD)
                        eng.activation(dstT, srcT, AF.Copy).then_inc(C.s("tpE"), 1)
                        C.cnt["tpE"] += 1
                    eng.wait_ge(C.s("nd1"), B["nd1"] + 4)
                    eng.activation(Xh[:, 512:528], P2[:, 0:16], AF.Identity,
                                   bias=epsA[:]).then_inc(C.s("dnE"), 1)
                    C.cnt["dnE"] += 1

                bk_.scalar(f_scal)

                def f_tens(eng):
                    eng.wait_ge(C.s("hld"), B["hld"] + 16)
                    eng.wait_ge(C.s("awl"), AWL_KG[kg])
                    for t in range(4):
                        for k in range(KD):
                            eng.matmul(P0[:, t * 256:(t + 1) * 256],
                                       Hf[:, k * S + t * 128:k * S + (t + 1) * 128],
                                       wdb[:, k * 256:(k + 1) * 256],
                                       start=(k == 0), stop=False)
                        ins = eng.matmul(P0[:, t * 256:(t + 1) * 256],
                                         ones[0:1, 0:128],
                                         bkv_s[0:1, kg * 256:(kg + 1) * 256],
                                         start=False, stop=True)
                    ins.then_inc(C.s("kv1"), 1)
                    C.cnt["kv1"] += 1
                    for t in range(4):
                        pm = P1 if t < 2 else P2
                        msl = pm[:, (t % 2) * 512:(t % 2) * 512 + 512]
                        for k in range(KD):
                            eng.matmul(msl,
                                       Hf[:, k * S + t * 128:k * S + (t + 1) * 128],
                                       wdb[:, 4096 + k * 512:4096 + (k + 1) * 512],
                                       start=(k == 0), stop=False)
                        ins = eng.matmul(msl, ones[0:1, 0:128],
                                         bmb_s[0:1, kg * 512:(kg + 1) * 512],
                                         start=False, stop=True)
                    ins.then_inc(C.s("mb1"), 1)
                    C.cnt["mb1"] += 1
                    eng.wait_ge(C.s("vo"), M["mkn"])
                    for t in range(4):
                        par = t % 2
                        if t >= 2:
                            eng.wait_ge(C.s("tpE"), B["tpE"] + t - 1)
                        for c in range(NFD):
                            ins = eng.transpose(
                                tpp[:, par * 1024 + c * 128:
                                    par * 1024 + (c + 1) * 128],
                                inter[:, t * 768 + c * 128:t * 768 + (c + 1) * 128],
                                ident[:])
                        ins.then_inc(C.s("tpA"), 1)
                        C.cnt["tpA"] += 1
                    eng.wait_ge(C.s("kvE"), B["kvE"] + 1)
                    eng.wait_ge(C.s("mbE"), B["mbE"] + 2)
                    for t in range(4):
                        eng.wait_ge(C.s("tpE"), B["tpE"] + t + 1)
                        nps = (P0 if t < 2 else P1)
                        nsl = nps[:, (t % 2) * 512:(t % 2) * 512 + 512]
                        for c in range(NFD):
                            lh = inter[:, 3072 + c * 512 + t * 128:
                                       3072 + c * 512 + (t + 1) * 128]
                            eng.matmul(nsl, lh,
                                       wdb[:, 12288 + c * 512:12288 + (c + 1) * 512],
                                       start=(c == 0), stop=(c == NFD - 1))
                            ins = eng.matmul(P2[:, t * 4:(t + 1) * 4], lh,
                                             wdb[:, 15360 + c * 4:15360 + (c + 1) * 4],
                                             start=(c == 0), stop=(c == NFD - 1))
                        ins.then_inc(C.s("nd1"), 1)
                        C.cnt["nd1"] += 1
                    eng.wait_ge(C.s("vo"), M["wmv"])
                    for hh in range(4):
                        if hh >= 1:
                            eng.wait_ge(C.s("vo"), M["dlE%d" % (hh - 1)])
                        elif blk >= 1:
                            eng.wait_ge(C.s("vo"), B_DL_PREV[0])
                        for c in range(NFD):
                            for t in range(4):
                                ws = 6144 + (t * 4 + hh) * 129
                                ins = eng.matmul(
                                    P2[:, DCO[c]:DCO[c] + 129],
                                    inter[:, t * 768 + c * 128:
                                          t * 768 + (c + 1) * 128],
                                    inter[:, ws:ws + 129],
                                    start=(t == 0), stop=(t == 3))
                        ins.then_inc(C.s("dl1"), 1)
                        C.cnt["dl1"] += 1

                bk_.tensor(f_tens)

                def f_sync(eng):
                    eng.dma_start(
                        out=Hf.ap().rearrange("p (k j) -> p k j", k=KD)[:, :, 0:512],
                        in_=h2_dram[:, :, blkoff:blkoff + 512]
                            .rearrange("k p j -> p k j"),
                    ).then_inc(C.s("hld"), 16)
                    C.cnt["hld"] += 16

                bk_.sync(f_sync)
                bk_.gpsimd(f_gp)
                B_DL_PREV[0] = M_DL[0]

            B_DL_PREV = [0]
            AUG_LAST = [0]

            def emit_aug_kg(kg):
                B = C.snap("agl", "ags", "ccs")
                M = {}

                def f_gp(eng):
                    eng.wait_ge(C.s("ccs"), kg + 1)
                    eng.dma_start(
                        out=Xh[:, 0:3096].rearrange("p (h c j) -> p h c j",
                                                    h=4, c=NFD),
                        in_=maug[4 * kg:4 * kg + 4]
                            .rearrange("h c p j -> p h c j"),
                    ).then_inc(C.s("agl"), 16)
                    eng.dma_start(out=Xh[:, 4096:7192], in_=cc_dst[kg]
                                  ).then_inc(C.s("agl"), 16)
                    C.cnt["agl"] += 32
                    eng.wait_ge(C.s("vo"), M["aug"])
                    eng.dma_start(
                        out=aug_dram[4 * kg:4 * kg + 4]
                            .rearrange("h p j -> p h j"),
                        in_=wdb[:, 16384:19480].rearrange("p (h j) -> p h j", h=4),
                    ).then_inc(C.s("ags"), 16)
                    C.cnt["ags"] += 16
                    eng.wait_ge(C.s("ags"), C.cnt["ags"])

                def f_vec(ueng):
                    eng = _OrderedEng(ueng, C, "vo")
                    ueng.wait_ge(C.s("agl"), B["agl"] + 32)
                    eng.tensor_tensor(wdb[:, 16384:19480], Xh[:, 0:3096],
                                      Xh[:, 4096:7192], ALU.add)
                    M["aug"] = C.vmark()
                    AUG_LAST[0] = M["aug"]

                bk_.vector(f_vec)
                bk_.gpsimd(f_gp)

            BWL_KG = [0]

            def emit_attnB_unit(blk, kg):
                B = C.snap("hldB", "xldB", "bwl", "q1", "g1", "tpAB", "na1",
                           "qE", "gE", "tpEB", "sqB", "sqtB", "dnB", "xstB")
                M = {}
                blkoff = blk * 512
                from concourse.bass import broadcast_tensor_aps as _bc
                NT_ = (P1, P1, P0, P0, P2, P2)

                def nslot(n):
                    tn = NT_[n // 3]
                    off = ((n // 3) % 2) * 512 + (n % 3) * 129
                    return tn, off

                def f_gp(eng):
                    eng.dma_start(
                        out=wdb[:, 0:8192].rearrange("p (k c) -> p k c", k=KD),
                        in_=wq_t[:, kg * 512:(kg + 1) * 512]
                            .rearrange("(k p) j -> p k j", p=128),
                    ).then_inc(C.s("bwl"), 16)
                    eng.dma_start(
                        out=wdb[:, 8192:16384].rearrange("p (k c) -> p k c", k=KD),
                        in_=wgt_t[:, kg * 512:(kg + 1) * 512]
                            .rearrange("(k p) j -> p k j", p=128),
                    ).then_inc(C.s("bwl"), 16)
                    eng.wait_ge(C.s("ags"), 16 * (kg + 1))
                    eng.dma_start(
                        out=inter[:, 3840:6936].rearrange("p (h j) -> p h j", h=4),
                        in_=aug_dram[4 * kg:4 * kg + 4]
                            .rearrange("h p j -> p h j"),
                    ).then_inc(C.s("bwl"), 16)
                    C.cnt["bwl"] += 48

                def f_vec(ueng):
                    eng = _OrderedEng(ueng, C, "vo")
                    if kg == 0:
                        ueng.wait_ge(C.s("xldB"), B["xldB"] + 64)
                        for t in range(4):
                            g2 = blk * 4 + t
                            eng.tensor_scalar(cols[:, 0 + t:1 + t],
                                              ln2_mr[:, 2 * g2 + 1:2 * g2 + 2],
                                              1.0, None, ALU.add)
                            eng.tensor_tensor(cols[:, 8 + t:9 + t],
                                              ln2_mr[:, 2 * g2:2 * g2 + 1],
                                              ln2_mr[:, 2 * g2 + 1:2 * g2 + 2],
                                              ALU.mult)
                            eng.tensor_scalar(Xh[:, t * D:(t + 1) * D],
                                              Xh[:, t * D:(t + 1) * D],
                                              cols[:, 0 + t:1 + t],
                                              cols[:, 8 + t:9 + t],
                                              ALU.mult, ALU.subtract)
                        M["lnB"] = C.vmark()
                    for t in range(4):
                        qb = 8192 + t * 512
                        ueng.wait_ge(C.s("qE"), B["qE"] + 2)
                        if t >= 1:
                            ueng.wait_ge(C.s("tpAB"), B["tpAB"] + 4 * t)
                        xq = Xh[:, 12288:13312].rearrange("p (h c) -> p h c", h=4)
                        q4v = Xh[:, qb:qb + 512].rearrange("p (h c) -> p h c", h=4)
                        mq4v = Xh[:, 13312:16384].rearrange("p (h c) -> p h c", h=4)
                        eng.tensor_scalar(xq[:, :, 0:128], q4v, 0.0, None, ALU.max)
                        eng.tensor_scalar(xq[:, :, 128:256], q4v, -1.0, 0.0,
                                          ALU.mult, ALU.max)
                        for c in range(NFD):
                            jr = c // 2 + 1
                            hh2 = c % 2
                            bb = hh2 * 128
                            eng.tensor_tensor(
                                mq4v[:, :, c * 128 + jr:(c + 1) * 128],
                                xq[:, :, bb + jr:bb + 128],
                                xq[:, :, bb:bb + 128 - jr], ALU.mult)
                            wsrc = (256 if hh2 == 0 else 128) - jr
                            eng.tensor_tensor(
                                mq4v[:, :, c * 128:c * 128 + jr],
                                xq[:, :, bb:bb + jr],
                                xq[:, :, wsrc:wsrc + jr], ALU.mult)
                        M["mq%d" % t] = C.vmark()
                        ueng.wait_ge(C.s("sqB"), B["sqB"] + t + 1)
                        eng.tensor_copy(cols[:, 0:4], cols[:, 16:20])
                        M["bnB%d" % t] = C.vmark()
                        ueng.wait_ge(C.s("sqtB"), B["sqtB"] + t + 1)
                        eng.tensor_scalar(cols[:, 40:44], cols[:, 8:12], 1e-12,
                                          None, ALU.max)
                        eng.reciprocal(cols[:, 48:52], cols[:, 40:44])
                        for hh in range(4):
                            eng.tensor_scalar(
                                inter[:, hh * 768:(hh + 1) * 768],
                                Xh[:, 13312 + hh * 768:13312 + (hh + 1) * 768],
                                cols[:, 48 + hh:49 + hh], None, ALU.mult)
                        M["mqn%d" % t] = C.vmark()
                    ueng.wait_ge(C.s("dnB"), B["dnB"] + 6)
                    eng.reciprocal(Xh[:, 15424:15440], Xh[:, 15360:15376])
                    for grp in range(6):
                        nn = 3 if grp < 5 else 1
                        tn = NT_[grp]
                        off = (grp % 2) * 512
                        n3 = tn.ap()[:, off:off + nn * 129].rearrange(
                            "p (s j) -> p s j", j=129)[:, :, 0:128]
                        rc3 = Xh[:, 15424 + grp * 3:15424 + grp * 3 + nn].rearrange(
                            "p (s j) -> p s j", s=nn)
                        _, rc3b = _bc(n3, rc3)
                        mh3 = Xh[:, 13312 + grp * 384:13312 + grp * 384 + nn * 128]\
                            .rearrange("p (s j) -> p s j", s=nn)
                        eng.tensor_tensor(mh3, n3, rc3b, ALU.mult)
                    M["mh"] = C.vmark()
                    ueng.wait_ge(C.s("gE"), B["gE"] + 2)
                    for t in range(4):
                        gm = Xh[:, 12288:12800]
                        eng.tensor_tensor(gm,
                                          Xh[:, 10240 + t * 512:10240 + (t + 1) * 512],
                                          Xh[:, 13312 + t * 512:13312 + (t + 1) * 512],
                                          ALU.mult)
                        xsl = Xh[:, t * D + kg * 512:t * D + (kg + 1) * 512]
                        eng.tensor_tensor(xsl, xsl, gm, ALU.add)
                    M["updF"] = C.vmark()

                bk_.vector(f_vec)

                def f_scal(eng):
                    eng.wait_ge(C.s("q1"), B["q1"] + 1)
                    q01 = Xh[:, 8192:9216].rearrange("p (t c) -> p t c", t=2)
                    q23 = Xh[:, 9216:10240].rearrange("p (t c) -> p t c", t=2)
                    eng.activation(q01, P0.ap().rearrange("p (t c) -> p t c", t=2),
                                   AF.Copy).then_inc(C.s("qE"), 1)
                    eng.activation(q23, P2.ap().rearrange("p (t c) -> p t c", t=2),
                                   AF.Copy).then_inc(C.s("qE"), 1)
                    C.cnt["qE"] += 2
                    eng.wait_ge(C.s("g1"), B["g1"] + 1)
                    g01 = Xh[:, 10240:11264].rearrange("p (t c) -> p t c", t=2)
                    g23 = Xh[:, 11264:12288].rearrange("p (t c) -> p t c", t=2)
                    eng.activation(g01, P0.ap().rearrange("p (t c) -> p t c", t=2),
                                   AF.Sigmoid).then_inc(C.s("gE"), 1)
                    eng.activation(g23, P2.ap().rearrange("p (t c) -> p t c", t=2),
                                   AF.Sigmoid).then_inc(C.s("gE"), 1)
                    C.cnt["gE"] += 2
                    for t in range(4):
                        eng.wait_ge(C.s("vo"), M["mq%d" % t])
                        for hh in range(4):
                            ins = eng.activation(gsilu[:, 0:768],
                                                 Xh[:, 13312 + hh * 768:
                                                    13312 + (hh + 1) * 768],
                                                 AF.Square,
                                                 accum_out=cols[:, 16 + hh:17 + hh])
                        ins.then_inc(C.s("sqB"), 1)
                        C.cnt["sqB"] += 1
                        eng.wait_ge(C.s("vo"), M["bnB%d" % t])
                        eng.activation(cols[:, 8:12], cols[:, 0:4], AF.Sqrt
                                       ).then_inc(C.s("sqtB"), 1)
                        C.cnt["sqtB"] += 1
                        for hh in range(4):
                            n = t * 4 + hh
                            eng.wait_ge(C.s("tpAB"), B["tpAB"] + n + 1)
                            if n >= 2:
                                eng.wait_ge(C.s("na1"), B["na1"] + n - 1)
                            eng.activation(inter[:, 6936 + (n % 2) * 768:
                                                 6936 + (n % 2) * 768 + 768],
                                           tpp[:, (n % 2) * 1024:
                                               (n % 2) * 1024 + 768],
                                           AF.Copy).then_inc(C.s("tpEB"), 1)
                            C.cnt["tpEB"] += 1
                    eng.wait_ge(C.s("na1"), B["na1"] + 16)
                    for grp in range(6):
                        nn = 3 if grp < 5 else 1
                        tn = NT_[grp]
                        off = (grp % 2) * 512
                        src3 = tn.ap()[:, off:off + nn * 129].rearrange(
                            "p (s j) -> p s j", j=129)[:, :, 128:129]
                        eng.activation(Xh[:, 15360 + grp * 3:15360 + grp * 3 + nn]
                                       .rearrange("p (s j) -> p s j", s=nn),
                                       src3, AF.Identity, bias=epsA[:]
                                       ).then_inc(C.s("dnB"), 1)
                        C.cnt["dnB"] += 1

                bk_.scalar(f_scal)

                def f_tens(eng):
                    eng.wait_ge(C.s("hldB"), B["hldB"] + (16 if kg == 0 else 0))
                    eng.wait_ge(C.s("bwl"), B["bwl"] + 48)
                    if kg == 0:
                        eng.wait_ge(C.s("vo"), M["lnB"])
                    for t in range(4):
                        pq = P0 if t < 2 else P2
                        qs = pq[:, (t % 2) * 512:(t % 2) * 512 + 512]
                        for k in range(KD):
                            eng.matmul(qs,
                                       Hf[:, k * S + t * 128:k * S + (t + 1) * 128],
                                       wdb[:, k * 512:(k + 1) * 512],
                                       start=(k == 0), stop=False)
                        ins = eng.matmul(qs, ones[0:1, 0:128],
                                         bq_s[0:1, kg * 512:(kg + 1) * 512],
                                         start=False, stop=True)
                    ins.then_inc(C.s("q1"), 1)
                    C.cnt["q1"] += 1
                    eng.wait_ge(C.s("qE"), B["qE"] + 2)
                    for t in range(4):
                        pq = P0 if t < 2 else P2
                        qs = pq[:, (t % 2) * 512:(t % 2) * 512 + 512]
                        for k in range(KD):
                            eng.matmul(qs,
                                       Hf[:, k * S + t * 128:k * S + (t + 1) * 128],
                                       wdb[:, 8192 + k * 512:8192 + (k + 1) * 512],
                                       start=(k == 0), stop=False)
                        ins = eng.matmul(qs, ones[0:1, 0:128],
                                         bg_s[0:1, kg * 512:(kg + 1) * 512],
                                         start=False, stop=True)
                    ins.then_inc(C.s("g1"), 1)
                    C.cnt["g1"] += 1
                    eng.wait_ge(C.s("gE"), B["gE"] + 2)
                    for t in range(4):
                        eng.wait_ge(C.s("vo"), M["mqn%d" % t])
                        for hh in range(4):
                            n = t * 4 + hh
                            if n >= 2:
                                eng.wait_ge(C.s("tpEB"), B["tpEB"] + n - 1)
                            for c in range(NFD):
                                ins = eng.transpose(
                                    tpp[:, (n % 2) * 1024 + c * 128:
                                        (n % 2) * 1024 + (c + 1) * 128],
                                    inter[:, hh * 768 + c * 128:
                                          hh * 768 + (c + 1) * 128],
                                    ident[:])
                            ins.then_inc(C.s("tpAB"), 1)
                            C.cnt["tpAB"] += 1
                            eng.wait_ge(C.s("tpEB"), B["tpEB"] + n + 1)
                            tn, off = nslot(n)
                            mq0 = 6936 + (n % 2) * 768
                            for c in range(NFD):
                                ins = eng.matmul(
                                    tn[:, off:off + 129],
                                    inter[:, mq0 + c * 128:mq0 + (c + 1) * 128],
                                    inter[:, 3840 + hh * 774 + c * 129:
                                          3840 + hh * 774 + (c + 1) * 129],
                                    start=(c == 0), stop=(c == NFD - 1))
                            ins.then_inc(C.s("na1"), 1)
                            C.cnt["na1"] += 1

                bk_.tensor(f_tens)

                def f_sync(eng):
                    if kg == 0:
                        if blk == 0:
                            eng.wait_ge(C.s("vo"), AUG_LAST[0])
                        for t in range(4):
                            eng.dma_start(
                                out=Xh[:, t * D:(t + 1) * D],
                                in_=x_dram[(blk * 4 + t) * 128:
                                           (blk * 4 + t + 1) * 128, :],
                            ).then_inc(C.s("xldB"), 16)
                        C.cnt["xldB"] += 64
                        eng.dma_start(
                            out=Hf.ap().rearrange("p (k j) -> p k j", k=KD)[
                                :, :, 0:512],
                            in_=h2_dram[:, :, blkoff:blkoff + 512]
                                .rearrange("k p j -> p k j"),
                        ).then_inc(C.s("hldB"), 16)
                        C.cnt["hldB"] += 16
                    if kg == 3:
                        eng.wait_ge(C.s("vo"), M["updF"])
                        for t in range(4):
                            eng.dma_start(
                                out=x_dram[(blk * 4 + t) * 128:
                                           (blk * 4 + t + 1) * 128, :],
                                in_=Xh[:, t * D:(t + 1) * D],
                            ).then_inc(C.s("xstB"), 16)
                        C.cnt["xstB"] += 64
                        eng.wait_ge(C.s("xstB"), C.cnt["xstB"])

                bk_.sync(f_sync)
                bk_.gpsimd(f_gp)

            ln2_on = not globals().get("LN2_OFF", False)
            if "ffn1" in phases:
                emit_ffn_sweep(0, x_in, x_dram, ln2_on, 0.5)
                barrier()
                emit_ffn_sweep(1, x_in, x_dram, ln2_on, 0.5)
                barrier()
            if "attnA" in phases:
                for kg_ in range(4):
                    for blk_ in range(4):
                        emit_attnA_unit(kg_, blk_)
                        barrier()
            if "aug" in phases:
                for kg_ in range(4):
                    emit_aug_kg(kg_)
            if "attnB" in phases:
                for blk_ in range(4):
                    for kg_ in range(4):
                        emit_attnB_unit(blk_, kg_)
                        barrier()
            if "dev_i" in phases:
                inter_d = nc.dram_tensor("inter_d", [128, NFC * S], BF16,
                                         kind="ExternalOutput").ap()
                cw_d = nc.dram_tensor("cw_d", [128, 128], F32,
                                      kind="ExternalOutput").ap()
                gsilu_d = nc.dram_tensor("gsilu_d", [128, S], F32,
                                         kind="ExternalOutput").ap()
                cols_d = nc.dram_tensor("cols_d", [128, 64], F32,
                                        kind="ExternalOutput").ap()
                hf_d = nc.dram_tensor("hf_d", [128, KD * S], BF16,
                                      kind="ExternalOutput").ap()

                sc_d = nc.dram_tensor("sc_d", [128, 128], F32,
                                      kind="ExternalOutput").ap()
                esc_d = nc.dram_tensor("esc_d", [128, 128], F32,
                                       kind="ExternalOutput").ap()

                gw_d = nc.dram_tensor("gw_d", [128, KD * 16], BF16,
                                      kind="ExternalOutput").ap()

                escsn_d = nc.dram_tensor("escsn_d", [128, 128], F32,
                                         kind="ExternalOutput").ap()
                scsn_d = nc.dram_tensor("scsn_d", [128, 128], F32,
                                        kind="ExternalOutput").ap()
                colsn_d = nc.dram_tensor("colsn_d", [128, 32], F32,
                                         kind="ExternalOutput").ap()

                def f_dump(eng):
                    for dst, srcb in ((inter_d, inter), (cw_d, cw),
                                      (gsilu_d, gsilu), (cols_d, cols),
                                      (hf_d, Hf), (sc_d, sc_sb), (esc_d, esc),
                                      (gw_d, gw_s), (escsn_d, escsn),
                                      (scsn_d, scsn), (colsn_d, colsn)):
                        eng.dma_start(out=dst, in_=srcb[:]).then_inc(C.s("dvd"), 16)
                        C.cnt["dvd"] = C.cnt.get("dvd", 0) + 16
                    eng.wait_ge(C.s("dvd"), C.cnt["dvd"])
                bk_.sync(f_dump)
                barrier()
            if "ffn2" in phases:
                src = x_dram if ("attnB" in phases or "ffn1" in phases) else x_in
                emit_ffn_sweep(0, src, y_out, False, 0.5)
                barrier()
                emit_ffn_sweep(1, src, y_out, False, 0.5)
                barrier()

    return nc


def kernel(**inputs):
    maps = _prep(inputs)
    nc = build_program()
    res = run_bass_kernel_spmd(nc, maps, core_ids=list(range(NCORE)))
    out = np.concatenate([res.results[c]["y_out"] for c in range(NCORE)], axis=0)
    return out.reshape(4, 4096, 2048)


# version 05
# speedup vs baseline: 1.0821x; 1.0261x over previous
import numpy as np
import ml_dtypes
import concourse.bass as bass
import concourse.mybir as mybir
from concourse.bass_utils import run_bass_kernel_spmd

F32 = mybir.dt.float32
BF16 = mybir.dt.bfloat16
AF = mybir.ActivationFunctionType
ALU = mybir.AluOpType
AX = mybir.AxisListType

NCORE = 8
TOK = 2048
NT = 16
D = 2048
KD = 16
MOE_D = 1408
NFC = 11
NE = 16
TOPK = 6
S = 1024
HKV = 4
NH = 16
FD = 768
NFD = 6
EPS_A = 1e-8
EPS_LN = 1e-5
BIG = 1.0e30


def _prep(i):
    f32 = np.float32

    def c(a):
        return np.ascontiguousarray(a, dtype=f32)

    wg = np.asarray(i["L0_ffn_w_gate"], f32)
    wu = np.asarray(i["L0_ffn_w_up"], f32)
    wd = np.asarray(i["L0_ffn_w_down"], f32)
    gw = np.asarray(i["L0_ffn_gate_w"], f32)
    wg_t = c(wg.reshape(NE, NFC, 128, KD, 128).transpose(0, 1, 4, 3, 2)
             .reshape(NE, NFC, 128, D))
    wu_t = c(wu.reshape(NE, NFC, 128, KD, 128).transpose(0, 1, 4, 3, 2)
             .reshape(NE, NFC, 128, D))
    wd_t = c(wd.reshape(NE, D, NFC, 128).transpose(0, 2, 3, 1))
    gw_t = c(gw.reshape(16, KD, 128).transpose(2, 1, 0).reshape(128, KD * 16))

    wk = np.asarray(i["L0_attn_Wk"], f32)
    wv = np.asarray(i["L0_attn_Wv"], f32)
    wmb = np.asarray(i["L0_attn_Wmb"], f32)
    wq = np.asarray(i["L0_attn_Wq"], f32)
    wgt = np.asarray(i["L0_attn_Wg"], f32)
    wk_t, wv_t = c(wk.T), c(wv.T)
    wmb_t, wq_t, wgt_t = c(wmb.T), c(wq.T), c(wgt.T)

    bk = np.asarray(i["L0_attn_bk"], f32)
    bv = np.asarray(i["L0_attn_bv"], f32)
    bkv = np.concatenate(
        [np.stack([bk.reshape(4, 128)[g], bv.reshape(4, 128)[g]]).reshape(256)
         for g in range(4)]).reshape(1, 1024)
    bkv = c(bkv)
    bmb = c(np.asarray(i["L0_attn_bmb"], f32).reshape(1, D))
    bq = c(np.asarray(i["L0_attn_bq"], f32).reshape(1, D))
    bg = c(np.asarray(i["L0_attn_bg"], f32).reshape(1, D))

    mem = np.asarray(i["L0_attn_memory"], f32)
    nrm = np.asarray(i["L0_attn_norm"], f32)
    mem4 = c(mem.reshape(4, 4, NFD, 128, 128).transpose(0, 2, 3, 1, 4)
             .reshape(4, NFD, 128, 512))
    norm4 = c(nrm.reshape(4, 4, NFD, 128).transpose(0, 2, 3, 1))
    maug = c(np.concatenate(
        [mem.reshape(NH, NFD, 128, 128), nrm.reshape(NH, NFD, 128, 1)], axis=3))

    x = np.asarray(i["L0_attn_hidden_states"], f32).reshape(NCORE * TOK, D)
    shared = dict(wg_t=wg_t, wu_t=wu_t, wd_t=wd_t, gw_t=gw_t,
                  wk_t=wk_t, wv_t=wv_t, wmb_t=wmb_t, wq_t=wq_t, wgt_t=wgt_t,
                  bkv=bkv, bmb=bmb, bq=bq, bg=bg,
                  mem4=mem4, norm4=norm4, maug=maug)
    maps = []
    for cix in range(NCORE):
        m = dict(shared)
        m["x_in"] = np.ascontiguousarray(x[cix * TOK:(cix + 1) * TOK])
        maps.append(m)
    return maps


class Ctx:

    def __init__(self, nc, stack):
        self.nc = nc
        self.stack = stack
        self.sem = {}
        self.cnt = {}

    def s(self, name):
        if name not in self.sem:
            self.sem[name] = self.stack.enter_context(self.nc.semaphore(name))
            self.cnt[name] = 0
        return self.sem[name]

    def inc(self, ins, name, k=1):
        ins.then_inc(self.s(name), k)
        self.cnt[name] += k
        return self.cnt[name]

    def snap(self, *names):
        return {n: self.cnt.get(n, 0) for n in names}

    def vmark(self):
        return self.cnt.get("vo", 0)


_ORDERED_OPS = ("activation", "tensor_tensor", "tensor_scalar",
                "scalar_tensor_tensor", "tensor_reduce", "tensor_copy",
                "reciprocal", "memset", "tensor_add", "tensor_sub",
                "tensor_mul", "select")


class _OrderedEng:

    def __init__(self, eng, C, nm):
        self._e = eng
        self._C = C
        self._nm = nm

    def __getattr__(self, a):
        f = getattr(self._e, a)
        if a not in _ORDERED_OPS:
            return f

        def wrap(*args, **kw):
            n = self._C.cnt.get(self._nm, 0)
            if n > 0:
                self._e.wait_ge(self._C.s(self._nm), n)
            ins = f(*args, **kw)
            ins.then_inc(self._C.s(self._nm), 1)
            self._C.cnt[self._nm] = n + 1
            return ins
        return wrap


def build_program(phases=("ffn1", "attnA", "aug", "attnB", "ffn2"),
                  sim_safe=False):
    import contextlib
    nc = bass.Bass("TRN2", target_bir_lowering=False, debug=False,
                   detect_race_conditions=not sim_safe)

    x_in = nc.dram_tensor("x_in", [TOK, D], F32, kind="ExternalInput").ap()
    y_out = nc.dram_tensor("y_out", [TOK, D], F32, kind="ExternalOutput").ap()
    wg_t = nc.dram_tensor("wg_t", [NE, NFC, 128, D], F32, kind="ExternalInput").ap()
    wu_t = nc.dram_tensor("wu_t", [NE, NFC, 128, D], F32, kind="ExternalInput").ap()
    wd_t = nc.dram_tensor("wd_t", [NE, NFC, 128, D], F32, kind="ExternalInput").ap()
    gw_t = nc.dram_tensor("gw_t", [128, KD * 16], F32, kind="ExternalInput").ap()
    wk_t = nc.dram_tensor("wk_t", [D, 512], F32, kind="ExternalInput").ap()
    wv_t = nc.dram_tensor("wv_t", [D, 512], F32, kind="ExternalInput").ap()
    wmb_t = nc.dram_tensor("wmb_t", [D, D], F32, kind="ExternalInput").ap()
    wq_t = nc.dram_tensor("wq_t", [D, D], F32, kind="ExternalInput").ap()
    wgt_t = nc.dram_tensor("wgt_t", [D, D], F32, kind="ExternalInput").ap()
    bkv = nc.dram_tensor("bkv", [1, 1024], F32, kind="ExternalInput").ap()
    bmb = nc.dram_tensor("bmb", [1, D], F32, kind="ExternalInput").ap()
    bq = nc.dram_tensor("bq", [1, D], F32, kind="ExternalInput").ap()
    bg = nc.dram_tensor("bg", [1, D], F32, kind="ExternalInput").ap()
    mem4 = nc.dram_tensor("mem4", [4, NFD, 128, 512], F32, kind="ExternalInput").ap()
    norm4 = nc.dram_tensor("norm4", [4, NFD, 128, 4], F32, kind="ExternalInput").ap()
    maug = nc.dram_tensor("maug", [NH, NFD, 128, 129], F32, kind="ExternalInput").ap()

    dev = "dev_x" in phases
    x_dram = nc.dram_tensor("x_dram", [TOK, D], F32,
                            kind="ExternalOutput" if dev else "Internal").ap()
    h2_dram = nc.dram_tensor(
        "h2_dram", [KD, 128, TOK], BF16,
        kind="ExternalOutput" if "dev_h2" in phases else "Internal").ap()
    cc_src = [nc.dram_tensor("cc_src%d" % g, [128, 3096], F32,
                             kind="Internal").ap() for g in range(4)]
    cc_dst = [nc.dram_tensor("cc_dst%d" % g, [128, 3096], F32,
                             addr_space="Shared", kind="Internal").ap()
              for g in range(4)]
    aug_dram = nc.dram_tensor("aug_dram", [NH, 128, NFD * 129], BF16,
                              kind="Internal").ap()

    ident_h = nc.inline_tensor(np.eye(128, dtype=ml_dtypes.bfloat16), name="identc")
    identf_h = nc.inline_tensor(np.eye(16, dtype=np.float32), name="identfc")
    ones_h = nc.inline_tensor(np.ones((1, 128), dtype=ml_dtypes.bfloat16), name="onesc")

    st = contextlib.ExitStack()
    with st:
        C = Ctx(nc, st)
        sb = lambda n, sh, dt: st.enter_context(nc.sbuf_tensor(n, sh, dt))
        ps = lambda n, sh, dt: st.enter_context(nc.psum_tensor(n, sh, dt))

        Hf = sb("Hf", [128, KD * S], BF16)
        ident = sb("ident", [128, 128], BF16)
        identf = sb("identf", [16, 16], F32)
        ones = sb("ones", [1, 128], BF16)
        bkv_s = sb("bkv_s", [1, 1024], BF16)
        bmb_s = sb("bmb_s", [1, D], BF16)
        bq_s = sb("bq_s", [1, D], BF16)
        bg_s = sb("bg_s", [1, D], BF16)
        gw_s = sb("gw_s", [128, KD * 16], BF16)
        ln2_mr = sb("ln2_mr", [128, 2 * NT], F32)
        cols = sb("cols", [128, 64], F32)
        epsA = sb("epsA", [128, 1], F32)

        Xh = sb("Xh", [128, 8 * D], F32)
        xts = sb("xts", [128, D], BF16)
        gsilu = sb("gsilu", [128, S], F32)
        inter = sb("inter", [128, NFC * S], BF16)
        wgb = sb("wgb", [128, 2 * D], BF16)
        wub = sb("wub", [128, 2 * D], BF16)
        wdb = sb("wdb", [128, NFC * D], BF16)
        cw = sb("cw", [128, 8 * 16], F32)
        sc_sb = sb("sc_sb", [128, 8 * 16], F32)
        esc = sb("esc", [128, 8 * 16], F32)
        wk_sc = sb("wk_sc", [128, 8 * 16], F32)
        escsn = sb("escsn", [128, 8 * 16], F32)
        scsn = sb("scsn", [128, 8 * 16], F32)
        colsn = sb("colsn", [128, 32], F32)

        tpp = ps("tpp", [128, 2048], BF16)
        P0 = ps("P0", [128, 1024], F32)
        P1 = ps("P1", [128, 1024], F32)
        P2 = ps("P2", [128, 1024], F32)

        with nc.Block() as bk_:
            BARS = [0]

            def barrier():
                BARS[0] += 5
                tgt = BARS[0]
                for en in ("tensor", "vector", "scalar", "sync", "gpsimd"):
                    def f(eng, en=en, tgt=tgt):
                        if en == "vector" and C.cnt.get("vo", 0) > 0:
                            eng.wait_ge(C.s("vo"), C.cnt["vo"])
                        eng.sem_inc(C.s("BAR"), 1)
                        eng.wait_ge(C.s("BAR"), tgt)
                    getattr(bk_, en)(f)

            def p0_sync(eng):
                eng.dma_start(out=ident[:], in_=ident_h.ap()).then_inc(C.s("cld"), 16)
                eng.dma_start(out=identf[:], in_=identf_h.ap()).then_inc(C.s("cld"), 16)
                eng.dma_start(out=ones[:], in_=ones_h.ap()).then_inc(C.s("cld"), 16)
                C.cnt["cld"] = C.cnt.get("cld", 0) + 48
                eng.wait_ge(C.s("cld"), C.cnt["cld"])

            def p0_gp(eng):
                for dst, src in ((bkv_s, bkv), (bmb_s, bmb), (bq_s, bq),
                                 (bg_s, bg), (gw_s, gw_t)):
                    eng.dma_start(out=dst[:], in_=src).then_inc(C.s("cldg"), 16)
                    C.cnt["cldg"] = C.cnt.get("cldg", 0) + 16
                eng.wait_ge(C.s("cldg"), C.cnt["cldg"])

            def p0_vec(ueng):
                eng = _OrderedEng(ueng, C, "vo")
                eng.memset(epsA[:], EPS_A)

            bk_.sync(p0_sync)
            bk_.gpsimd(p0_gp)
            bk_.vector(p0_vec)
            barrier()

            def emit_ffn_sweep(sw, x_src, x_dst, ln2_out, half):
                t0g = sw * 8
                nE = globals().get("NE_RUN", NE)
                sweep_ne = globals().get("NE_RUN_SW", None)
                if sweep_ne is not None:
                    nE = sweep_ne[globals().setdefault("_SW_IDX", 0) % len(sweep_ne)]
                    globals()["_SW_IDX"] = globals()["_SW_IDX"] + 1

                def prev_slot(e, fc):
                    if fc >= 2:
                        return e * NFC + fc - 2
                    if e == 0:
                        return None
                    return (e - 1) * NFC + (NFC - 1 if fc == 0 else NFC - 2)

                B = C.snap("xld", "sq", "st2", "tp", "hE", "rt1", "scE", "smE",
                           "g", "sil", "u", "wg0", "wg1", "wu0", "wu1", "wdld",
                           "d", "xst", "h2E", "h2d", "sq2", "st22", "tp2")
                M = {}

                def f_gp(eng):
                    for e in range(nE):
                        for fc in range(NFC):
                            sg = "wg0" if fc % 2 == 0 else "wg1"
                            su = "wu0" if fc % 2 == 0 else "wu1"
                            pv = prev_slot(e, fc)
                            if pv is not None:
                                eng.wait_ge(C.s("g"), B["g"] + pv + 1)
                            eng.dma_start(out=wgb[:, (fc % 2) * D:(fc % 2 + 1) * D],
                                          in_=wg_t[e, fc]).then_inc(C.s(sg), 16)
                            C.cnt[sg] += 16
                            if pv is not None:
                                eng.wait_ge(C.s("u"), B["u"] + pv + 1)
                            eng.dma_start(out=wub[:, (fc % 2) * D:(fc % 2 + 1) * D],
                                          in_=wu_t[e, fc]).then_inc(C.s(su), 16)
                            C.cnt[su] += 16
                        if e >= 1:
                            eng.wait_ge(C.s("d"), B["d"] + e * 16)
                        for fc in range(NFC):
                            eng.dma_start(out=wdb[:, fc * D:(fc + 1) * D],
                                          in_=wd_t[e, fc]).then_inc(C.s("wdld"), 16)
                            C.cnt["wdld"] += 16

                bk_.gpsimd(f_gp)

                def f_vec(ueng):
                    eng = _OrderedEng(ueng, C, "vo")
                    ueng.wait_ge(C.s("xld"), B["xld"] + 128)
                    for t in range(8):
                        eng.tensor_reduce(cols[:, 56 + t:57 + t],
                                          Xh[:, t * D:(t + 1) * D], AX.X, ALU.add)
                    ueng.wait_ge(C.s("sq"), B["sq"] + 8)
                    eng.tensor_scalar(cols[:, 32:40], cols[:, 56:64],
                                      1.0 / D, None, ALU.mult)
                    eng.tensor_scalar(cols[:, 0:8], cols[:, 16:24],
                                      1.0 / D, None, ALU.mult)
                    eng.tensor_tensor(cols[:, 48:56], cols[:, 32:40],
                                      cols[:, 32:40], ALU.mult)
                    eng.tensor_tensor(cols[:, 0:8], cols[:, 0:8],
                                      cols[:, 48:56], ALU.subtract)
                    eng.tensor_scalar(cols[:, 0:8], cols[:, 0:8],
                                      EPS_LN, None, ALU.add)
                    M["st1"] = C.vmark()
                    ueng.wait_ge(C.s("st2"), B["st2"] + 1)
                    eng.reciprocal(cols[:, 40:48], cols[:, 8:16])
                    for t in range(8):
                        if t >= 1:
                            ueng.wait_ge(C.s("tp"), B["tp"] + t * 4)
                        eng.tensor_scalar(xts[:], Xh[:, t * D:(t + 1) * D],
                                          cols[:, 32 + t:33 + t],
                                          cols[:, 40 + t:41 + t],
                                          ALU.subtract, ALU.mult)
                        M["ap%d" % t] = C.vmark()
                    for t in range(8):
                        ueng.wait_ge(C.s("scE"), B["scE"] + t + 1)
                        eng.tensor_reduce(cols[:, 24 + t:25 + t],
                                          sc_sb[:, t * 16:(t + 1) * 16],
                                          AX.X, ALU.max)
                        eng.tensor_scalar(cols[:, 24 + t:25 + t],
                                          cols[:, 24 + t:25 + t], -1.0, None,
                                          ALU.mult)
                        M["sm1%d" % t] = C.vmark()
                    for t in range(8):
                        sl = slice(t * 16, (t + 1) * 16)
                        ueng.wait_ge(C.s("smE"), B["smE"] + t + 1)
                        eng.tensor_reduce(cols[:, 24 + t:25 + t], esc[:, sl],
                                          AX.X, ALU.add)
                        eng.reciprocal(cols[:, 48 + t:49 + t],
                                       cols[:, 24 + t:25 + t])
                        eng.tensor_scalar(sc_sb[:, sl], esc[:, sl],
                                          cols[:, 48 + t:49 + t], None, ALU.mult)
                        eng.tensor_copy(wk_sc[:, sl], sc_sb[:, sl])
                        for r2 in range(TOPK):
                            eng.tensor_reduce(cols[:, 24 + t:25 + t],
                                              wk_sc[:, sl], AX.X, ALU.max)
                            if r2 < TOPK - 1:
                                eng.tensor_scalar(esc[:, sl], wk_sc[:, sl],
                                                  cols[:, 24 + t:25 + t], None,
                                                  ALU.is_ge)
                                eng.scalar_tensor_tensor(wk_sc[:, sl], esc[:, sl],
                                                         -BIG, wk_sc[:, sl],
                                                         ALU.mult, ALU.add)
                        eng.tensor_scalar(esc[:, sl], sc_sb[:, sl],
                                          cols[:, 24 + t:25 + t], None, ALU.is_ge)
                        eng.scalar_tensor_tensor(cw[:, sl], sc_sb[:, sl], half,
                                                 esc[:, sl], ALU.mult, ALU.mult)
                    M["cwd"] = C.vmark()
                    for e in range(nE):
                        for fc in range(NFC):
                            idx = e * NFC + fc
                            ueng.wait_ge(C.s("u"), B["u"] + idx + 1)
                            ueng.wait_ge(C.s("sil"), B["sil"] + idx + 1)
                            if fc == 0 and e >= 1:
                                ueng.wait_ge(C.s("d"), B["d"] + e * 16)
                            eng.tensor_tensor(inter[:, fc * S:(fc + 1) * S],
                                              gsilu[:], P2[:], ALU.mult)
                            M["itr%d" % idx] = C.vmark()
                        for t in range(8):
                            for dh in range(2):
                                gidx = t * 2 + dh
                                ueng.wait_ge(C.s("d"), B["d"] + e * 16 + gidx + 1)
                                pr = (P0, P1, P2)[gidx % 3]
                                eng.scalar_tensor_tensor(
                                    Xh[:, t * D + dh * 1024:t * D + (dh + 1) * 1024],
                                    pr[:], cw[:, t * 16 + e:t * 16 + e + 1],
                                    Xh[:, t * D + dh * 1024:t * D + (dh + 1) * 1024],
                                    ALU.mult, ALU.add)
                                M["xa%d_%d" % (e, gidx)] = C.vmark()
                    M["xa_all"] = C.vmark()
                    if ln2_out:
                        for t in range(8):
                            eng.tensor_reduce(cols[:, 56 + t:57 + t],
                                              Xh[:, t * D:(t + 1) * D], AX.X,
                                              ALU.add)
                        ueng.wait_ge(C.s("sq2"), B["sq2"] + 8)
                        eng.tensor_scalar(cols[:, 32:40], cols[:, 56:64],
                                          1.0 / D, None, ALU.mult)
                        eng.tensor_scalar(cols[:, 0:8], cols[:, 16:24],
                                          1.0 / D, None, ALU.mult)
                        eng.tensor_tensor(cols[:, 48:56], cols[:, 32:40],
                                          cols[:, 32:40], ALU.mult)
                        eng.tensor_tensor(cols[:, 0:8], cols[:, 0:8],
                                          cols[:, 48:56], ALU.subtract)
                        eng.tensor_scalar(cols[:, 0:8], cols[:, 0:8],
                                          EPS_LN, None, ALU.add)
                        M["st12"] = C.vmark()
                        ueng.wait_ge(C.s("st22"), B["st22"] + 1)
                        eng.reciprocal(cols[:, 40:48], cols[:, 8:16])
                        for t in range(8):
                            g2 = t0g + t
                            eng.tensor_copy(ln2_mr[:, 2 * g2:2 * g2 + 1],
                                            cols[:, 32 + t:33 + t])
                            eng.tensor_copy(ln2_mr[:, 2 * g2 + 1:2 * g2 + 2],
                                            cols[:, 40 + t:41 + t])
                            if t >= 1:
                                ueng.wait_ge(C.s("tp2"), B["tp2"] + t * 4)
                            eng.tensor_scalar(xts[:], Xh[:, t * D:(t + 1) * D],
                                              cols[:, 32 + t:33 + t],
                                              cols[:, 40 + t:41 + t],
                                              ALU.subtract, ALU.mult)
                            M["ap2%d" % t] = C.vmark()

                bk_.vector(f_vec)

                def f_scal(eng):
                    eng.wait_ge(C.s("xld"), B["xld"] + 128)
                    for t in range(8):
                        eng.activation(inter[:, 0:D], Xh[:, t * D:(t + 1) * D],
                                       AF.Square,
                                       accum_out=cols[:, 16 + t:17 + t]
                                       ).then_inc(C.s("sq"), 1)
                    C.cnt["sq"] += 8
                    eng.wait_ge(C.s("vo"), M["st1"])
                    eng.activation(cols[:, 8:16], cols[:, 0:8], AF.Sqrt
                                   ).then_inc(C.s("st2"), 1)
                    C.cnt["st2"] += 1
                    for t in range(8):
                        for gb in range(4):
                            eng.wait_ge(C.s("tp"), B["tp"] + t * 4 + gb + 1)
                            dst = Hf.ap().rearrange("p (k j) -> p k j", k=KD)[
                                :, gb * 4:(gb + 1) * 4, t * 128:(t + 1) * 128]
                            src2 = tpp[:, ((t * 4 + gb) % 2) * 1024:
                                       ((t * 4 + gb) % 2) * 1024 + 512].rearrange(
                                "p (k j) -> p k j", k=4)
                            eng.activation(dst, src2, AF.Copy).then_inc(C.s("hE"), 1)
                    C.cnt["hE"] += 32
                    for t in range(8):
                        eng.wait_ge(C.s("rt1"), B["rt1"] + t + 1)
                        ps_r = P0 if t % 2 == 0 else P1
                        eng.activation(sc_sb[:, t * 16:(t + 1) * 16],
                                       ps_r[:, 0:16], AF.Copy
                                       ).then_inc(C.s("scE"), 1)
                    C.cnt["scE"] += 8
                    for t in range(8):
                        eng.wait_ge(C.s("vo"), M["sm1%d" % t])
                        eng.activation(esc[:, t * 16:(t + 1) * 16],
                                       sc_sb[:, t * 16:(t + 1) * 16], AF.Exp,
                                       bias=cols[:, 24 + t:25 + t]
                                       ).then_inc(C.s("smE"), 1)
                    C.cnt["smE"] += 8
                    for e in range(nE):
                        for fc in range(NFC):
                            idx = e * NFC + fc
                            eng.wait_ge(C.s("g"), B["g"] + idx + 1)
                            if idx >= 1:
                                eng.wait_ge(C.s("vo"), M["itr%d" % (idx - 1)])
                            gp = P0 if fc % 2 == 0 else P1
                            eng.activation(gsilu[:], gp[:], AF.Silu
                                           ).then_inc(C.s("sil"), 1)
                            C.cnt["sil"] += 1
                    if ln2_out:
                        eng.wait_ge(C.s("vo"), M["xa_all"])
                        for t in range(8):
                            eng.activation(inter[:, 0:D], Xh[:, t * D:(t + 1) * D],
                                           AF.Square,
                                           accum_out=cols[:, 16 + t:17 + t]
                                           ).then_inc(C.s("sq2"), 1)
                        C.cnt["sq2"] += 8
                        eng.wait_ge(C.s("vo"), M["st12"])
                        eng.activation(cols[:, 8:16], cols[:, 0:8], AF.Sqrt
                                       ).then_inc(C.s("st22"), 1)
                        C.cnt["st22"] += 1
                        for t in range(8):
                            for gb in range(4):
                                eng.wait_ge(C.s("tp2"), B["tp2"] + t * 4 + gb + 1)
                                dst = Hf.ap().rearrange("p (k j) -> p k j", k=KD)[
                                    :, gb * 4:(gb + 1) * 4, t * 128:(t + 1) * 128]
                                src2 = tpp[:, ((t * 4 + gb) % 2) * 1024:
                                           ((t * 4 + gb) % 2) * 1024 + 512].rearrange(
                                    "p (k j) -> p k j", k=4)
                                eng.activation(dst, src2, AF.Copy
                                               ).then_inc(C.s("h2E"), 1)
                        C.cnt["h2E"] += 32

                bk_.scalar(f_scal)

                def f_tens(eng):
                    for t in range(8):
                        eng.wait_ge(C.s("vo"), M["ap%d" % t])
                        for gb in range(4):
                            bi = t * 4 + gb
                            if bi >= 2:
                                eng.wait_ge(C.s("hE"), B["hE"] + bi - 1)
                            for j in range(4):
                                k = gb * 4 + j
                                ins = eng.transpose(
                                    tpp[:, (bi % 2) * 1024 + j * 128:
                                        (bi % 2) * 1024 + (j + 1) * 128],
                                    xts[:, k * 128:(k + 1) * 128],
                                    ident[:])
                            ins.then_inc(C.s("tp"), 1)
                    C.cnt["tp"] += 32
                    eng.wait_ge(C.s("hE"), B["hE"] + 32)
                    for t in range(8):
                        ps_r = P0 if t % 2 == 0 else P1
                        if t >= 2:
                            eng.wait_ge(C.s("scE"), B["scE"] + t - 1)
                        for k in range(KD):
                            ins = eng.matmul(
                                ps_r[:, 0:16],
                                Hf[:, k * S + t * 128:k * S + (t + 1) * 128],
                                gw_s[:, k * 16:(k + 1) * 16],
                                start=(k == 0), stop=(k == KD - 1))
                        ins.then_inc(C.s("rt1"), 1)
                        C.cnt["rt1"] += 1
                    for e in range(nE):
                        if e >= 1:
                            eng.wait_ge(C.s("vo"), M["xa%d_15" % (e - 1)])
                        for fc in range(NFC):
                            idx = e * NFC + fc
                            gp = P0 if fc % 2 == 0 else P1
                            if e == 0 and fc <= 1:
                                eng.wait_ge(C.s("scE"), B["scE"] + 8)
                            pv = prev_slot(e, fc)
                            if pv is not None:
                                eng.wait_ge(C.s("sil"), B["sil"] + pv + 1)
                            sg = "wg0" if fc % 2 == 0 else "wg1"
                            ng = e * 6 + fc // 2 + 1 if fc % 2 == 0 \
                                else e * 5 + fc // 2 + 1
                            eng.wait_ge(C.s(sg), B[sg] + 16 * ng)
                            for nh2 in range(2):
                                for k in range(KD):
                                    ins = eng.matmul(
                                        gp[:, nh2 * 512:(nh2 + 1) * 512],
                                        wgb[:, (fc % 2) * D + k * 128:
                                            (fc % 2) * D + (k + 1) * 128],
                                        Hf[:, k * S + nh2 * 512:k * S + (nh2 + 1) * 512],
                                        start=(k == 0), stop=(k == KD - 1))
                            ins.then_inc(C.s("g"), 1)
                            C.cnt["g"] += 1
                            if idx >= 1:
                                eng.wait_ge(C.s("vo"), M["itr%d" % (idx - 1)])
                            su = "wu0" if fc % 2 == 0 else "wu1"
                            eng.wait_ge(C.s(su), B[su] + 16 * ng)
                            for nh2 in range(2):
                                for k in range(KD):
                                    ins = eng.matmul(
                                        P2[:, nh2 * 512:(nh2 + 1) * 512],
                                        wub[:, (fc % 2) * D + k * 128:
                                            (fc % 2) * D + (k + 1) * 128],
                                        Hf[:, k * S + nh2 * 512:k * S + (nh2 + 1) * 512],
                                        start=(k == 0), stop=(k == KD - 1))
                            ins.then_inc(C.s("u"), 1)
                            C.cnt["u"] += 1
                        eng.wait_ge(C.s("sil"), B["sil"] + e * NFC + NFC)
                        eng.wait_ge(C.s("vo"), M["itr%d" % (e * NFC + NFC - 1)])
                        eng.wait_ge(C.s("wdld"), B["wdld"] + 16 * (e + 1) * NFC)
                        for t in range(8):
                            for dh in range(2):
                                gidx = t * 2 + dh
                                pr = (P0, P1, P2)[gidx % 3]
                                if gidx >= 3:
                                    eng.wait_ge(C.s("vo"),
                                                M["xa%d_%d" % (e, gidx - 3)])
                                for nh2 in range(2):
                                    for fc in range(NFC):
                                        ins = eng.matmul(
                                            pr[:, nh2 * 512:(nh2 + 1) * 512],
                                            inter[:, fc * S + t * 128:
                                                  fc * S + (t + 1) * 128],
                                            wdb[:, fc * D + dh * 1024 + nh2 * 512:
                                                fc * D + dh * 1024 + (nh2 + 1) * 512],
                                            start=(fc == 0), stop=(fc == NFC - 1))
                                ins.then_inc(C.s("d"), 1)
                                C.cnt["d"] += 1
                    if ln2_out:
                        for t in range(8):
                            eng.wait_ge(C.s("vo"), M["ap2%d" % t])
                            for gb in range(4):
                                bi = t * 4 + gb
                                if bi >= 2:
                                    eng.wait_ge(C.s("h2E"), B["h2E"] + bi - 1)
                                for j in range(4):
                                    k = gb * 4 + j
                                    ins = eng.transpose(
                                        tpp[:, (bi % 2) * 1024 + j * 128:
                                            (bi % 2) * 1024 + (j + 1) * 128],
                                        xts[:, k * 128:(k + 1) * 128],
                                        ident[:])
                                ins.then_inc(C.s("tp2"), 1)
                        C.cnt["tp2"] += 32

                bk_.tensor(f_tens)

                def f_sync(eng):
                    for t in range(8):
                        eng.dma_start(
                            out=Xh[:, t * D:(t + 1) * D],
                            in_=x_src[(t0g + t) * 128:(t0g + t + 1) * 128, :],
                        ).then_inc(C.s("xld"), 16)
                    C.cnt["xld"] += 128
                    eng.wait_ge(C.s("vo"), M["xa_all"])
                    for t in range(8):
                        eng.dma_start(
                            out=x_dst[(t0g + t) * 128:(t0g + t + 1) * 128, :],
                            in_=Xh[:, t * D:(t + 1) * D],
                        ).then_inc(C.s("xst"), 16)
                    C.cnt["xst"] += 128
                    if ln2_out:
                        eng.wait_ge(C.s("h2E"), B["h2E"] + 32)
                        eng.dma_start(
                            out=h2_dram[:, :, t0g * 128:t0g * 128 + S]
                                .rearrange("k p j -> p k j"),
                            in_=Hf.ap().rearrange("p (k j) -> p k j", k=KD),
                        ).then_inc(C.s("h2d"), 16)
                        C.cnt["h2d"] += 16
                        eng.wait_ge(C.s("h2d"), C.cnt["h2d"])
                    eng.wait_ge(C.s("xst"), C.cnt["xst"])

                bk_.sync(f_sync)

            DCO = (0, 129, 258, 512, 641, 770)
            AWL_KG = [0] * 4

            def emit_attnA_unit(kg, blk):
                B = C.snap("hld", "awl", "kv1", "mb1", "nd1", "tpA", "dl1",
                           "kvE", "sqA", "sqt", "mbE", "dnE", "tpE", "dout",
                           "ccs")
                M = {}
                blkoff = blk * 512
                from concourse.bass import broadcast_tensor_aps as _bc

                def f_gp(eng):
                    if blk == 0:
                        eng.wait_ge(C.s("nd1"), B["nd1"])
                        wkv = wdb[:, 0:4096].rearrange("p (k c) -> p k c", k=KD)
                        eng.dma_start(
                            out=wkv[:, :, 0:128],
                            in_=wk_t[:, kg * 128:(kg + 1) * 128]
                                .rearrange("(k p) j -> p k j", p=128),
                        ).then_inc(C.s("awl"), 16)
                        eng.dma_start(
                            out=wkv[:, :, 128:256],
                            in_=wv_t[:, kg * 128:(kg + 1) * 128]
                                .rearrange("(k p) j -> p k j", p=128),
                        ).then_inc(C.s("awl"), 16)
                        eng.dma_start(
                            out=wdb[:, 4096:12288].rearrange("p (k c) -> p k c",
                                                             k=KD),
                            in_=wmb_t[:, kg * 512:(kg + 1) * 512]
                                .rearrange("(k p) j -> p k j", p=128),
                        ).then_inc(C.s("awl"), 16)
                        eng.dma_start(
                            out=wdb[:, 12288:15360].rearrange("p (c j) -> p c j",
                                                              c=NFD),
                            in_=mem4[kg].rearrange("c p j -> p c j"),
                        ).then_inc(C.s("awl"), 16)
                        eng.dma_start(
                            out=wdb[:, 15360:15384].rearrange("p (c j) -> p c j",
                                                              c=NFD),
                            in_=norm4[kg].rearrange("c p j -> p c j"),
                        ).then_inc(C.s("awl"), 16)
                        C.cnt["awl"] += 80
                        AWL_KG[kg] = C.cnt["awl"]
                    if blk == 3:
                        eng.wait_ge(C.s("vo"), M_DL[0])
                        eng.dma_start(out=cc_src[kg], in_=Xh[:, 8192:11288]
                                      ).then_inc(C.s("dout"), 16)
                        C.cnt["dout"] += 16
                        eng.wait_ge(C.s("dout"), C.cnt["dout"])
                        if True:
                            eng.collective_compute(
                                "AllReduce", ALU.add,
                                replica_groups=[list(range(NCORE))],
                                ins=[cc_src[kg].opt()], outs=[cc_dst[kg].opt()],
                            ).then_inc(C.s("ccs"), 1)
                        C.cnt["ccs"] += 1

                M_DL = [0]

                def f_vec(ueng):
                    eng = _OrderedEng(ueng, C, "vo")
                    xc4 = Xh[:, 1024:2048].rearrange("p (t c) -> p t c", t=4)
                    mk4 = Xh[:, 4864:7936].rearrange("p (t c) -> p t c", t=4)
                    ueng.wait_ge(C.s("sqA"), B["sqA"])
                    ueng.wait_ge(C.s("kvE"), B["kvE"] + 1)
                    for c in range(NFD):
                        jr = c // 2 + 1
                        hh2 = c % 2
                        bb = hh2 * 128
                        eng.tensor_tensor(
                            mk4[:, :, c * 128 + jr:(c + 1) * 128],
                            xc4[:, :, bb + jr:bb + 128],
                            xc4[:, :, bb:bb + 128 - jr], ALU.mult)
                        wsrc = (256 if hh2 == 0 else 128) - jr
                        eng.tensor_tensor(
                            mk4[:, :, c * 128:c * 128 + jr],
                            xc4[:, :, bb:bb + jr],
                            xc4[:, :, wsrc:wsrc + jr], ALU.mult)
                    M["dp"] = C.vmark()
                    ueng.wait_ge(C.s("sqA"), B["sqA"] + 1)
                    eng.tensor_copy(cols[:, 0:4], cols[:, 16:20])
                    M["bn"] = C.vmark()
                    ueng.wait_ge(C.s("sqt"), B["sqt"] + 1)
                    eng.tensor_scalar(cols[:, 24:28], cols[:, 8:12], 1e-12,
                                      None, ALU.max)
                    eng.reciprocal(cols[:, 32:36], cols[:, 24:28])
                    ueng.wait_ge(C.s("dl1"), B["dl1"])
                    for t in range(4):
                        eng.tensor_scalar(inter[:, t * 768:(t + 1) * 768],
                                          Xh[:, 4864 + t * 768:4864 + (t + 1) * 768],
                                          cols[:, 32 + t:33 + t], None, ALU.mult)
                    M["mkn"] = C.vmark()
                    eng.tensor_tensor(cols[:, 40:44], cols[:, 8:12],
                                      cols[:, 32:36], ALU.mult)
                    eng.tensor_tensor(cols[:, 48:52], cols[:, 40:44],
                                      cols[:, 40:44], ALU.mult)
                    eng.reciprocal(cols[:, 56:60], cols[:, 48:52])
                    ueng.wait_ge(C.s("dnE"), B["dnE"] + 1)
                    eng.reciprocal(Xh[:, 528:544], Xh[:, 512:528])
                    d3 = Xh[:, 512:528].rearrange("p (t h) -> p t h", t=4)
                    r3 = cols[:, 56:60].rearrange("p (t h) -> p t h", t=4)
                    c3 = Xh[:, 544:560].rearrange("p (t h) -> p t h", t=4)
                    _, r3b = _bc(d3, r3)
                    eng.tensor_tensor(c3, d3, r3b, ALU.mult)
                    eng.tensor_scalar(Xh[:, 544:560], Xh[:, 544:560],
                                      -1.0, 1.0, ALU.mult, ALU.add)
                    eng.tensor_scalar(Xh[:, 544:560], Xh[:, 544:560],
                                      0.0, 1.0, ALU.max, ALU.min)
                    ueng.wait_ge(C.s("mbE"), B["mbE"] + 2)
                    for t in range(4):
                        ueng.wait_ge(C.s("nd1"), B["nd1"] + t + 1)
                        nps = (P0 if t < 2 else P1)
                        nsl = nps[:, (t % 2) * 512:(t % 2) * 512 + 512]
                        n3 = nsl.rearrange("p (h j) -> p h j", h=4)
                        rc3 = Xh[:, 528 + t * 4:528 + (t + 1) * 4].rearrange(
                            "p (h j) -> p h j", h=4)
                        _, rc3b = _bc(n3, rc3)
                        pv3 = Xh[:, 4352:4864].rearrange("p (h j) -> p h j", h=4)
                        eng.tensor_tensor(pv3, n3, rc3b, ALU.mult)
                        v3 = Xh[:, t * 128:(t + 1) * 128].rearrange(
                            "p (h j) -> p h j", h=1)
                        mv3 = Xh[:, 11288:11800].rearrange("p (h j) -> p h j", h=4)
                        _, v3b = _bc(pv3, v3)
                        eng.tensor_tensor(mv3, v3b, pv3, ALU.subtract)
                        ws = 6144 + t * 4 * 129
                        wm3 = inter.ap()[:, ws:ws + 4 * 129].rearrange(
                            "p (h j) -> p h j", j=129)[:, :, 0:128]
                        eng.tensor_tensor(wm3, mv3,
                                          Xh[:, 2048 + t * 512:2048 + (t + 1) * 512]
                                          .rearrange("p (h j) -> p h j", h=4),
                                          ALU.mult)
                        cf3 = inter.ap()[:, ws:ws + 4 * 129].rearrange(
                            "p (h j) -> p h j", j=129)[:, :, 128:129]
                        eng.tensor_copy(cf3,
                                        Xh[:, 544 + t * 4:544 + (t + 1) * 4]
                                        .rearrange("p (h j) -> p h j", h=4))
                        M["pv%d" % t] = C.vmark()
                    M["wmv"] = C.vmark()
                    if blk == 0:
                        ueng.wait_ge(C.s("dout"), B["dout"])
                    for hh in range(4):
                        ueng.wait_ge(C.s("dl1"), B["dl1"] + hh + 1)
                        ds = 8192 + hh * 774
                        if blk == 0:
                            eng.tensor_copy(Xh[:, ds:ds + 387], P2[:, 0:387])
                            eng.tensor_copy(Xh[:, ds + 387:ds + 774],
                                            P2[:, 512:899])
                        else:
                            eng.tensor_tensor(Xh[:, ds:ds + 387],
                                              Xh[:, ds:ds + 387],
                                              P2[:, 0:387], ALU.add)
                            eng.tensor_tensor(Xh[:, ds + 387:ds + 774],
                                              Xh[:, ds + 387:ds + 774],
                                              P2[:, 512:899], ALU.add)
                        M["dlE%d" % hh] = C.vmark()
                    M_DL[0] = C.vmark()

                bk_.vector(f_vec)

                def f_scal(eng):
                    if APREV[0] is not None:
                        eng.wait_ge(C.s("vo"), APREV[0]["wmv"])
                    eng.wait_ge(C.s("kv1"), B["kv1"] + 1)
                    p03 = P0.ap().rearrange("p (t c) -> p t c", t=4)
                    xc4 = Xh[:, 1024:2048].rearrange("p (t c) -> p t c", t=4)
                    v4 = Xh[:, 0:512].rearrange("p (t c) -> p t c", t=4)
                    eng.activation(xc4[:, :, 0:128], p03[:, :, 0:128], AF.Relu)
                    eng.activation(xc4[:, :, 128:256], p03[:, :, 0:128],
                                   AF.Relu, scale=-1.0)
                    eng.activation(v4, p03[:, :, 128:256], AF.Copy
                                   ).then_inc(C.s("kvE"), 1)
                    C.cnt["kvE"] += 1
                    eng.wait_ge(C.s("vo"), M["dp"])
                    for t in range(4):
                        ins = eng.activation(gsilu[:, 0:768],
                                             Xh[:, 4864 + t * 768:4864 + (t + 1) * 768],
                                             AF.Square,
                                             accum_out=cols[:, 16 + t:17 + t])
                    ins.then_inc(C.s("sqA"), 1)
                    C.cnt["sqA"] += 1
                    eng.wait_ge(C.s("vo"), M["bn"])
                    eng.activation(cols[:, 8:12], cols[:, 0:4], AF.Sqrt
                                   ).then_inc(C.s("sqt"), 1)
                    C.cnt["sqt"] += 1
                    eng.wait_ge(C.s("mb1"), B["mb1"] + 1)
                    mb01 = Xh[:, 2048:3072].rearrange("p (t c) -> p t c", t=2)
                    mb23 = Xh[:, 3072:4096].rearrange("p (t c) -> p t c", t=2)
                    eng.activation(mb01, P1.ap().rearrange("p (t c) -> p t c", t=2),
                                   AF.Sigmoid).then_inc(C.s("mbE"), 1)
                    eng.activation(mb23, P2.ap().rearrange("p (t c) -> p t c", t=2),
                                   AF.Sigmoid).then_inc(C.s("mbE"), 1)
                    C.cnt["mbE"] += 2
                    eng.wait_ge(C.s("nd1"), B["nd1"])
                    for t in range(4):
                        par = t % 2
                        eng.wait_ge(C.s("tpA"), B["tpA"] + t + 1)
                        dstT = inter[:, 3072:6144].rearrange(
                            "p (c j) -> p c j", c=NFD)[:, :, t * 128:(t + 1) * 128]
                        srcT = tpp[:, par * 1024:par * 1024 + 768].rearrange(
                            "p (c j) -> p c j", c=NFD)
                        eng.activation(dstT, srcT, AF.Copy).then_inc(C.s("tpE"), 1)
                        C.cnt["tpE"] += 1
                    eng.wait_ge(C.s("nd1"), B["nd1"] + 4)
                    eng.activation(Xh[:, 512:528], P2[:, 0:16], AF.Identity,
                                   bias=epsA[:]).then_inc(C.s("dnE"), 1)
                    C.cnt["dnE"] += 1

                bk_.scalar(f_scal)

                def f_tens(eng):
                    eng.wait_ge(C.s("hld"), B["hld"] + 16)
                    eng.wait_ge(C.s("awl"), AWL_KG[kg])
                    if APREV[0] is not None:
                        eng.wait_ge(C.s("vo"), APREV[0]["pv1"])
                    for t in range(4):
                        for k in range(KD):
                            eng.matmul(P0[:, t * 256:(t + 1) * 256],
                                       Hf[:, k * S + t * 128:k * S + (t + 1) * 128],
                                       wdb[:, k * 256:(k + 1) * 256],
                                       start=(k == 0), stop=False)
                        ins = eng.matmul(P0[:, t * 256:(t + 1) * 256],
                                         ones[0:1, 0:128],
                                         bkv_s[0:1, kg * 256:(kg + 1) * 256],
                                         start=False, stop=True)
                    ins.then_inc(C.s("kv1"), 1)
                    C.cnt["kv1"] += 1
                    for t in range(4):
                        if APREV[0] is not None:
                            if t == 0:
                                eng.wait_ge(C.s("vo"), APREV[0]["pv3"])
                            elif t == 2:
                                eng.wait_ge(C.s("vo"), B_DL_PREV[0])
                        pm = P1 if t < 2 else P2
                        msl = pm[:, (t % 2) * 512:(t % 2) * 512 + 512]
                        for k in range(KD):
                            eng.matmul(msl,
                                       Hf[:, k * S + t * 128:k * S + (t + 1) * 128],
                                       wdb[:, 4096 + k * 512:4096 + (k + 1) * 512],
                                       start=(k == 0), stop=False)
                        ins = eng.matmul(msl, ones[0:1, 0:128],
                                         bmb_s[0:1, kg * 512:(kg + 1) * 512],
                                         start=False, stop=True)
                    ins.then_inc(C.s("mb1"), 1)
                    C.cnt["mb1"] += 1
                    eng.wait_ge(C.s("vo"), M["mkn"])
                    for t in range(4):
                        par = t % 2
                        if t < 2:
                            eng.wait_ge(C.s("tpE"), B["tpE"])
                        else:
                            eng.wait_ge(C.s("tpE"), B["tpE"] + t - 1)
                        for c in range(NFD):
                            ins = eng.transpose(
                                tpp[:, par * 1024 + c * 128:
                                    par * 1024 + (c + 1) * 128],
                                inter[:, t * 768 + c * 128:t * 768 + (c + 1) * 128],
                                ident[:])
                        ins.then_inc(C.s("tpA"), 1)
                        C.cnt["tpA"] += 1
                    eng.wait_ge(C.s("kvE"), B["kvE"] + 1)
                    eng.wait_ge(C.s("mbE"), B["mbE"] + 2)
                    for t in range(4):
                        eng.wait_ge(C.s("tpE"), B["tpE"] + t + 1)
                        nps = (P0 if t < 2 else P1)
                        nsl = nps[:, (t % 2) * 512:(t % 2) * 512 + 512]
                        for c in range(NFD):
                            lh = inter[:, 3072 + c * 512 + t * 128:
                                       3072 + c * 512 + (t + 1) * 128]
                            eng.matmul(nsl, lh,
                                       wdb[:, 12288 + c * 512:12288 + (c + 1) * 512],
                                       start=(c == 0), stop=(c == NFD - 1))
                            ins = eng.matmul(P2[:, t * 4:(t + 1) * 4], lh,
                                             wdb[:, 15360 + c * 4:15360 + (c + 1) * 4],
                                             start=(c == 0), stop=(c == NFD - 1))
                        ins.then_inc(C.s("nd1"), 1)
                        C.cnt["nd1"] += 1
                    eng.wait_ge(C.s("vo"), M["wmv"])
                    for hh in range(4):
                        if hh >= 1:
                            eng.wait_ge(C.s("vo"), M["dlE%d" % (hh - 1)])
                        elif blk >= 1:
                            eng.wait_ge(C.s("vo"), B_DL_PREV[0])
                        for c in range(NFD):
                            for t in range(4):
                                ws = 6144 + (t * 4 + hh) * 129
                                ins = eng.matmul(
                                    P2[:, DCO[c]:DCO[c] + 129],
                                    inter[:, t * 768 + c * 128:
                                          t * 768 + (c + 1) * 128],
                                    inter[:, ws:ws + 129],
                                    start=(t == 0), stop=(t == 3))
                        ins.then_inc(C.s("dl1"), 1)
                        C.cnt["dl1"] += 1

                bk_.tensor(f_tens)

                def f_sync(eng):
                    eng.wait_ge(C.s("mb1"), B["mb1"])
                    eng.dma_start(
                        out=Hf.ap().rearrange("p (k j) -> p k j", k=KD)[:, :, 0:512],
                        in_=h2_dram[:, :, blkoff:blkoff + 512]
                            .rearrange("k p j -> p k j"),
                    ).then_inc(C.s("hld"), 16)
                    C.cnt["hld"] += 16

                bk_.sync(f_sync)
                bk_.gpsimd(f_gp)
                B_DL_PREV[0] = M_DL[0]
                APREV[0] = dict(M)

            B_DL_PREV = [0]
            AUG_LAST = [0]
            APREV = [None]

            def emit_aug_kg(kg):
                B = C.snap("agl", "ags", "ccs")
                M = {}

                def f_gp(eng):
                    eng.wait_ge(C.s("ccs"), kg + 1)
                    eng.dma_start(
                        out=Xh[:, 0:3096].rearrange("p (h c j) -> p h c j",
                                                    h=4, c=NFD),
                        in_=maug[4 * kg:4 * kg + 4]
                            .rearrange("h c p j -> p h c j"),
                    ).then_inc(C.s("agl"), 16)
                    eng.dma_start(out=Xh[:, 4096:7192], in_=cc_dst[kg]
                                  ).then_inc(C.s("agl"), 16)
                    C.cnt["agl"] += 32
                    eng.wait_ge(C.s("vo"), M["aug"])
                    eng.dma_start(
                        out=aug_dram[4 * kg:4 * kg + 4]
                            .rearrange("h p j -> p h j"),
                        in_=wdb[:, 16384:19480].rearrange("p (h j) -> p h j", h=4),
                    ).then_inc(C.s("ags"), 16)
                    C.cnt["ags"] += 16
                    eng.wait_ge(C.s("ags"), C.cnt["ags"])

                def f_vec(ueng):
                    eng = _OrderedEng(ueng, C, "vo")
                    ueng.wait_ge(C.s("agl"), B["agl"] + 32)
                    eng.tensor_tensor(wdb[:, 16384:19480], Xh[:, 0:3096],
                                      Xh[:, 4096:7192], ALU.add)
                    M["aug"] = C.vmark()
                    AUG_LAST[0] = M["aug"]

                bk_.vector(f_vec)
                bk_.gpsimd(f_gp)

            BWL_KG = [0]

            def emit_attnB_unit(blk, kg):
                B = C.snap("hldB", "xldB", "bwq", "bwg", "bwa", "q1", "g1",
                           "tpAB", "na1",
                           "qE", "gE", "tpEB", "sqB", "sqtB", "dnB", "xstB")
                M = {}
                blkoff = blk * 512
                from concourse.bass import broadcast_tensor_aps as _bc
                NT_ = (P1, P1, P0, P0, P2, P2)

                def nslot(n):
                    tn = NT_[n // 3]
                    off = ((n // 3) % 2) * 512 + (n % 3) * 129
                    return tn, off

                def f_gp(eng):
                    eng.dma_start(
                        out=wdb[:, 0:8192].rearrange("p (k c) -> p k c", k=KD),
                        in_=wq_t[:, kg * 512:(kg + 1) * 512]
                            .rearrange("(k p) j -> p k j", p=128),
                    ).then_inc(C.s("bwq"), 16)
                    C.cnt["bwq"] += 16
                    eng.dma_start(
                        out=wdb[:, 8192:16384].rearrange("p (k c) -> p k c", k=KD),
                        in_=wgt_t[:, kg * 512:(kg + 1) * 512]
                            .rearrange("(k p) j -> p k j", p=128),
                    ).then_inc(C.s("bwg"), 16)
                    C.cnt["bwg"] += 16
                    eng.wait_ge(C.s("ags"), 16 * (kg + 1))
                    eng.dma_start(
                        out=inter[:, 3840:6936].rearrange("p (h j) -> p h j", h=4),
                        in_=aug_dram[4 * kg:4 * kg + 4]
                            .rearrange("h p j -> p h j"),
                    ).then_inc(C.s("bwa"), 16)
                    C.cnt["bwa"] += 16

                def f_vec(ueng):
                    eng = _OrderedEng(ueng, C, "vo")
                    if kg == 0:
                        ueng.wait_ge(C.s("xldB"), B["xldB"] + 64)
                        for t in range(4):
                            g2 = blk * 4 + t
                            eng.tensor_scalar(cols[:, 0 + t:1 + t],
                                              ln2_mr[:, 2 * g2 + 1:2 * g2 + 2],
                                              1.0, None, ALU.add)
                            eng.tensor_tensor(cols[:, 8 + t:9 + t],
                                              ln2_mr[:, 2 * g2:2 * g2 + 1],
                                              ln2_mr[:, 2 * g2 + 1:2 * g2 + 2],
                                              ALU.mult)
                            eng.tensor_scalar(Xh[:, t * D:(t + 1) * D],
                                              Xh[:, t * D:(t + 1) * D],
                                              cols[:, 0 + t:1 + t],
                                              cols[:, 8 + t:9 + t],
                                              ALU.mult, ALU.subtract)
                        M["lnB"] = C.vmark()
                    for t in range(4):
                        qb = 8192 + t * 512
                        ueng.wait_ge(C.s("qE"), B["qE"] + 2)
                        if t >= 1:
                            ueng.wait_ge(C.s("tpAB"), B["tpAB"] + 4 * t)
                        xq = Xh[:, 12288:13312].rearrange("p (h c) -> p h c", h=4)
                        q4v = Xh[:, qb:qb + 512].rearrange("p (h c) -> p h c", h=4)
                        mq4v = Xh[:, 13312:16384].rearrange("p (h c) -> p h c", h=4)
                        eng.tensor_scalar(xq[:, :, 0:128], q4v, 0.0, None, ALU.max)
                        eng.tensor_scalar(xq[:, :, 128:256], q4v, -1.0, 0.0,
                                          ALU.mult, ALU.max)
                        for c in range(NFD):
                            jr = c // 2 + 1
                            hh2 = c % 2
                            bb = hh2 * 128
                            eng.tensor_tensor(
                                mq4v[:, :, c * 128 + jr:(c + 1) * 128],
                                xq[:, :, bb + jr:bb + 128],
                                xq[:, :, bb:bb + 128 - jr], ALU.mult)
                            wsrc = (256 if hh2 == 0 else 128) - jr
                            eng.tensor_tensor(
                                mq4v[:, :, c * 128:c * 128 + jr],
                                xq[:, :, bb:bb + jr],
                                xq[:, :, wsrc:wsrc + jr], ALU.mult)
                        M["mq%d" % t] = C.vmark()
                        ueng.wait_ge(C.s("sqB"), B["sqB"] + t + 1)
                        eng.tensor_copy(cols[:, 0:4], cols[:, 16:20])
                        M["bnB%d" % t] = C.vmark()
                        ueng.wait_ge(C.s("sqtB"), B["sqtB"] + t + 1)
                        eng.tensor_scalar(cols[:, 40:44], cols[:, 8:12], 1e-12,
                                          None, ALU.max)
                        eng.reciprocal(cols[:, 48:52], cols[:, 40:44])
                        for hh in range(4):
                            eng.tensor_scalar(
                                inter[:, hh * 768:(hh + 1) * 768],
                                Xh[:, 13312 + hh * 768:13312 + (hh + 1) * 768],
                                cols[:, 48 + hh:49 + hh], None, ALU.mult)
                        M["mqn%d" % t] = C.vmark()
                    ueng.wait_ge(C.s("dnB"), B["dnB"] + 6)
                    eng.reciprocal(Xh[:, 15424:15440], Xh[:, 15360:15376])
                    for grp in range(6):
                        nn = 3 if grp < 5 else 1
                        tn = NT_[grp]
                        off = (grp % 2) * 512
                        n3 = tn.ap()[:, off:off + nn * 129].rearrange(
                            "p (s j) -> p s j", j=129)[:, :, 0:128]
                        rc3 = Xh[:, 15424 + grp * 3:15424 + grp * 3 + nn].rearrange(
                            "p (s j) -> p s j", s=nn)
                        _, rc3b = _bc(n3, rc3)
                        mh3 = Xh[:, 13312 + grp * 384:13312 + grp * 384 + nn * 128]\
                            .rearrange("p (s j) -> p s j", s=nn)
                        eng.tensor_tensor(mh3, n3, rc3b, ALU.mult)
                    M["mh"] = C.vmark()
                    ueng.wait_ge(C.s("gE"), B["gE"] + 2)
                    for t in range(4):
                        gm = Xh[:, 12288:12800]
                        eng.tensor_tensor(gm,
                                          Xh[:, 10240 + t * 512:10240 + (t + 1) * 512],
                                          Xh[:, 13312 + t * 512:13312 + (t + 1) * 512],
                                          ALU.mult)
                        xsl = Xh[:, t * D + kg * 512:t * D + (kg + 1) * 512]
                        eng.tensor_tensor(xsl, xsl, gm, ALU.add)
                    M["updF"] = C.vmark()

                bk_.vector(f_vec)

                def f_scal(eng):
                    eng.wait_ge(C.s("q1"), B["q1"] + 1)
                    q01 = Xh[:, 8192:9216].rearrange("p (t c) -> p t c", t=2)
                    q23 = Xh[:, 9216:10240].rearrange("p (t c) -> p t c", t=2)
                    eng.activation(q01, P0.ap().rearrange("p (t c) -> p t c", t=2),
                                   AF.Copy).then_inc(C.s("qE"), 1)
                    eng.activation(q23, P2.ap().rearrange("p (t c) -> p t c", t=2),
                                   AF.Copy).then_inc(C.s("qE"), 1)
                    C.cnt["qE"] += 2
                    eng.wait_ge(C.s("g1"), B["g1"] + 1)
                    g01 = Xh[:, 10240:11264].rearrange("p (t c) -> p t c", t=2)
                    g23 = Xh[:, 11264:12288].rearrange("p (t c) -> p t c", t=2)
                    eng.activation(g01, P0.ap().rearrange("p (t c) -> p t c", t=2),
                                   AF.Sigmoid).then_inc(C.s("gE"), 1)
                    eng.activation(g23, P2.ap().rearrange("p (t c) -> p t c", t=2),
                                   AF.Sigmoid).then_inc(C.s("gE"), 1)
                    C.cnt["gE"] += 2
                    for t in range(4):
                        eng.wait_ge(C.s("vo"), M["mq%d" % t])
                        for hh in range(4):
                            ins = eng.activation(gsilu[:, 0:768],
                                                 Xh[:, 13312 + hh * 768:
                                                    13312 + (hh + 1) * 768],
                                                 AF.Square,
                                                 accum_out=cols[:, 16 + hh:17 + hh])
                        ins.then_inc(C.s("sqB"), 1)
                        C.cnt["sqB"] += 1
                        eng.wait_ge(C.s("vo"), M["bnB%d" % t])
                        eng.activation(cols[:, 8:12], cols[:, 0:4], AF.Sqrt
                                       ).then_inc(C.s("sqtB"), 1)
                        C.cnt["sqtB"] += 1
                        for hh in range(4):
                            n = t * 4 + hh
                            eng.wait_ge(C.s("tpAB"), B["tpAB"] + n + 1)
                            if n >= 2:
                                eng.wait_ge(C.s("na1"), B["na1"] + n - 1)
                            eng.activation(inter[:, 6936 + (n % 2) * 768:
                                                 6936 + (n % 2) * 768 + 768],
                                           tpp[:, (n % 2) * 1024:
                                               (n % 2) * 1024 + 768],
                                           AF.Copy).then_inc(C.s("tpEB"), 1)
                            C.cnt["tpEB"] += 1
                    eng.wait_ge(C.s("na1"), B["na1"] + 16)
                    for grp in range(6):
                        nn = 3 if grp < 5 else 1
                        tn = NT_[grp]
                        off = (grp % 2) * 512
                        src3 = tn.ap()[:, off:off + nn * 129].rearrange(
                            "p (s j) -> p s j", j=129)[:, :, 128:129]
                        eng.activation(Xh[:, 15360 + grp * 3:15360 + grp * 3 + nn]
                                       .rearrange("p (s j) -> p s j", s=nn),
                                       src3, AF.Identity, bias=epsA[:]
                                       ).then_inc(C.s("dnB"), 1)
                        C.cnt["dnB"] += 1

                bk_.scalar(f_scal)

                def f_tens(eng):
                    eng.wait_ge(C.s("hldB"), B["hldB"] + (16 if kg == 0 else 0))
                    eng.wait_ge(C.s("bwq"), B["bwq"] + 16)
                    if kg == 0:
                        eng.wait_ge(C.s("vo"), M["lnB"])
                    for t in range(4):
                        pq = P0 if t < 2 else P2
                        qs = pq[:, (t % 2) * 512:(t % 2) * 512 + 512]
                        for k in range(KD):
                            eng.matmul(qs,
                                       Hf[:, k * S + t * 128:k * S + (t + 1) * 128],
                                       wdb[:, k * 512:(k + 1) * 512],
                                       start=(k == 0), stop=False)
                        ins = eng.matmul(qs, ones[0:1, 0:128],
                                         bq_s[0:1, kg * 512:(kg + 1) * 512],
                                         start=False, stop=True)
                    ins.then_inc(C.s("q1"), 1)
                    C.cnt["q1"] += 1
                    eng.wait_ge(C.s("qE"), B["qE"] + 2)
                    eng.wait_ge(C.s("bwg"), B["bwg"] + 16)
                    for t in range(4):
                        pq = P0 if t < 2 else P2
                        qs = pq[:, (t % 2) * 512:(t % 2) * 512 + 512]
                        for k in range(KD):
                            eng.matmul(qs,
                                       Hf[:, k * S + t * 128:k * S + (t + 1) * 128],
                                       wdb[:, 8192 + k * 512:8192 + (k + 1) * 512],
                                       start=(k == 0), stop=False)
                        ins = eng.matmul(qs, ones[0:1, 0:128],
                                         bg_s[0:1, kg * 512:(kg + 1) * 512],
                                         start=False, stop=True)
                    ins.then_inc(C.s("g1"), 1)
                    C.cnt["g1"] += 1
                    eng.wait_ge(C.s("gE"), B["gE"] + 2)
                    eng.wait_ge(C.s("bwa"), B["bwa"] + 16)
                    for t in range(4):
                        eng.wait_ge(C.s("vo"), M["mqn%d" % t])
                        for hh in range(4):
                            n = t * 4 + hh
                            if n >= 2:
                                eng.wait_ge(C.s("tpEB"), B["tpEB"] + n - 1)
                            for c in range(NFD):
                                ins = eng.transpose(
                                    tpp[:, (n % 2) * 1024 + c * 128:
                                        (n % 2) * 1024 + (c + 1) * 128],
                                    inter[:, hh * 768 + c * 128:
                                          hh * 768 + (c + 1) * 128],
                                    ident[:])
                            ins.then_inc(C.s("tpAB"), 1)
                            C.cnt["tpAB"] += 1
                            eng.wait_ge(C.s("tpEB"), B["tpEB"] + n + 1)
                            tn, off = nslot(n)
                            mq0 = 6936 + (n % 2) * 768
                            for c in range(NFD):
                                ins = eng.matmul(
                                    tn[:, off:off + 129],
                                    inter[:, mq0 + c * 128:mq0 + (c + 1) * 128],
                                    inter[:, 3840 + hh * 774 + c * 129:
                                          3840 + hh * 774 + (c + 1) * 129],
                                    start=(c == 0), stop=(c == NFD - 1))
                            ins.then_inc(C.s("na1"), 1)
                            C.cnt["na1"] += 1

                bk_.tensor(f_tens)

                def f_sync(eng):
                    if kg == 0:
                        if blk == 0:
                            eng.wait_ge(C.s("vo"), AUG_LAST[0])
                        for t in range(4):
                            eng.dma_start(
                                out=Xh[:, t * D:(t + 1) * D],
                                in_=x_dram[(blk * 4 + t) * 128:
                                           (blk * 4 + t + 1) * 128, :],
                            ).then_inc(C.s("xldB"), 16)
                        C.cnt["xldB"] += 64
                        eng.dma_start(
                            out=Hf.ap().rearrange("p (k j) -> p k j", k=KD)[
                                :, :, 0:512],
                            in_=h2_dram[:, :, blkoff:blkoff + 512]
                                .rearrange("k p j -> p k j"),
                        ).then_inc(C.s("hldB"), 16)
                        C.cnt["hldB"] += 16
                    if kg == 3:
                        eng.wait_ge(C.s("vo"), M["updF"])
                        for t in range(4):
                            eng.dma_start(
                                out=x_dram[(blk * 4 + t) * 128:
                                           (blk * 4 + t + 1) * 128, :],
                                in_=Xh[:, t * D:(t + 1) * D],
                            ).then_inc(C.s("xstB"), 16)
                        C.cnt["xstB"] += 64
                        eng.wait_ge(C.s("xstB"), C.cnt["xstB"])

                bk_.sync(f_sync)
                bk_.gpsimd(f_gp)

            ln2_on = not globals().get("LN2_OFF", False)
            if "ffn1" in phases:
                emit_ffn_sweep(0, x_in, x_dram, ln2_on, 0.5)
                barrier()
                emit_ffn_sweep(1, x_in, x_dram, ln2_on, 0.5)
                barrier()
            if "attnA" in phases:
                for kg_ in range(4):
                    for blk_ in range(4):
                        emit_attnA_unit(kg_, blk_)
                barrier()
            if "aug" in phases:
                for kg_ in range(4):
                    emit_aug_kg(kg_)
            if "attnB" in phases:
                for blk_ in range(4):
                    for kg_ in range(4):
                        emit_attnB_unit(blk_, kg_)
                        barrier()
            if "dev_i" in phases:
                inter_d = nc.dram_tensor("inter_d", [128, NFC * S], BF16,
                                         kind="ExternalOutput").ap()
                cw_d = nc.dram_tensor("cw_d", [128, 128], F32,
                                      kind="ExternalOutput").ap()
                gsilu_d = nc.dram_tensor("gsilu_d", [128, S], F32,
                                         kind="ExternalOutput").ap()
                cols_d = nc.dram_tensor("cols_d", [128, 64], F32,
                                        kind="ExternalOutput").ap()
                hf_d = nc.dram_tensor("hf_d", [128, KD * S], BF16,
                                      kind="ExternalOutput").ap()

                sc_d = nc.dram_tensor("sc_d", [128, 128], F32,
                                      kind="ExternalOutput").ap()
                esc_d = nc.dram_tensor("esc_d", [128, 128], F32,
                                       kind="ExternalOutput").ap()

                gw_d = nc.dram_tensor("gw_d", [128, KD * 16], BF16,
                                      kind="ExternalOutput").ap()

                escsn_d = nc.dram_tensor("escsn_d", [128, 128], F32,
                                         kind="ExternalOutput").ap()
                scsn_d = nc.dram_tensor("scsn_d", [128, 128], F32,
                                        kind="ExternalOutput").ap()
                colsn_d = nc.dram_tensor("colsn_d", [128, 32], F32,
                                         kind="ExternalOutput").ap()

                def f_dump(eng):
                    for dst, srcb in ((inter_d, inter), (cw_d, cw),
                                      (gsilu_d, gsilu), (cols_d, cols),
                                      (hf_d, Hf), (sc_d, sc_sb), (esc_d, esc),
                                      (gw_d, gw_s), (escsn_d, escsn),
                                      (scsn_d, scsn), (colsn_d, colsn)):
                        eng.dma_start(out=dst, in_=srcb[:]).then_inc(C.s("dvd"), 16)
                        C.cnt["dvd"] = C.cnt.get("dvd", 0) + 16
                    eng.wait_ge(C.s("dvd"), C.cnt["dvd"])
                bk_.sync(f_dump)
                barrier()
            if "ffn2" in phases:
                src = x_dram if ("attnB" in phases or "ffn1" in phases) else x_in
                emit_ffn_sweep(0, src, y_out, False, 0.5)
                barrier()
                emit_ffn_sweep(1, src, y_out, False, 0.5)
                barrier()

    return nc


def kernel(**inputs):
    maps = _prep(inputs)
    nc = build_program()
    res = run_bass_kernel_spmd(nc, maps, core_ids=list(range(NCORE)))
    out = np.concatenate([res.results[c]["y_out"] for c in range(NCORE)], axis=0)
    return out.reshape(4, 4096, 2048)
